# version 12
# baseline (speedup 1.0000x reference)
import numpy as np
import concourse.bass as bass
import concourse.mybir as mybir
from concourse.bass_utils import run_bass_kernel_spmd

F32 = mybir.dt.float32
BF16 = mybir.dt.bfloat16
AF = mybir.ActivationFunctionType
ALU = mybir.AluOpType
AX = mybir.AxisListType

MAGIC = 12582912.0
SM_SCALE = 0.03125
LN_EPS = 1e-5
QP = 127.0
MIN_RANGE = 1e-6

B, T, D, HH = 2, 8192, 1024, 4096
NCORE = 8
CH = 2048
PAD = 128
WQ = PAD + CH
DC = D // 128
HC = HH // 128
QT = WQ // 128
KB = 1024
NKB = T // KB
NTS_W = [(0, 512), (512, 512), (1024, 512), (1536, 512), (2048, 128)]
NTS_OWN = [(128, 512), (640, 512), (1152, 512), (1664, 512)]
NTS_CH = [(0, 512), (512, 512), (1024, 512), (1536, 512)]

ARENA_LO = 28672
ARENA_HI = 229344


class Sem:
    def __init__(self, nc, name, owner):
        self.h = nc.semaphore(name).__enter__()
        self.owner = owner
        self.dbg_name = name
        self.n = 0


class QSem:

    R = 4

    def __init__(self, nc, name, owner):
        self.sems = [Sem(nc, f"{name}{i}", owner) for i in range(self.R)]
        self.owner = owner
        self.dbg_name = name
        self.k = 0

    @property
    def n(self):
        return 16 * self.k

    def expand_wait(self, v):
        k = (v + 15) // 16
        out = []
        for p in range(self.R):
            c = ((k - 1 - p) // self.R + 1) if k > p else 0
            if c > 0:
                out.append((self.sems[p], 16 * c))
        return out

    def next_issue(self):
        p = self.k % self.R
        c = self.k // self.R
        gate = [(self.sems[p], 16 * c)] if c > 0 else []
        self.k += 1
        return gate, self.sems[p]


class Prog:
    ENGINES = ("sync", "scalar", "vector", "tensor", "gpsimd")

    def __init__(self, nc):
        self.nc = nc
        self.ops = {e: [] for e in self.ENGINES}
        self.chain = {}

    def do(self, eng, fn, wait=(), inc=()):
        import traceback
        wait_x = []
        for s, v in wait:
            if v <= 0:
                continue
            if isinstance(s, QSem):
                wait_x.extend(s.expand_wait(v))
            else:
                wait_x.append((s, v))
        inc_x = []
        for s, amt in inc:
            assert s.owner == eng, (s.owner, eng)
            if isinstance(s, QSem):
                assert amt == 16
                gate, tgt = s.next_issue()
                wait_x = _merge_waits(wait_x, gate)
                inc_x.append((tgt, 16))
            else:
                inc_x.append((s, amt))
        csem = self.chain.get(eng)
        if csem is not None:
            wait_x = _merge_waits(wait_x, [(csem, csem.n)])
            if not inc_x:
                inc_x.append((csem, 1))
        wait = _merge_waits(wait_x)
        inc = inc_x

        stk = traceback.extract_stack()
        fr = next((f for f in reversed(stk[:-1]) if f.name not in ("do",)), stk[-2])
        tag = f"{fr.name}:{fr.lineno}"
        is_dma = any(sm.owner in ("sync", "gpsimd") and amt % 16 == 0 and amt >= 16 for sm, amt in inc)

        def run(e):
            for s, v in wait:
                e.wait_ge(s.h, v)
            n0 = self.nc.n_instructions()
            instr = fn(e)
            if is_dma and self.nc.n_instructions() - n0 != 1:
                raise RuntimeError(f"DMA at {tag} lowered to {self.nc.n_instructions() - n0} "
                                   f"instructions; semaphore accounting assumes 1")
            for s, amt in inc:
                instr = instr.then_inc(s.h, amt)

        run.meta = (eng, wait, inc, tag)
        self.ops[eng].append(run)
        for s, amt in inc:
            s.n += amt

    def verify_no_deadlock(self):
        heads = {e: 0 for e in self.ENGINES}
        cur = {}
        total = sum(len(v) for v in self.ops.values())
        done = 0
        progress = True
        while done < total and progress:
            progress = False
            for e in self.ENGINES:
                while heads[e] < len(self.ops[e]):
                    eng, wait, inc, tag = self.ops[e][heads[e]].meta
                    if any(cur.get(id(sm), 0) < v for sm, v in wait):
                        break
                    for sm, amt in inc:
                        cur[id(sm)] = cur.get(id(sm), 0) + amt
                    heads[e] += 1
                    done += 1
                    progress = True
        if done < total:
            lines = []
            for e in self.ENGINES:
                if heads[e] < len(self.ops[e]):
                    eng, wait, inc, tag = self.ops[e][heads[e]].meta
                    blocked = [(getattr(sm, 'dbg_name', '?'), v, cur.get(id(sm), 0)) for sm, v in wait
                               if cur.get(id(sm), 0) < v]
                    lines.append(f"  {e}[{heads[e]}/{len(self.ops[e])}] at {tag} blocked on {blocked}")
            raise RuntimeError("DEADLOCK in planned program:\n" + "\n".join(lines))

    def emit(self):
        with self.nc.Block() as block:
            for eng in self.ENGINES:
                ops = self.ops[eng]

                def body(e, ops=ops):
                    for op in ops:
                        op(e)

                getattr(block, eng)(body)


class Arena:

    def __init__(self, nc):
        self.nc = nc
        self.free_list = [(ARENA_LO, ARENA_HI, [])]
        self.live = {}
        self.uid = 0
        self.peak = 0

    def alloc(self, name, shape, dtype, align=64):
        nbytes = int(np.prod(shape[1:])) * (2 if dtype == BF16 else 4)
        order = sorted(range(len(self.free_list)),
                       key=lambda i: self.free_list[i][1] - self.free_list[i][0])
        for i in order:
            lo, hi, gu = self.free_list[i]
            start = (lo + align - 1) // align * align
            if start + nbytes <= hi:
                segs = []
                if start > lo:
                    segs.append((lo, start, gu))
                if start + nbytes < hi:
                    segs.append((start + nbytes, hi, gu))
                self.free_list[i:i + 1] = segs
                self.uid += 1
                h = self.nc.alloc_sbuf_tensor_at(f"{name}_{self.uid}", list(shape), dtype, offset=start)
                self.live[id(h)] = (start, start + nbytes)
                self.peak = max(self.peak, start + nbytes)
                return h, list(gu)
        raise MemoryError(f"SBUF arena exhausted for {name} {shape}; free={[(a, b) for a, b, _ in self.free_list]}")

    def free(self, h, guard):
        lo, hi = self.live.pop(id(h))
        self.free_list.append((lo, hi, list(guard)))
        self.free_list.sort(key=lambda s: s[0])
        merged = []
        for seg in self.free_list:
            if merged and merged[-1][1] >= seg[0]:
                prev = merged[-1]
                merged[-1] = (prev[0], max(prev[1], seg[1]), prev[2] + seg[2])
            else:
                merged.append(seg)
        self.free_list = merged


def _merge_waits(*wait_lists):
    best = {}
    for wl in wait_lists:
        for s, v in wl:
            best[id(s)] = (s, max(v, best.get(id(s), (s, 0))[1]))
    return list(best.values())


def build_program(dbg=None):
    nc = bass.Bass("TRN2", target_bir_lowering=False)

    x_ext = nc.dram_tensor("x_ext", [D, WQ], F32, kind="ExternalInput").ap()
    wqkvT = nc.dram_tensor("wqkvT", [D, 3 * D], F32, kind="ExternalInput").ap()
    wfcT = nc.dram_tensor("wfcT", [D, D], F32, kind="ExternalInput").ap()
    w1T = nc.dram_tensor("w1T", [D, HH], F32, kind="ExternalInput").ap()
    w2T = nc.dram_tensor("w2T", [HH, D], F32, kind="ExternalInput").ap()
    vecs = nc.dram_tensor("vecs", [16, D], F32, kind="ExternalInput").ap()
    b1v = nc.dram_tensor("b1v", [HH], F32, kind="ExternalInput").ap()
    wl1T = nc.dram_tensor("wl1T", [7, D, D], F32, kind="ExternalInput").ap()
    wmx = nc.dram_tensor("wmx", [128, 8192], F32, kind="ExternalInput").ap()
    out_ext = nc.dram_tensor("out_ext", [D, CH], F32, kind="ExternalOutput").ap()
    dbg_ext = nc.dram_tensor("dbgout", [D, WQ], F32, kind="ExternalOutput").ap() if dbg else None
    dbg2 = nc.dram_tensor("dbg2", [6, D, WQ], F32, kind="ExternalOutput").ap() if dbg == "all" else None
    dbg3 = nc.dram_tensor("dbg3", [128, 128], F32, kind="ExternalOutput").ap() if dbg == "all" else None

    xn_sp = nc.dram_tensor("xn_sp", [D, WQ], F32).ap()
    h_sp = nc.dram_tensor("h_sp", [HH, WQ], F32).ap()
    x1_sp = nc.dram_tensor("x1_sp", [D, WQ], F32).ap()
    k_sp = nc.dram_tensor("k_sp", [D, CH], F32).ap()
    v_sp = nc.dram_tensor("v_sp", [D, CH], F32).ap()
    sr_sp = nc.dram_tensor("sr_sp", [D, CH], F32).ap()
    r2_sp = nc.dram_tensor("r2_sp", [D, CH], F32).ap()
    cc_m1_i = nc.dram_tensor("cc_m1_i", [1, 8], F32).ap()
    cc_m1_o = nc.dram_tensor("cc_m1_o", [8, 8], F32).ap()
    cc_m2_i = nc.dram_tensor("cc_m2_i", [1, 8], F32).ap()
    cc_m2_o = nc.dram_tensor("cc_m2_o", [8, 8], F32).ap()
    cc_vs_i = nc.dram_tensor("cc_vs_i", [1, D], F32).ap()
    cc_vs_o = nc.dram_tensor("cc_vs_o", [8, D], F32).ap()
    cc_wk_i = [nc.dram_tensor(f"cc_wk_i{g}", [1, 256], F32).ap() for g in range(DC)]
    cc_wk_o = [nc.dram_tensor(f"cc_wk_o{g}", [8, 256], F32).ap() for g in range(DC)]
    wkv_dbg = nc.dram_tensor("wkv_dbg", [7, D, CH], F32).ap() if dbg == "wkv" else None

    P = Prog(nc)
    ld = QSem(nc, "ld", "sync")
    gd = QSem(nc, "gd", "gpsimd")
    vs = Sem(nc, "vs", "vector")
    ss = Sem(nc, "ss", "scalar")
    ts = Sem(nc, "ts", "tensor")
    gs = Sem(nc, "gs", "gpsimd")
    cs = Sem(nc, "cs", "gpsimd")
    P.chain["vector"] = vs
    P.chain["gpsimd"] = gs
    P.chain["scalar"] = ss

    A = Arena(nc)

    def alloc(name, shape, dtype):
        h, gu = A.alloc(name, shape, dtype)
        return h, gu

    psA = nc.psum_tensor("psA", [128, 1024], F32).__enter__()
    psB = nc.psum_tensor("psB", [128, 1024], F32).__enter__()
    psC = nc.psum_tensor("psC", [128, 1024], F32).__enter__()
    psD = nc.psum_tensor("psD", [128, 512], F32).__enter__()
    psE = nc.psum_tensor("psE", [128, 512], F32).__enter__()

    idn, _ = alloc("idn", [128, 128], F32)
    idnb, _ = alloc("idnb", [128, 128], BF16)
    ones128b, _ = alloc("ones128b", [128, 128], BF16)
    onescol, _ = alloc("onescol", [1, 128], F32)
    epsc, _ = alloc("epsc", [128, 1], F32)
    magicc, _ = alloc("magicc", [128, 1], F32)
    smc, _ = alloc("smc", [128, 1], F32)
    neghalfc, _ = alloc("neghalfc", [128, 1], F32)
    invdc, _ = alloc("invdc", [128, 1], F32)
    negmagicc, _ = alloc("negmagicc", [128, 1], F32)
    nc.const_aps.register(F32, -MAGIC, negmagicc[:, :])
    nc.const_aps.register(F32, MAGIC, magicc[:, :])
    nc.const_aps.register(F32, SM_SCALE, smc[:, :])
    nc.const_aps.register(F32, -0.5, neghalfc[:, :])

    vcm, _ = alloc("vcm", [128, 16 * 8], F32)
    VEC = {n: i for i, n in enumerate(
        ["bfc", "gamma", "beta", "b2", "td", "tf", "tmk", "tmv", "tmr", "mur", "muk"])}

    def vc(name):
        i = VEC[name]
        return vcm[:, i * 8:(i + 1) * 8]

    b1cm, _ = alloc("b1cm", [128, HC], F32)
    b1ss, _ = alloc("b1ss", [128, HC], F32)
    scl, _ = alloc("scl", [128, 32], F32)
    e2048bc, _ = alloc("e2048bc", [128, 8], F32)
    mjbc, _ = alloc("mjbc", [128, 8], F32)
    mbbc, _ = alloc("mbbc", [128, 8], F32)
    expv, _ = alloc("expv", [128, 32], F32)
    wmaxp, _ = alloc("wmaxp", [128, 2], F32)
    rowt, _ = alloc("rowt", [32, 512], F32)
    rgA = []
    rgB = []
    rgA_busy = [[], [], [], []]
    rgB_busy = [[], [], [], []]
    for i_ in range(4):
        h, _ = alloc(f"rgA{i_}", [128, 512], F32)
        rgA.append(h)
        h, _ = alloc(f"rgB{i_}", [128, 512], F32)
        rgB.append(h)
    rg_ix = [0]

    P.do("gpsimd", lambda e: e.memset(idn[:, :], 0.0), inc=[(gs, 1)])
    P.do("gpsimd", lambda e: e.affine_select(out=idn[:, :], in_=idn[:, :], compare_op=ALU.not_equal,
                                             fill=1.0, base=0, pattern=[[-1, 128]], channel_multiplier=1),
         wait=[(gs, gs.n)], inc=[(gs, 1)])
    P.do("gpsimd", lambda e: e.memset(ones128b[:, :], 1.0))
    P.do("gpsimd", lambda e: e.memset(onescol[:, :], 1.0))
    P.do("gpsimd", lambda e: e.memset(epsc[:, :], LN_EPS))
    P.do("gpsimd", lambda e: e.memset(magicc[:, :], MAGIC))
    P.do("gpsimd", lambda e: e.memset(smc[:, :], SM_SCALE))
    P.do("gpsimd", lambda e: e.memset(neghalfc[:, :], -0.5))
    P.do("gpsimd", lambda e: e.memset(invdc[:, :], 1.0 / D))
    P.do("gpsimd", lambda e: e.memset(negmagicc[:, :], -MAGIC))
    P.do("gpsimd", lambda e: e.memset(wmaxp[:, :], 0.0))
    P.do("gpsimd", lambda e: e.memset(scl[:, :], 0.0))
    P.do("gpsimd", lambda e: e.memset(rowt[:, :], 0.0), inc=[(gs, 1)])
    gs_pro = gs.n
    P.do("vector", lambda e: e.tensor_copy(idnb[:, :], idn[:, :]), wait=[(gs, gs_pro)], inc=[(vs, 1)])

    for i, name in enumerate(VEC):
        P.do("sync", lambda e, i=i: e.dma_start(
            out=rowt[0:8, 0:128],
            in_=vecs.rearrange("v (j p) -> (v j) p", p=128)[i * 8:(i + 1) * 8, :]),
            wait=[(ts, ts.n), (gs, gs_pro)], inc=[(ld, 16)])
        P.do("tensor", lambda e: e.transpose(psD[:, 0:8], rowt[0:8, 0:128], idn[0:8, 0:8]),
             wait=[(ld, ld.n), (vs, vs.n)], inc=[(ts, 1)])
        P.do("vector", lambda e, i=i: e.tensor_copy(vcm[:, i * 8:(i + 1) * 8], psD[:, 0:8]),
             wait=[(ts, ts.n)], inc=[(vs, 1)])
    P.do("sync", lambda e: e.dma_start(out=rowt[0:32, 0:128], in_=b1v.rearrange("(j p) -> j p", p=128)),
         wait=[(ts, ts.n)], inc=[(ld, 16)])
    P.do("tensor", lambda e: e.transpose(psD[:, 0:32], rowt[0:32, 0:128], idn[0:32, 0:32]),
         wait=[(ld, ld.n), (vs, vs.n)], inc=[(ts, 1)])
    P.do("vector", lambda e: e.tensor_copy(b1cm[:, :], psD[:, 0:32]), wait=[(ts, ts.n)], inc=[(vs, 1)])
    P.do("sync", lambda e: e.dma_start(out=rowt[0:1, 128:160], in_=vecs[11:12, 0:32]), inc=[(ld, 16)])
    ld_ctl = ld.n

    def bcast_row(dst, src_row, n, wait):
        P.do("tensor", lambda e: e.matmul(psE[:, 0:n], onescol[0:1, :], src_row, start=True, stop=True),
             wait=_merge_waits(wait, [(gs, gs_pro), (ss, ss.n), (vs, vs.n)]), inc=[(ts, 1)])
        P.do("vector", lambda e: e.tensor_copy(dst, psE[:, 0:n]), wait=[(ts, ts.n)], inc=[(vs, 1)])

    bcast_row(scl[:, 4:5], rowt[0:1, 128:129], 1, [(ld, ld_ctl), (vs, vs.n)])
    bcast_row(mjbc[:, :], rowt[0:1, 129:137], 8, [(vs, vs.n)])
    bcast_row(e2048bc[:, :], rowt[0:1, 137:145], 8, [(vs, vs.n)])
    bcast_row(mbbc[:, :], rowt[0:1, 145:153], 8, [(vs, vs.n)])
    vs_ctl = vs.n

    P.do("scalar", lambda e: e.activation(out=expv[:, 0:8], in_=vc("td"), func=AF.Exp),
         wait=[(vs, vs_ctl)], inc=[(ss, 1)])
    P.do("vector", lambda e: e.tensor_scalar(out=expv[:, 8:16], in0=expv[:, 0:8], scalar1=-1.0,
                                             scalar2=None, op0=ALU.mult), wait=[(ss, ss.n)], inc=[(vs, 1)])
    P.do("vector", lambda e: e.tensor_scalar(out=expv[:, 8:16], in0=expv[:, 8:16], scalar1=-87.0,
                                             scalar2=None, op0=ALU.max), wait=[(vs, vs.n)], inc=[(vs, 1)])
    P.do("scalar", lambda e: e.activation(out=expv[:, 16:24], in_=expv[:, 8:16], func=AF.Exp),
         wait=[(vs, vs.n)], inc=[(ss, 1)])
    P.do("scalar", lambda e: e.activation(out=expv[:, 24:32], in_=vc("tf"), func=AF.Exp), inc=[(ss, 1)])

    wm_first = [True, True]
    for j in range(16):
        col = 0 if j < 8 else 1
        i_ = rg_ix[0] % 4
        rg_ix[0] += 1
        bA = rgA[i_]
        P.do("sync", lambda e, j=j, bA=bA: e.dma_start(out=bA[:, :], in_=wmx[:, j * 512:(j + 1) * 512]),
             wait=_merge_waits(rgA_busy[i_], [(gs, gs_pro)] if j == 0 else []), inc=[(ld, 16)])
        ldn = ld.n
        if wm_first[col]:
            P.do("vector", lambda e, col=col, bA=bA: e.reduce_max(out=wmaxp[:, col:col + 1], in_=bA[:, :],
                                                                  axis=AX.X, apply_absolute_value=True),
                 wait=[(ld, ldn)], inc=[(vs, 1)])
            wm_first[col] = False
        else:
            P.do("vector", lambda e, bA=bA: e.reduce_max(out=scl[:, 11:12], in_=bA[:, :], axis=AX.X,
                                                         apply_absolute_value=True),
                 wait=[(ld, ldn)], inc=[(vs, 1)])
            P.do("vector", lambda e, col=col: e.tensor_tensor(out=wmaxp[:, col:col + 1],
                                                              in0=wmaxp[:, col:col + 1],
                                                              in1=scl[:, 11:12], op=ALU.max), inc=[(vs, 1)])
        rgA_busy[i_] = [(vs, vs.n)]
    P.do("tensor", lambda e: e.transpose(psE[0:2, 0:128], wmaxp[:, 0:2], idn[:, :]),
         wait=[(vs, vs.n), (ss, ss.n)], inc=[(ts, 1)])
    P.do("vector", lambda e: e.reduce_max(out=rowt[0:2, 152:153], in_=psE[0:2, 0:128], axis=AX.X),
         wait=[(ts, ts.n)], inc=[(vs, 1)])
    P.do("tensor", lambda e: e.transpose(psE[0:1, 0:2], rowt[0:2, 152:153], idn[0:2, 0:2]),
         wait=[(vs, vs.n)], inc=[(ts, 1)])
    P.do("vector", lambda e: e.tensor_copy(scl[0:1, 13:15], psE[0:1, 0:2]), wait=[(ts, ts.n)], inc=[(vs, 1)])
    vs_wmx = vs.n

    x_fm = []
    xfm_gu = []
    for g in range(DC):
        h, gu = alloc(f"x_fm{g}", [128, WQ], F32)
        x_fm.append(h)
        xfm_gu += gu
    for g in range(DC):
        P.do("sync", lambda e, g=g: e.dma_start(out=x_fm[g][:, :], in_=x_ext[g * 128:(g + 1) * 128, :]),
             wait=(_merge_waits(xfm_gu) if g == 0 else []), inc=[(ld, 16)])
    ld_xfm = ld.n

    def dump2(slot, chunks, wait, width=WQ):
        if dbg != "all":
            return
        for g in range(DC):
            P.do("sync", lambda e, g=g: e.dma_start(
                out=dbg2[slot, g * 128:(g + 1) * 128, 0:width], in_=chunks[g][:, 0:width]),
                wait=wait if g == 0 else [], inc=[(ld, 16)])

    ln_sc, _ = alloc("ln_sc", [128, 512], F32)
    castp = []
    for i_ in range(2):
        h0_, _ = alloc(f"cast0_{i_}", [128, 512], BF16)
        h1_, _ = alloc(f"cast1_{i_}", [128, 512], BF16)
        castp.append((h0_, h1_))
    cast_busy = [0, 0]

    def emit_ln(src, mu_t, rstd_t, first_wait, width_tiles=NTS_W):
        for ti, (c0, w) in enumerate(width_tiles):
            for g in range(DC):
                p_ = g % 2
                c0b, c1b = castp[p_]
                wcast = _merge_waits(list(first_wait) if (ti == 0 and g == 0) else [],
                                     [(ts, cast_busy[p_])])
                P.do("vector", lambda e, g=g, c0=c0, w=w, c0b=c0b: e.tensor_copy(
                    c0b[:, 0:w], src[g][:, c0:c0 + w]), wait=wcast, inc=[(vs, 1)])
                P.do("vector", lambda e, g=g, c0=c0, w=w, c1b=c1b: e.tensor_tensor(
                    out=c1b[:, 0:w], in0=src[g][:, c0:c0 + w], in1=src[g][:, c0:c0 + w], op=ALU.mult),
                    inc=[(vs, 1)])
                vsn = vs.n
                P.do("tensor", lambda e, g=g, w=w, c0b=c0b: e.matmul(psD[:, 0:w], ones128b[:, :], c0b[:, 0:w],
                                                                     start=(g == 0), stop=(g == DC - 1)),
                     wait=[(vs, vsn)])
                P.do("tensor", lambda e, g=g, w=w, c1b=c1b: e.matmul(psE[:, 0:w], ones128b[:, :], c1b[:, 0:w],
                                                                     start=(g == 0), stop=(g == DC - 1)),
                     inc=[(ts, 1)])
                cast_busy[p_] = ts.n
            tsn = ts.n
            P.do("vector", lambda e, c0=c0, w=w: e.tensor_scalar(
                out=mu_t[:, c0:c0 + w], in0=psD[:, 0:w], scalar1=invdc[:, 0:1], scalar2=None, op0=ALU.mult),
                wait=[(ts, tsn)], inc=[(vs, 1)])
            P.do("vector", lambda e, c0=c0, w=w: e.tensor_scalar(
                out=ln_sc[:, 0:w], in0=psE[:, 0:w], scalar1=invdc[:, 0:1], scalar2=None, op0=ALU.mult),
                wait=[(ss, ss.n)], inc=[(vs, 1)])
            P.do("vector", lambda e, c0=c0, w=w: e.tensor_tensor(
                out=rstd_t[:, c0:c0 + w], in0=mu_t[:, c0:c0 + w], in1=mu_t[:, c0:c0 + w], op=ALU.mult),
                inc=[(vs, 1)])
            P.do("vector", lambda e, c0=c0, w=w: e.tensor_tensor(
                out=ln_sc[:, 0:w], in0=ln_sc[:, 0:w], in1=rstd_t[:, c0:c0 + w], op=ALU.subtract),
                inc=[(vs, 1)])
            vsn = vs.n
            P.do("scalar", lambda e, c0=c0, w=w: e.activation(out=ln_sc[:, 0:w], in_=ln_sc[:, 0:w],
                                                              func=AF.Ln, bias=epsc[:, 0:1], scale=1.0),
                 wait=[(vs, vsn)], inc=[(ss, 1)])
            P.do("scalar", lambda e, c0=c0, w=w: e.activation(out=rstd_t[:, c0:c0 + w], in_=ln_sc[:, 0:w],
                                                              func=AF.Exp, scale=-0.5), inc=[(ss, 1)])
            P.do("tensor", lambda e: e.nop(), wait=[(vs, vsn)])

    def emit_ln_norm(src, dst, dst_guard, mu_t, rstd_t, gamma=None, beta=None):
        ss_stats = ss.n
        for g in range(DC):
            P.do("vector", lambda e, g=g: e.tensor_tensor(
                out=dst[g][:, :], in0=src[g][:, :], in1=mu_t[:, :], op=ALU.subtract),
                wait=([(ss, ss_stats)] + dst_guard) if g == 0 else [], inc=[(vs, 1)])
            P.do("vector", lambda e, g=g: e.tensor_tensor(
                out=dst[g][:, :], in0=dst[g][:, :], in1=rstd_t[:, :], op=ALU.mult), inc=[(vs, 1)])
            if gamma is not None:
                P.do("vector", lambda e, g=g: e.tensor_scalar(
                    out=dst[g][:, :], in0=dst[g][:, :], scalar1=gamma[:, g:g + 1], scalar2=beta[:, g:g + 1],
                    op0=ALU.mult, op1=ALU.add), inc=[(vs, 1)])

    mu_t, mu_gu = alloc("mu_t", [128, WQ], F32)
    rstd_t, rs_gu = alloc("rstd_t", [128, WQ], F32)
    emit_ln(x_fm, mu_t, rstd_t, _merge_waits([(ld, ld_xfm)], mu_gu, rs_gu))
    xn = []
    xn_gu = []
    for g in range(DC):
        h, gu = alloc(f"xn{g}", [128, WQ], F32)
        xn.append(h)
        xn_gu += gu
    emit_ln_norm(x_fm, xn, xn_gu, mu_t, rstd_t)
    gs_xn = vs.n
    for g in range(DC):
        A.free(x_fm[g], [(vs, gs_xn)])
    A.free(mu_t, [(vs, gs_xn)])
    A.free(rstd_t, [(vs, gs_xn)])

    dump2(0, xn, [(vs, gs_xn)])
    def absmax_chunks(chunks, cslice, dst_acc, dst_tmp, first_wait):
        for g in range(DC):
            if g == 0:
                P.do("vector", lambda e, g=g: e.reduce_max(out=dst_acc, in_=chunks[g][:, cslice[0]:cslice[1]],
                                                           axis=AX.X, apply_absolute_value=True),
                     wait=first_wait, inc=[(vs, 1)])
            else:
                P.do("vector", lambda e, g=g: e.reduce_max(out=dst_tmp, in_=chunks[g][:, cslice[0]:cslice[1]],
                                                           axis=AX.X, apply_absolute_value=True), inc=[(vs, 1)])
                P.do("vector", lambda e: e.tensor_tensor(out=dst_acc, in0=dst_acc, in1=dst_tmp, op=ALU.max),
                     inc=[(vs, 1)])

    absmax_chunks(xn, (PAD, WQ), scl[:, 10:11], scl[:, 11:12], [(vs, gs_xn)])
    P.do("tensor", lambda e: e.transpose(psE[0:1, 0:128], scl[:, 10:11], idn[:, :]),
         wait=[(vs, vs.n), (ss, ss.n)], inc=[(ts, 1)])
    P.do("vector", lambda e: e.reduce_max(out=scl[0:1, 12:13], in_=psE[0:1, 0:128], axis=AX.X),
         wait=[(ts, ts.n)], inc=[(vs, 1)])
    vs_m1 = vs.n
    P.do("gpsimd", lambda e: e.dma_start(out=cc_m1_i[:, 0:8], in_=scl[0:1, 12:20]),
         wait=[(vs, vs_m1)], inc=[(gd, 16)])
    P.do("gpsimd", lambda e: e.collective_compute(
        "AllGather", ALU.bypass, replica_groups=[list(range(8))],
        ins=[cc_m1_i.opt()], outs=[cc_m1_o.opt()]), wait=[(gd, gd.n)], inc=[(cs, 1)])
    cs_m1 = cs.n

    def load_wT_bf16(dram_rows_cols, dst, first_wait=()):
        for kk in range(DC):
            for cq in range(0, D, 512):
                i_ = rg_ix[0] % 4
                rg_ix[0] += 1
                bB = rgB[i_]
                P.do("sync", lambda e, kk=kk, cq=cq, bB=bB: e.dma_start(
                    out=bB[:, :], in_=dram_rows_cols[kk * 128:(kk + 1) * 128, cq:cq + 512]),
                    wait=_merge_waits(rgB_busy[i_],
                                      list(first_wait) if (kk == 0 and cq == 0) else []), inc=[(ld, 16)])
                P.do("scalar", lambda e, kk=kk, cq=cq, dst=dst, bB=bB: e.activation(
                    out=dst[kk][:, cq:cq + 512], in_=bB[:, :], func=AF.Copy),
                    wait=[(ld, ld.n)], inc=[(ss, 1)])
                rgB_busy[i_] = [(ss, ss.n)]
        return ss.n

    wvb = []
    wv_gu = []
    for k in range(DC):
        h, gu = alloc(f"wvb{k}", [128, D], BF16)
        wvb.append(h)
        wv_gu += gu
    ss_wvb = load_wT_bf16(wqkvT[:, 2 * D:3 * D], wvb, first_wait=wv_gu)

    xsl = []
    xsl_gu = []
    for k in range(DC):
        h, gu = alloc(f"xsl{k}", [128, 512], BF16)
        xsl.append(h)
        xsl_gu += gu
    vsum_cm, vsg_gu0 = alloc("vsum_cm", [128, 16], F32)
    P.do("vector", lambda e: e.memset(vsum_cm[:, :], 0.0), wait=_merge_waits(vsg_gu0), inc=[(vs, 1)])
    vs_first = [True]
    for ni, (c0, w) in enumerate(NTS_OWN):
        for kk in range(DC):
            P.do("vector", lambda e, kk=kk, c0=c0, w=w: e.tensor_copy(xsl[kk][:, 0:w], xn[kk][:, c0:c0 + w]),
                 wait=(_merge_waits([(ss, ss_wvb), (vs, gs_xn)], xsl_gu) if (ni == 0 and kk == 0) else ([(ts, ts.n)] if kk == 0 else [])),
                 inc=[(vs, 1)] if kk == DC - 1 else [])
        vs_sl = vs.n
        for m in range(DC):
            pb = psD if m % 2 == 0 else psE
            for kk in range(DC):
                P.do("tensor", lambda e, kk=kk, m=m, w=w, pb=pb: e.matmul(
                    pb[:, 0:w], wvb[kk][:, m * 128:(m + 1) * 128], xsl[kk][:, 0:w],
                    start=(kk == 0), stop=(kk == DC - 1)),
                    wait=[(vs, vs_sl), (vs, vs.n)] if kk == 0 else [],
                    inc=[(ts, 1)] if kk == DC - 1 else [])
            if vs_first[0]:
                P.do("vector", lambda e, m=m, w=w, pb=pb: e.reduce_sum(
                    out=vsum_cm[:, m:m + 1], in_=pb[:, 0:w], axis=AX.X), wait=[(ts, ts.n)], inc=[(vs, 1)])
            else:
                P.do("vector", lambda e, m=m, w=w, pb=pb: e.reduce_sum(
                    out=vsum_cm[:, 8:9], in_=pb[:, 0:w], axis=AX.X), wait=[(ts, ts.n)], inc=[(vs, 1)])
                P.do("vector", lambda e, m=m: e.tensor_tensor(
                    out=vsum_cm[:, m:m + 1], in0=vsum_cm[:, m:m + 1], in1=vsum_cm[:, 8:9], op=ALU.add),
                    wait=[(vs, vs.n)], inc=[(vs, 1)])
        vs_first[0] = False
    vs_vsum = vs.n
    for g in range(DC):
        P.do("gpsimd", lambda e, g=g: e.dma_start(out=cc_vs_i[0:1, g * 128:(g + 1) * 128],
                                                  in_=vsum_cm[:, g:g + 1]),
             wait=[(vs, vs_vsum)] if g == 0 else [], inc=[(gd, 16)])
    P.do("gpsimd", lambda e: e.collective_compute(
        "AllGather", ALU.bypass, replica_groups=[list(range(8))],
        ins=[cc_vs_i.opt()], outs=[cc_vs_o.opt()]), wait=[(gd, gd.n)], inc=[(cs, 1)])
    cs_vs = cs.n
    ts_kv_proj = ts.n
    for k in range(DC):
        A.free(wvb[k], [(ts, ts_kv_proj)])

    P.do("sync", lambda e: e.dma_start(out=rowt[0:8, 160:168], in_=cc_m1_o[:, :]),
         wait=[(cs, cs_m1)], inc=[(ld, 16)])
    P.do("tensor", lambda e: e.transpose(psE[0:8, 0:8], rowt[0:8, 160:168], idn[0:8, 0:8]),
         wait=[(ld, ld.n), (vs, vs.n), (ss, ss.n)], inc=[(ts, 1)])
    P.do("vector", lambda e: e.reduce_max(out=rowt[0:3, 155:156], in_=psE[0:3, 0:8], axis=AX.X),
         wait=[(ts, ts.n)], inc=[(vs, 1)])
    P.do("tensor", lambda e: e.transpose(psE[0:1, 0:3], rowt[0:3, 155:156], idn[0:3, 0:3]),
         wait=[(vs, vs.n)], inc=[(ts, 1)])
    P.do("vector", lambda e: e.tensor_copy(rowt[0:1, 152:155], psE[0:1, 0:3]), wait=[(ts, ts.n)],
         inc=[(vs, 1)])
    P.do("vector", lambda e: e.tensor_scalar(out=rowt[0:1, 156:157], in0=rowt[0:1, 155:156],
                                             scalar1=MIN_RANGE, scalar2=1.0 / QP, op0=ALU.max, op1=ALU.mult),
         inc=[(vs, 1)])
    P.do("vector", lambda e: e.reciprocal(out=rowt[0:1, 157:158], in_=rowt[0:1, 156:157]), inc=[(vs, 1)])
    P.do("vector", lambda e: e.tensor_scalar(out=rowt[0:1, 158:159], in0=rowt[0:1, 153:154],
                                             scalar1=MIN_RANGE, scalar2=1.0 / QP, op0=ALU.max, op1=ALU.mult),
         inc=[(vs, 1)])
    P.do("vector", lambda e: e.reciprocal(out=rowt[0:1, 159:160], in_=rowt[0:1, 158:159]), inc=[(vs, 1)])
    P.do("vector", lambda e: e.tensor_tensor(out=rowt[0:1, 168:169], in0=rowt[0:1, 158:159],
                                             in1=rowt[0:1, 156:157], op=ALU.mult), inc=[(vs, 1)])
    vs_s1e = vs.n
    bcast_row(scl[:, 13:15], rowt[0:1, 156:158], 2, [(vs, vs_s1e)])
    bcast_row(scl[:, 15:17], rowt[0:1, 158:160], 2, [(vs, vs.n)])
    bcast_row(scl[:, 17:18], rowt[0:1, 168:169], 1, [(vs, vs.n)])
    vs_s1bc = vs.n
    P.do("vector", lambda e: e.tensor_scalar(out=b1ss[:, :], in0=b1cm[:, :], scalar1=scl[:, 17:18],
                                             scalar2=None, op0=ALU.mult), wait=[(vs, vs_s1bc)], inc=[(vs, 1)])
    vs_ss1 = vs.n

    w1qb = []
    w1_gu = []
    for k in range(DC):
        h, gu = alloc(f"w1qb{k}", [128, HH], BF16)
        w1qb.append(h)
        w1_gu += gu
    for kk in range(DC):
        for c in range(0, HH, 512):
            i_ = rg_ix[0] % 4
            rg_ix[0] += 1
            bB, bA = rgB[i_], rgA[i_]
            P.do("sync", lambda e, kk=kk, c=c, bB=bB: e.dma_start(
                out=bB[:, :], in_=w1T[kk * 128:(kk + 1) * 128, c:c + 512]),
                wait=rgB_busy[i_] + (w1_gu + [(vs, vs_ss1)] if (kk == 0 and c == 0) else []),
                inc=[(ld, 16)])
            P.do("scalar", lambda e, bB=bB, bA=bA: e.activation(out=bA[:, :], in_=bB[:, :], func=AF.Identity,
                                                                bias=MAGIC, scale=scl[:, 16:17]),
                 wait=_merge_waits([(ld, ld.n)], rgA_busy[i_]), inc=[(ss, 1)])
            rgB_busy[i_] = [(ss, ss.n)]
            P.do("vector", lambda e, kk=kk, c=c, bA=bA: e.tensor_scalar(
                out=w1qb[kk][:, c:c + 512], in0=bA[:, :], scalar1=-MAGIC, scalar2=None, op0=ALU.add),
                wait=[(ss, ss.n)], inc=[(vs, 1)])
            rgA_busy[i_] = [(vs, vs.n)]
    vs_w1q = vs.n

    h_tile = []
    ht_gu = []
    for i_ in range(4):
        h, gu = alloc(f"ht{i_}", [128, 512], F32)
        h_tile.append(h)
        ht_gu += gu
    h_busy = [0] * 4
    h_vbusy = [0] * 4
    m2_first = [True]
    bank_ss1 = [0, 0]
    xslB = []
    xslB_gu = []
    for k in range(DC):
        h, gu = alloc(f"xslB{k}", [128, 512], BF16)
        xslB.append(h)
        xslB_gu += gu
    xsl2 = [xsl, xslB]
    xsl2_busy = [[(ts, ts_kv_proj)] + xslB_gu, list(xslB_gu)]
    vs_feed = [0] * len(NTS_W)

    def m1_feed(t):
        c0, w = NTS_W[t]
        dst = xsl2[t % 2]
        for kk in range(DC):
            i_ = rg_ix[0] % 4
            rg_ix[0] += 1
            bA = rgA[i_]
            P.do("scalar", lambda e, kk=kk, c0=c0, w=w, bA=bA: e.activation(
                out=bA[:, 0:w], in_=xn[kk][:, c0:c0 + w], func=AF.Identity, bias=MAGIC,
                scale=scl[:, 14:15]),
                wait=_merge_waits(rgA_busy[i_],
                                  ([(vs, vs_s1bc)] if t == 0 else []) if kk == 0 else []),
                inc=[(ss, 1)])
            P.do("vector", lambda e, kk=kk, w=w, bA=bA, dst=dst: e.tensor_scalar(
                out=dst[kk][:, 0:w], in0=bA[:, 0:w], scalar1=-MAGIC, scalar2=None, op0=ALU.add),
                wait=_merge_waits([(ss, ss.n)], xsl2_busy[t % 2] if kk == 0 else []), inc=[(vs, 1)])
            rgA_busy[i_] = [(vs, vs.n)]
        vs_feed[t] = vs.n

    m1_feed(0)
    for ni, (c0, w) in enumerate(NTS_W):
        if ni + 1 < len(NTS_W):
            m1_feed(ni + 1)
        rhs = xsl2[ni % 2]
        for m in range(HC):
            pb = psD if m % 2 == 0 else psE
            for kk in range(DC):
                P.do("tensor", lambda e, kk=kk, m=m, w=w, pb=pb, rhs=rhs: e.matmul(
                    pb[:, 0:w], w1qb[kk][:, m * 128:(m + 1) * 128], rhs[kk][:, 0:w],
                    start=(kk == 0), stop=(kk == DC - 1)),
                    wait=(_merge_waits([(ss, bank_ss1[m % 2]), (vs, vs_feed[ni])],
                                       [(vs, vs_w1q)] if (ni == 0 and m < 2) else [])
                          if kk == 0 else []),
                    inc=[(ts, 1)] if kk == DC - 1 else [])
            tile = h_tile[m % 4]
            P.do("scalar", lambda e, m=m, w=w, pb=pb, tile=tile: e.activation(
                out=tile[:, 0:w], in_=pb[:, 0:w], func=AF.Gelu, bias=b1ss[:, m:m + 1], scale=scl[:, 17:18]),
                wait=_merge_waits([(ts, ts.n), (gd, h_busy[m % 4]), (vs, h_vbusy[m % 4])],
                                  ht_gu if (ni == 0 and m < 4) else []), inc=[(ss, 1)])
            ssn = ss.n
            bank_ss1[m % 2] = ssn
            o0 = max(PAD - c0, 0)
            if o0 < w:
                if m2_first[0]:
                    P.do("vector", lambda e, w=w, o0=o0, tile=tile: e.reduce_max(
                        out=scl[:, 18:19], in_=tile[:, o0:w], axis=AX.X, apply_absolute_value=True),
                        wait=[(ss, ssn)], inc=[(vs, 1)])
                    m2_first[0] = False
                else:
                    P.do("vector", lambda e, w=w, o0=o0, tile=tile: e.reduce_max(
                        out=scl[:, 19:20], in_=tile[:, o0:w], axis=AX.X, apply_absolute_value=True),
                        wait=[(ss, ssn)], inc=[(vs, 1)])
                    P.do("vector", lambda e: e.tensor_tensor(out=scl[:, 18:19], in0=scl[:, 18:19],
                                                             in1=scl[:, 19:20], op=ALU.max), inc=[(vs, 1)])
            P.do("gpsimd", lambda e, m=m, c0=c0, w=w, tile=tile: e.dma_start(
                out=h_sp[m * 128:(m + 1) * 128, c0:c0 + w], in_=tile[:, 0:w]),
                wait=[(ss, ssn)], inc=[(gd, 16)])
            h_busy[m % 4] = gd.n
            h_vbusy[m % 4] = vs.n
        xsl2_busy[ni % 2] = [(ts, ts.n)]
    gd_hsp = gd.n
    ts_m1 = ts.n
    for k in range(DC):
        A.free(w1qb[k], [(ts, ts_m1)])
        A.free(xslB[k], [(ts, ts_m1)])
    for i_ in range(4):
        A.free(h_tile[i_], [(gd, h_busy[i_]), (vs, h_vbusy[i_])])

    P.do("tensor", lambda e: e.transpose(psE[0:1, 0:128], scl[:, 18:19], idn[:, :]),
         wait=[(vs, vs.n), (ss, ss.n)], inc=[(ts, 1)])
    P.do("vector", lambda e: e.reduce_max(out=scl[0:1, 19:20], in_=psE[0:1, 0:128], axis=AX.X),
         wait=[(ts, ts.n)], inc=[(vs, 1)])
    vs_m2 = vs.n
    P.do("gpsimd", lambda e: e.dma_start(out=cc_m2_i[:, 0:8], in_=scl[0:1, 19:27]),
         wait=[(vs, vs_m2)], inc=[(gd, 16)])
    P.do("gpsimd", lambda e: e.collective_compute(
        "AllGather", ALU.bypass, replica_groups=[list(range(8))],
        ins=[cc_m2_i.opt()], outs=[cc_m2_o.opt()]), wait=[(gd, gd.n)], inc=[(cs, 1)])
    cs_m2 = cs.n

    P.do("sync", lambda e: e.dma_start(out=rowt[0:8, 160:168], in_=cc_m2_o[:, :]),
         wait=[(cs, cs_m2)], inc=[(ld, 16)])
    P.do("tensor", lambda e: e.transpose(psE[0:8, 0:8], rowt[0:8, 160:168], idn[0:8, 0:8]),
         wait=[(ld, ld.n), (vs, vs.n), (ss, ss.n)], inc=[(ts, 1)])
    P.do("vector", lambda e: e.reduce_max(out=rowt[0:1, 169:170], in_=psE[0:1, 0:8], axis=AX.X),
         wait=[(ts, ts.n)], inc=[(vs, 1)])
    P.do("vector", lambda e: e.tensor_scalar(out=rowt[0:1, 170:171], in0=rowt[0:1, 169:170],
                                             scalar1=MIN_RANGE, scalar2=1.0 / QP, op0=ALU.max, op1=ALU.mult),
         inc=[(vs, 1)])
    P.do("vector", lambda e: e.reciprocal(out=rowt[0:1, 171:172], in_=rowt[0:1, 170:171]), inc=[(vs, 1)])
    P.do("vector", lambda e: e.tensor_scalar(out=rowt[0:1, 172:173], in0=rowt[0:1, 154:155],
                                             scalar1=MIN_RANGE, scalar2=1.0 / QP, op0=ALU.max, op1=ALU.mult),
         inc=[(vs, 1)])
    P.do("vector", lambda e: e.reciprocal(out=rowt[0:1, 173:174], in_=rowt[0:1, 172:173]), inc=[(vs, 1)])
    P.do("vector", lambda e: e.tensor_tensor(out=rowt[0:1, 174:175], in0=rowt[0:1, 172:173],
                                             in1=rowt[0:1, 170:171], op=ALU.mult), inc=[(vs, 1)])
    vs_s2 = vs.n
    bcast_row(scl[:, 20:25], rowt[0:1, 170:175], 5, [(vs, vs_s2)])
    vs_ss2 = vs.n
    P.do("vector", lambda e: e.tensor_scalar(out=vcm[:, 88:96], in0=vc("b2"), scalar1=scl[:, 24:25],
                                             scalar2=None, op0=ALU.mult), wait=[(vs, vs_ss2)], inc=[(vs, 1)])
    vs_b2ss = vs.n

    for k in range(DC):
        A.free(xsl[k], [(ts, ts_m1)])
    w2qb = []
    w2_gu = []
    for k in range(HC):
        h, gu = alloc(f"w2qb{k}", [128, D], BF16)
        w2qb.append(h)
        w2_gu += gu
    for kk in range(HC):
        for cq in range(0, D, 512):
            i_ = rg_ix[0] % 4
            rg_ix[0] += 1
            bB, bA = rgB[i_], rgA[i_]
            P.do("sync", lambda e, kk=kk, cq=cq, bB=bB: e.dma_start(
                out=bB[:, :], in_=w2T[kk * 128:(kk + 1) * 128, cq:cq + 512]),
                wait=rgB_busy[i_] + (_merge_waits(w2_gu, [(vs, vs_ss2)]) if (kk == 0 and cq == 0) else []),
                inc=[(ld, 16)])
            P.do("scalar", lambda e, bB=bB, bA=bA: e.activation(out=bA[:, :], in_=bB[:, :], func=AF.Identity,
                                                                bias=MAGIC, scale=scl[:, 23:24]),
                 wait=_merge_waits([(ld, ld.n)], rgA_busy[i_]), inc=[(ss, 1)])
            rgB_busy[i_] = [(ss, ss.n)]
            P.do("scalar", lambda e, kk=kk, cq=cq, bA=bA: e.activation(
                out=w2qb[kk][:, cq:cq + 512], in_=bA[:, :], func=AF.Identity, bias=-MAGIC, scale=1.0),
                inc=[(ss, 1)])
            rgA_busy[i_] = [(ss, ss.n)]
    vs_w2q = ss.n

    vsg, vsg_gu = alloc("vsg", [8, D], F32)
    P.do("sync", lambda e: e.dma_start(out=vsg[0:8, :], in_=cc_vs_o[:, :]),
         wait=_merge_waits(vsg_gu, [(cs, cs_vs)]), inc=[(ld, 16)])
    ld_vsg = ld.n
    vstot, vst_gu = alloc("vstot", [128, 8], F32)
    for g in range(DC):
        P.do("tensor", lambda e, g=g: e.transpose(psE[0:128, 0:8], vsg[0:8, g * 128:(g + 1) * 128],
                                                  idn[0:8, 0:8]),
             wait=[(ld, ld_vsg), (vs, vs.n), (ss, ss.n)], inc=[(ts, 1)])
        P.do("vector", lambda e, g=g: e.tensor_tensor(out=vsum_cm[:, 8:16], in0=psE[0:128, 0:8],
                                                      in1=mbbc[:, :], op=ALU.mult),
             wait=[(ts, ts.n)], inc=[(vs, 1)])
        P.do("vector", lambda e, g=g: e.reduce_sum(out=vstot[:, g:g + 1], in_=vsum_cm[:, 8:16], axis=AX.X),
             wait=(_merge_waits(vst_gu) if g == 0 else []), inc=[(vs, 1)])
    vs_vstot = vs.n
    for m in range(DC):
        for kk in range(DC):
            i_ = rg_ix[0] % 4
            rg_ix[0] += 1
            bB = rgB[i_]
            P.do("sync", lambda e, m=m, kk=kk, bB=bB: e.dma_start(
                out=bB[:, 0:128], in_=wfcT[kk * 128:(kk + 1) * 128, m * 128:(m + 1) * 128]),
                wait=rgB_busy[i_], inc=[(ld, 16)])
            ldw = ld.n
            P.do("tensor", lambda e, m=m, kk=kk, bB=bB: e.matmul(
                psD[:, m:m + 1], bB[:, 0:128], vstot[:, kk:kk + 1],
                start=(kk == 0), stop=(kk == DC - 1)),
                wait=_merge_waits([(ld, ldw)],
                                  [(vs, vs_vstot), (ss, ss.n)] if (m == 0 and kk == 0) else []),
                inc=[(ts, 1)])
            rgB_busy[i_] = [(ts, ts.n)]
    y2cm, y2_gu = alloc("y2cm", [128, 8], F32)
    P.do("vector", lambda e: e.tensor_tensor(out=y2cm[:, :], in0=psD[:, 0:8], in1=vc("bfc"), op=ALU.add),
         wait=_merge_waits([(ts, ts.n)], y2_gu), inc=[(vs, 1)])
    vs_y2 = vs.n
    ts_y2 = ts.n
    A.free(vsg, [(ts, ts_y2)])

    if dbg == "xn":
        for g in range(DC):
            P.do("sync", lambda e, g=g: e.dma_start(out=dbg_ext[g * 128:(g + 1) * 128, :], in_=xn[g][:, :]),
                 wait=[(vs, vs.n), (ss, ss.n)], inc=[(ld, 16)])
    dbg_ld0 = ld.n

    for g in range(DC):
        P.do("vector", lambda e, g=g: e.tensor_scalar(
            out=xn[g][:, :], in0=xn[g][:, :], scalar1=y2cm[:, g:g + 1], scalar2=None, op0=ALU.add),
            wait=[(vs, vs_y2), (ss, ss.n), (ld, dbg_ld0)] if g == 0 else [], inc=[(vs, 1)])
    vs_h0 = vs.n
    mu_t, mu_gu = A.alloc("mu2_t", [128, WQ], F32)
    rstd_t, rs_gu = A.alloc("rstd2_t", [128, WQ], F32)
    emit_ln(xn, mu_t, rstd_t, _merge_waits([(vs, vs_h0)], mu_gu, rs_gu))
    vs_ln2st = vs.n
    emit_ln_norm(xn, xn, [], mu_t, rstd_t, gamma=vc("gamma"), beta=vc("beta"))
    gs_attn0 = vs.n
    A.free(mu_t, [(vs, gs_attn0)])
    A.free(rstd_t, [(vs, gs_attn0)])
    if dbg == "attn0":
        for g in range(DC):
            P.do("sync", lambda e, g=g: e.dma_start(out=dbg_ext[g * 128:(g + 1) * 128, :], in_=xn[g][:, :]),
                 wait=[(vs, gs_attn0)], inc=[(ld, 16)])
    dbg_ld = ld.n

    dump2(1, xn, [(vs, gs_attn0)])
    ld_dmp1 = ld.n
    x1 = xn
    xrt = []
    xrt_gu = []
    for i_ in range(2):
        h, gu = alloc(f"xrt{i_}", [128, WQ], F32)
        xrt.append(h)
        xrt_gu += gu
    xrt_busy = [0, 0]
    for g in range(DC):
        buf = xrt[g % 2]
        P.do("sync", lambda e, g=g, buf=buf: e.dma_start(out=buf[:, :], in_=x_ext[g * 128:(g + 1) * 128, :]),
             wait=_merge_waits(xrt_gu if g < 2 else [], [(vs, xrt_busy[g % 2])]), inc=[(ld, 16)])
        ldx = ld.n
        P.do("vector", lambda e, g=g, buf=buf: e.tensor_tensor(out=x1[g][:, :], in0=x1[g][:, :],
                                                               in1=buf[:, :], op=ALU.add),
             wait=_merge_waits([(ld, ldx)], [(vs, gs_attn0), (ld, ld_dmp1)] if g == 0 else []),
             inc=[(vs, 1)])
        xrt_busy[g % 2] = vs.n
    gd_xacc = vs.n
    for i_ in range(2):
        A.free(xrt[i_], [(vs, gd_xacc)])
    vs_x1a = vs.n
    gs_attn0_ctx = gs_attn0
    dump2(2, x1, [(vs, gd_xacc)])
    if dbg == "all":
        P.do("sync", lambda e: e.dma_start(out=dbg3[:, 0:32], in_=scl[:, 0:32]),
             wait=[(vs, vs.n), (ss, ss.n)], inc=[(ld, 16)])
        P.do("sync", lambda e: e.dma_start(out=dbg3[:, 32:48], in_=vsum_cm[:, 0:16]), inc=[(ld, 16)])
        P.do("sync", lambda e: e.dma_start(out=dbg3[:, 48:56], in_=y2cm[:, 0:8]), inc=[(ld, 16)])
        P.do("sync", lambda e: e.dma_start(out=dbg3[:, 56:88], in_=expv[:, 0:32]), inc=[(ld, 16)])
        P.do("sync", lambda e: e.dma_start(out=dbg3[:, 88:96], in_=vstot[:, 0:8]), inc=[(ld, 16)])
        P.do("sync", lambda e: e.dma_start(out=dbg3[:, 96:104], in_=mbbc[:, 0:8]), inc=[(ld, 16)])
        P.do("sync", lambda e: e.dma_start(out=dbg3[:, 104:112], in_=b1ss[:, 0:8]), inc=[(ld, 16)])


    ctx = dict(
        nc=nc, P=P, A=A, alloc=alloc, dbg=dbg, dbg_ext=dbg_ext,

        ld=ld, gd=gd, vs=vs, ss=ss, ts=ts, gs=gs, cs=cs,
        psA=psA, psB=psB, psC=psC, psD=psD, psE=psE,
        idn=idn, idnb=idnb, ones128b=ones128b, onescol=onescol,
        epsc=epsc, scl=scl, vcm=vcm, vc=vc, b1cm=b1cm, b1ss=b1ss, e2048bc=e2048bc, mjbc=mjbc, expv=expv,
        rowt=rowt, xsl=xsl,
        ln_sc=ln_sc, castp=castp, rgA=rgA, rgB=rgB, rgA_busy=rgA_busy, rgB_busy=rgB_busy, rg_ix=rg_ix,
        emit_ln=emit_ln, emit_ln_norm=emit_ln_norm, absmax_chunks=absmax_chunks,
        bcast_row=bcast_row, load_wT_bf16=load_wT_bf16,
        wfcT=wfcT, w2T=w2T, wl1T=wl1T, vecs=vecs, out_ext=out_ext,
        xn_sp=xn_sp, h_sp=h_sp, x1_sp=x1_sp, k_sp=k_sp, v_sp=v_sp, sr_sp=sr_sp, r2_sp=r2_sp,
        cc_m2_o=cc_m2_o, cc_wk_i=cc_wk_i, cc_wk_o=cc_wk_o,
        gd_hsp=gd_hsp, cs_m2=cs_m2,
        x1=x1, vs_x1a=vs_x1a, wkv_dbg=wkv_dbg, dump2=dump2, dbg2=dbg2, dbg3=dbg3,
        w2qb=w2qb, vs_w2q=vs_w2q, gd_xacc=gd_xacc, vs_b2ss=vs_b2ss, vs_ln2st=vs_ln2st,
        gs_attn0=gs_attn0,
    )
    _build_part2(ctx)
    P.verify_no_deadlock()
    P.emit()
    return nc




def _build_part2(c):
    nc = c["nc"]; P = c["P"]; A = c["A"]; alloc = c["alloc"]
    ld = c["ld"]; gd = c["gd"]; vs = c["vs"]; ss = c["ss"]; ts = c["ts"]; gs = c["gs"]; cs = c["cs"]
    psA = c["psA"]; psB = c["psB"]; psC = c["psC"]; psD = c["psD"]; psE = c["psE"]
    idn = c["idn"]; idnb = c["idnb"]; ones128b = c["ones128b"]; onescol = c["onescol"]
    epsc = c["epsc"]; scl = c["scl"]
    vc = c["vc"]; vcm = c["vcm"]; b1cm = c["b1cm"]; e2048bc = c["e2048bc"]; mjbc = c["mjbc"]; expv = c["expv"]
    rowt = c["rowt"]; xsl = c["xsl"]
    ln_sc = c["ln_sc"]; castp = c["castp"]
    rgA = c["rgA"]; rgB = c["rgB"]; rgA_busy = c["rgA_busy"]; rgB_busy = c["rgB_busy"]; rg_ix = c["rg_ix"]
    emit_ln = c["emit_ln"]; emit_ln_norm = c["emit_ln_norm"]
    bcast_row = c["bcast_row"]; load_wT_bf16 = c["load_wT_bf16"]
    wfcT = c["wfcT"]; w2T = c["w2T"]; wl1T = c["wl1T"]; out_ext = c["out_ext"]
    h_sp = c["h_sp"]; x1_sp = c["x1_sp"]
    k_sp = c["k_sp"]; v_sp = c["v_sp"]; sr_sp = c["sr_sp"]; r2_sp = c["r2_sp"]
    cc_m2_o = c["cc_m2_o"]; cc_wk_i = c["cc_wk_i"]; cc_wk_o = c["cc_wk_o"]
    gd_hsp = c["gd_hsp"]; cs_m2 = c["cs_m2"]
    x1 = c["x1"]; vs_x1a = c["vs_x1a"]
    dbg = c["dbg"]; dbg_ext = c["dbg_ext"]; wkv_dbg = c["wkv_dbg"]; dump2 = c["dump2"]
    dbg2 = c["dbg2"]; dbg3 = c["dbg3"]

    vs_x1 = vs_x1a


    w2qb = c["w2qb"]
    vs_w2q = c["vs_w2q"]
    gd_xacc = c["gd_xacc"]
    vs_b2ss = c["vs_b2ss"]
    vs_ln2st = c["vs_ln2st"]
    gs_attn0 = c["gs_attn0"]

    hqsl2 = []
    hq_gu = []
    for p_ in range(2):
        row = []
        for k in range(HC):
            h, gu = alloc(f"hqsl{p_}_{k}", [128, 256], BF16)
            row.append(h)
            hq_gu += gu
        hqsl2.append(row)
    etl = []
    etl_gu = []
    for i_ in range(4):
        h, gu = alloc(f"etl{i_}", [128, 256], F32)
        etl.append(h)
        etl_gu += gu
    etl_busy = [0, 0, 0, 0]
    NTS_M2 = [(c, 256) for c in range(0, CH, 256)] + [(CH, 128)]
    hq2_busy = [_merge_waits(hq_gu, [(ss, vs_w2q)]), list(hq_gu)]
    gs_feed2 = [0] * len(NTS_M2)
    bank_ss2 = [0, 0]

    def m2_feed(t, k0=0, k1=HC):
        c0, w = NTS_M2[t]
        dst = hqsl2[t % 2]
        for kk in range(k0, k1):
            i_ = rg_ix[0] % 4
            rg_ix[0] += 1
            bB, bA = rgB[i_], rgA[i_]
            P.do("sync", lambda e, kk=kk, c0=c0, w=w, bB=bB: e.dma_start(
                out=bB[:, 0:w], in_=h_sp[kk * 128:(kk + 1) * 128, c0:c0 + w]),
                wait=rgB_busy[i_] + ([(gd, gd_hsp)] + hq2_busy[t % 2] if kk == 0 else []),
                inc=[(ld, 16)])
            P.do("scalar", lambda e, w=w, bB=bB, bA=bA: e.activation(out=bA[:, 0:w], in_=bB[:, 0:w],
                                                                     func=AF.Identity, bias=MAGIC,
                                                                     scale=scl[:, 21:22]),
                 wait=_merge_waits([(ld, ld.n)], rgA_busy[i_]), inc=[(ss, 1)])
            rgB_busy[i_] = [(ss, ss.n)]
            P.do("vector", lambda e, kk=kk, w=w, bA=bA, dst=dst: e.tensor_scalar(
                out=dst[kk][:, 0:w], in0=bA[:, 0:w], scalar1=-MAGIC, scalar2=None, op0=ALU.add),
                wait=[(ss, ss.n)], inc=[(vs, 1)])
            rgA_busy[i_] = [(vs, vs.n)]
        if k1 == HC:
            gs_feed2[t] = vs.n

    m2_feed(0)
    for t, (c0, w) in enumerate(NTS_M2):
        rhs = hqsl2[t % 2]
        for m in range(DC):
            if t + 1 < len(NTS_M2):
                m2_feed(t + 1, m * 4, m * 4 + 4)
            pb = psD if m % 2 == 0 else psE
            for kk in range(HC):
                P.do("tensor", lambda e, kk=kk, m=m, w=w, pb=pb, rhs=rhs: e.matmul(
                    pb[:, 0:w], w2qb[kk][:, m * 128:(m + 1) * 128], rhs[kk][:, 0:w],
                    start=(kk == 0), stop=(kk == HC - 1)),
                    wait=(_merge_waits([(vs, gs_feed2[t]), (ss, bank_ss2[m % 2])],
                                       [(vs, vs_ln2st)] if (t == 0 and m < 2) else [])
                          if kk == 0 else []),
                    inc=[(ts, 1)] if kk == HC - 1 else [])
            tix = (t * DC + m) % 4
            tile = etl[tix]
            P.do("scalar", lambda e, m=m, w=w, pb=pb, tile=tile: e.activation(
                out=tile[:, 0:w], in_=pb[:, 0:w], func=AF.Identity, bias=vcm[:, 88 + m:89 + m],
                scale=scl[:, 24:25]),
                wait=_merge_waits([(ts, ts.n), (vs, etl_busy[tix])],
                                  _merge_waits(etl_gu, [(vs, vs_b2ss)]) if (t == 0 and m < 4) else []),
                inc=[(ss, 1)])
            bank_ss2[m % 2] = ss.n
            P.do("vector", lambda e, m=m, c0=c0, w=w, tile=tile: e.tensor_tensor(
                out=x1[m][:, c0:c0 + w], in0=x1[m][:, c0:c0 + w], in1=tile[:, 0:w], op=ALU.add),
                wait=_merge_waits([(ss, ss.n)],
                                  [(vs, gd_xacc)] if (t == 0 and m == 0) else []),
                inc=[(vs, 1)])
            etl_busy[tix] = vs.n
        hq2_busy[t % 2] = [(ts, ts.n)]
    vs_x1 = vs.n
    gs_m2 = vs.n
    ts_m2 = ts.n
    for k in range(HC):
        A.free(w2qb[k], [(ts, ts_m2)])
        A.free(hqsl2[0][k], [(ts, ts_m2)])
        A.free(hqsl2[1][k], [(ts, ts_m2)])
    for i_ in range(4):
        A.free(etl[i_], [(vs, gs_m2)])

    for i_ in range(4):
        A.free(rgA[i_], rgA_busy[i_] + [(ss, ss.n), (vs, gs_m2)])
    for g in range(DC):
        P.do("gpsimd", lambda e, g=g: e.dma_start(out=x1_sp[g * 128:(g + 1) * 128, :], in_=x1[g][:, :]),
             wait=[(vs, gs_m2)] if g == 0 else [], inc=[(gd, 16)])
    gd_x1sp = gd.n
    if dbg == "x1":
        P.do("sync", lambda e: e.dma_start(out=dbg_ext[:, :], in_=x1_sp[:, :]), wait=[(gd, gd_x1sp)],
             inc=[(ld, 16)])

    mu_t, mu_gu = A.alloc("mu3_t", [128, WQ], F32)
    rstd_t, rs_gu = A.alloc("rstd3_t", [128, WQ], F32)
    emit_ln(x1, mu_t, rstd_t, _merge_waits([(vs, vs_x1)], mu_gu, rs_gu))
    xn1 = []
    xn1_gu = []
    for g in range(DC):
        h, gu = alloc(f"xn1_{g}", [128, WQ], F32)
        xn1.append(h)
        xn1_gu += gu
    emit_ln_norm(x1, xn1, xn1_gu, mu_t, rstd_t)
    gs_xn1 = vs.n
    for g in range(DC):
        P.do("vector", lambda e, g=g: e.tensor_scalar(
            out=xn1[g][:, PAD - 1:PAD], in0=xn1[g][:, PAD - 1:PAD], scalar1=scl[:, 4:5], scalar2=None,
            op0=ALU.mult), wait=[(vs, gs_xn1)] if g == 0 else [], inc=[(vs, 1)])
    vs_xn1 = vs.n
    for g in range(DC):
        A.free(x1[g], [(vs, gs_xn1), (gd, gd_x1sp)])
    A.free(mu_t, [(vs, gs_xn1)])
    A.free(rstd_t, [(vs, gs_xn1)])
    if dbg == "xn1":
        for g in range(DC):
            P.do("sync", lambda e, g=g: e.dma_start(out=dbg_ext[g * 128:(g + 1) * 128, :], in_=xn1[g][:, :]),
                 wait=[(vs, vs_xn1)], inc=[(ld, 16)])
    dump2(3, xn1, [(vs, vs_xn1)])

    xslq = []
    xsl_gu2 = []
    for p_ in range(2):
        row = []
        for k in range(DC):
            h, gu = alloc(f"xsl3_{p_}_{k}", [128, 512], BF16)
            row.append(h)
            xsl_gu2 += gu
        xslq.append(row)
    xslq_busy = [list(xsl_gu2), list(xsl_gu2)]
    wLbq = []
    wL_gu = []
    for p_ in range(2):
        row = []
        for k in range(DC):
            h, gu = alloc(f"wLb{p_}_{k}", [128, D], BF16)
            row.append(h)
            wL_gu += gu
        wLbq.append(row)
    wLbq_busy = [list(wL_gu), list(wL_gu)]
    wLb_ix = [0]

    def next_wLb():
        p_ = wLb_ix[0] % 2
        wLb_ix[0] += 1
        return wLbq[p_], p_

    kct = []
    kct_gu = []
    for g in range(DC):
        h, gu = alloc(f"kct{g}", [128, CH], BF16)
        kct.append(h)
        kct_gu += gu
    pt0, pt_gu0 = alloc("pt0", [128, 512], F32)
    pt1, pt_gu1 = alloc("pt1", [128, 512], F32)
    ptile = [pt0, pt1]
    ptile_busy = [list(pt_gu0), list(pt_gu1)]
    bank_busy = [0, 0]

    def l1_load(plane):
        wLb, wp = next_wLb()
        ss_w = load_wT_bf16(wl1T[plane, :, :], wLb, first_wait=wLbq_busy[wp])
        return (wLb, wp, ss_w)

    def l1_proj(wh, tmname, mode, dst_sp=None, dst_res=None, extra_first=()):
        wLb, wp, ss_w = wh
        tmv = vc(tmname)
        for ni, (c0, w) in enumerate(NTS_CH):
            a0 = PAD + c0
            xp = ni % 2
            xsl = xslq[xp]
            for kk in range(DC):
                P.do("vector", lambda e, kk=kk, a0=a0, w=w: e.tensor_tensor(
                    out=ln_sc[:, 0:w], in0=xn1[kk][:, a0:a0 + w], in1=xn1[kk][:, a0 - 1:a0 - 1 + w],
                    op=ALU.subtract),
                    wait=(_merge_waits([(vs, vs_xn1)], xslq_busy[xp],
                                       list(extra_first) if ni == 0 else []) if kk == 0 else []),
                    inc=[(vs, 1)])
                P.do("vector", lambda e, kk=kk, a0=a0, w=w, tmv=tmv, xsl=xsl: e.scalar_tensor_tensor(
                    out=xsl[kk][:, 0:w], in0=ln_sc[:, 0:w], scalar=tmv[:, kk:kk + 1],
                    in1=xn1[kk][:, a0 - 1:a0 - 1 + w], op0=ALU.mult, op1=ALU.add), inc=[(vs, 1)])
            vs_sl = vs.n
            for m in range(DC):
                pb = psD if m % 2 == 0 else psE
                for kk in range(DC):
                    P.do("tensor", lambda e, kk=kk, m=m, w=w, pb=pb, xsl=xsl, wLb=wLb: e.matmul(
                        pb[:, 0:w], wLb[kk][:, m * 128:(m + 1) * 128], xsl[kk][:, 0:w],
                        start=(kk == 0), stop=(kk == DC - 1)),
                        wait=(_merge_waits([(vs, vs_sl), (ss, bank_busy[m % 2])],
                                           [(ss, ss_w)] if (ni == 0 and m == 0) else [])
                              if kk == 0 else []),
                        inc=[(ts, 1)] if kk == DC - 1 else [])
                if mode == "sig_spill":
                    tile = ptile[m % 2]
                    P.do("scalar", lambda e, w=w, pb=pb, tile=tile: e.activation(
                        out=tile[:, 0:w], in_=pb[:, 0:w], func=AF.Sigmoid),
                        wait=_merge_waits([(ts, ts.n)], ptile_busy[m % 2]), inc=[(ss, 1)])
                    P.do("gpsimd", lambda e, m=m, c0=c0, w=w, tile=tile: e.dma_start(
                        out=dst_sp[m * 128:(m + 1) * 128, c0:c0 + w], in_=tile[:, 0:w]),
                        wait=[(ss, ss.n)], inc=[(gd, 16)])
                    ptile_busy[m % 2] = [(gd, gd.n)]
                    bank_busy[m % 2] = ss.n
                elif mode == "copy_spill":
                    tile = ptile[m % 2]
                    P.do("scalar", lambda e, w=w, pb=pb, tile=tile: e.activation(
                        out=tile[:, 0:w], in_=pb[:, 0:w], func=AF.Copy),
                        wait=_merge_waits([(ts, ts.n)], ptile_busy[m % 2]), inc=[(ss, 1)])
                    P.do("gpsimd", lambda e, m=m, c0=c0, w=w, tile=tile: e.dma_start(
                        out=dst_sp[m * 128:(m + 1) * 128, c0:c0 + w], in_=tile[:, 0:w]),
                        wait=[(ss, ss.n)], inc=[(gd, 16)])
                    ptile_busy[m % 2] = [(gd, gd.n)]
                    bank_busy[m % 2] = ss.n
                elif mode == "sig_res":
                    P.do("scalar", lambda e, m=m, c0=c0, w=w, pb=pb: e.activation(
                        out=dst_res[m][:, c0:c0 + w], in_=pb[:, 0:w], func=AF.Sigmoid),
                        wait=[(ts, ts.n)], inc=[(ss, 1)])
                    bank_busy[m % 2] = ss.n
                elif mode == "relusq_res":
                    tile = ptile[m % 2]
                    P.do("scalar", lambda e, w=w, pb=pb, tile=tile: e.activation(
                        out=tile[:, 0:w], in_=pb[:, 0:w], func=AF.Relu),
                        wait=_merge_waits([(ts, ts.n), (vs, vs.n)], ptile_busy[m % 2]), inc=[(ss, 1)])
                    P.do("vector", lambda e, m=m, c0=c0, w=w, tile=tile: e.tensor_tensor(
                        out=dst_res[m][:, c0:c0 + w], in0=tile[:, 0:w], in1=tile[:, 0:w], op=ALU.mult),
                        wait=[(ss, ss.n)], inc=[(vs, 1)])
                    ptile_busy[m % 2] = [(vs, vs.n)]
                    bank_busy[m % 2] = ss.n
            xslq_busy[xp] = [(ts, ts.n)]
        wLbq_busy[wp] = [(ts, ts.n)]

    wh0 = l1_load(0)
    wh1 = l1_load(1)
    l1_proj(wh0, "tmr", "sig_spill", dst_sp=sr_sp)
    gd_sr = gd.n
    wh2 = l1_load(2)
    l1_proj(wh1, "tmk", "copy_spill", dst_sp=k_sp)
    gd_k = gd.n
    wh3 = l1_load(3)
    l1_proj(wh2, "tmv", "copy_spill", dst_sp=v_sp)
    gd_v = gd.n
    wh4 = l1_load(4)
    l1_proj(wh3, "mur", "sig_spill", dst_sp=r2_sp)
    gd_r2 = gd.n
    wh5 = l1_load(5)
    l1_proj(wh4, "muk", "relusq_res", dst_res=kct, extra_first=kct_gu)
    vs_l1p = vs.n
    ss_l1p = ss.n

    wLb, wp5, ss_wv2 = wh5
    wh6 = l1_load(6)
    ffn1_sp = nc.dram_tensor("ffn1_sp", [D, CH], F32).ap()
    rt_busy = [0, 0]
    rt_vs = [0, 0]
    f1_bank = [0, 0]
    for ni, (c0, w) in enumerate(NTS_CH):
        for m in range(DC):
            pb = psD if m % 2 == 0 else psE
            tile = ptile[m % 2]
            P.do("sync", lambda e, m=m, c0=c0, w=w, tile=tile: e.dma_start(
                out=tile[:, 0:w], in_=r2_sp[m * 128:(m + 1) * 128, c0:c0 + w]),
                wait=_merge_waits([(gd, gd_r2), (gd, rt_busy[m % 2]), (vs, rt_vs[m % 2])],
                                  ptile_busy[m % 2] if (ni == 0 and m < 2) else []), inc=[(ld, 16)])
            ld_r2 = ld.n
            for kk in range(DC):
                P.do("tensor", lambda e, kk=kk, m=m, c0=c0, w=w, pb=pb: e.matmul(
                    pb[:, 0:w], wLb[kk][:, m * 128:(m + 1) * 128], kct[kk][:, c0:c0 + w],
                    start=(kk == 0), stop=(kk == DC - 1)),
                    wait=([(ss, ss_wv2), (ss, ss_l1p), (vs, vs_l1p)] if (ni == 0 and m == 0 and kk == 0)
                          else ([(vs, f1_bank[m % 2])] if kk == 0 else [])),
                    inc=[(ts, 1)] if kk == DC - 1 else [])
            P.do("vector", lambda e, m=m, w=w, pb=pb, tile=tile: e.tensor_tensor(
                out=tile[:, 0:w], in0=pb[:, 0:w], in1=tile[:, 0:w], op=ALU.mult),
                wait=[(ts, ts.n), (ld, ld_r2)], inc=[(vs, 1)])
            f1_bank[m % 2] = vs.n
            rt_vs[m % 2] = vs.n
            P.do("gpsimd", lambda e, m=m, c0=c0, w=w, tile=tile: e.dma_start(
                out=ffn1_sp[m * 128:(m + 1) * 128, c0:c0 + w], in_=tile[:, 0:w]),
                wait=[(vs, vs.n)], inc=[(gd, 16)])
            rt_busy[m % 2] = gd.n
    gd_ffn1 = gd.n
    ts_ffn1 = ts.n
    wLbq_busy[wp5] = [(ts, ts_ffn1)]
    for k in range(DC):
        A.free(wLbq[wp5][k], [(ts, ts_ffn1)])
        A.free(xslq[0][k], [(ts, ts_ffn1)])
        A.free(xslq[1][k], [(ts, ts_ffn1)])
    for g in range(DC):
        A.free(kct[g], [(ts, ts_ffn1)])
        A.free(xn1[g], [(vs, vs_l1p)])

    for cp_ in castp:
        A.free(cp_[0], [(ts, ts.n)])
        A.free(cp_[1], [(ts, ts.n)])
    A.free(ln_sc, [(vs, vs.n), (ss, ss.n)])
    ones2048, o2_gu = A.alloc("ones2048b", [128, 2048], F32)
    jrow, jr_gu = A.alloc("jrowb", [128, 2048], F32)
    P.do("gpsimd", lambda e: e.memset(ones2048[:, :], 1.0), wait=_merge_waits(o2_gu, jr_gu), inc=[(gs, 1)])
    P.do("gpsimd", lambda e: e.iota(jrow[:, :], pattern=[[1, 2048]], base=0, channel_multiplier=0,
                                    allow_small_or_imprecise_dtypes=True), inc=[(gs, 1)])
    gs_wkv = gs.n
    sets = []
    for si in range(2):
        kf, gu1 = alloc(f"wkv_kf{si}", [128, CH], F32)
        vf, gu2 = alloc(f"wkv_vf{si}", [128, CH], F32)
        p0a, gu3 = alloc(f"wkv_p0a{si}", [128, CH + 16], F32)
        p0b, gu4 = alloc(f"wkv_p0b{si}", [128, CH + 16], F32)
        et, gu5 = alloc(f"wkv_et{si}", [128, CH], F32)
        srf, gu6 = alloc(f"wkv_sr{si}", [128, CH], F32)
        aj, gu7 = alloc(f"wkv_aj{si}", [128, 24], F32)
        sets.append(dict(kf=kf, vf=vf, p0a=p0a, p0b=p0b, et=et, srf=srf, aj=aj,
                         gu=gu1 + gu2 + gu3 + gu4 + gu5 + gu6 + gu7, vs_done=0, ld_done=0))
    wkv_rs = []
    wkvrs_gu = []
    for g in range(DC):
        h, gu = alloc(f"wkv_rs{g}", [128, CH], BF16)
        wkv_rs.append(h)
        wkvrs_gu += gu

    cs_chunk = [0] * DC

    def wkv_pre(g):
        S = sets[g % 2]
        kf, vf, p0a, p0b, et, srf = S["kf"], S["vf"], S["p0a"], S["p0b"], S["et"], S["srf"]
        wload = _merge_waits(S["gu"] if g < 2 else [], [(vs, S["vs_done"]), (gs, S["gs_done"]),
                                                        (gd, gd_v), (gd, gd_sr)])
        P.do("sync", lambda e, g=g, kf=kf: e.dma_start(out=kf[:, :], in_=k_sp[g * 128:(g + 1) * 128, :]),
             wait=wload, inc=[(ld, 16)])
        P.do("sync", lambda e, g=g, vf=vf: e.dma_start(out=vf[:, :], in_=v_sp[g * 128:(g + 1) * 128, :]),
             inc=[(ld, 16)])
        P.do("sync", lambda e, g=g, srf=srf: e.dma_start(out=srf[:, :], in_=sr_sp[g * 128:(g + 1) * 128, :]),
             inc=[(ld, 16)])
        ld_kv = ld.n
        P.do("scalar", lambda e, kf=kf: e.activation(out=kf[:, :], in_=kf[:, :], func=AF.Exp),
             wait=[(ld, ld_kv)], inc=[(ss, 1)])
        P.do("gpsimd", lambda e, kf=kf, vf=vf: e.tensor_tensor(out=vf[:, :], in0=kf[:, :], in1=vf[:, :],
                                                               op=ALU.mult),
             wait=[(ss, ss.n)], inc=[(gs, 1)])
        gs_ekv = gs.n
        if wkv_dbg is not None:
            P.do("sync", lambda e, g=g, kf=kf: e.dma_start(
                out=wkv_dbg[0, g * 128:(g + 1) * 128, :], in_=kf[:, :]), wait=[(vs, vs.n)], inc=[(ld, 16)])
            P.do("sync", lambda e, g=g, vf=vf: e.dma_start(
                out=wkv_dbg[1, g * 128:(g + 1) * 128, :], in_=vf[:, :]), inc=[(ld, 16)])
        P.do("scalar", lambda e, g=g, et=et: e.activation(out=et[:, :], in_=ones2048[:, :], func=AF.Copy,
                                                          scale=expv[:, 16 + g:17 + g]),
             wait=[(gs, gs_wkv)] if g == 0 else [], inc=[(ss, 1)])
        ss_bc = ss.n
        P.do("vector", lambda e, p0a=p0a: e.memset(p0a[:, 15:16], 0.0), inc=[(vs, 1)])
        P.do("vector", lambda e, p0b=p0b: e.memset(p0b[:, 15:16], 0.0), inc=[(vs, 1)])
        P.do("vector", lambda e, p0a=p0a, et=et, vf=vf: e.tensor_tensor_scan(
            out=p0a[:, 16:16 + CH], data0=et[:, :], data1=vf[:, :], initial=0.0,
            op0=ALU.mult, op1=ALU.add), wait=[(ss, ss_bc), (gs, gs_ekv)], inc=[(vs, 1)])
        P.do("vector", lambda e, p0b=p0b, et=et, kf=kf: e.tensor_tensor_scan(
            out=p0b[:, 16:16 + CH], data0=et[:, :], data1=kf[:, :], initial=0.0,
            op0=ALU.mult, op1=ALU.add), inc=[(vs, 1)])
        vs_scan = vs.n
        if wkv_dbg is not None:
            P.do("sync", lambda e, g=g, p0a=p0a: e.dma_start(
                out=wkv_dbg[2, g * 128:(g + 1) * 128, :], in_=p0a[:, 16:16 + CH]),
                wait=[(vs, vs_scan)], inc=[(ld, 16)])
        P.do("gpsimd", lambda e, g=g, p0a=p0a: e.dma_start(out=cc_wk_i[g][0:1, 0:128],
                                                           in_=p0a[:, 15 + CH:16 + CH]),
             wait=[(vs, vs_scan)], inc=[(gd, 16)])
        P.do("gpsimd", lambda e, g=g, p0b=p0b: e.dma_start(out=cc_wk_i[g][0:1, 128:256],
                                                           in_=p0b[:, 15 + CH:16 + CH]),
             inc=[(gd, 16)])
        P.do("gpsimd", lambda e, g=g: e.collective_compute(
            "AllGather", ALU.bypass, replica_groups=[list(range(8))],
            ins=[cc_wk_i[g].opt()], outs=[cc_wk_o[g].opt()]), wait=[(gd, gd.n)], inc=[(cs, 1)])
        cs_chunk[g] = cs.n
        P.do("gpsimd", lambda e, g=g, et=et: e.tensor_scalar(out=et[:, :], in0=jrow[:, :],
                                                             scalar1=expv[:, 8 + g:9 + g], scalar2=-87.0,
                                                             op0=ALU.mult, op1=ALU.max),
             wait=[(vs, vs_scan)], inc=[(gs, 1)])
        P.do("scalar", lambda e, et=et: e.activation(out=et[:, :], in_=et[:, :], func=AF.Exp),
             wait=[(gs, gs.n)], inc=[(ss, 1)])
        S["ss_et"] = ss.n

    def wkv_post(g):
        S = sets[g % 2]
        kf, vf, p0a, p0b, et, srf, aj = S["kf"], S["vf"], S["p0a"], S["p0b"], S["et"], S["srf"], S["aj"]
        rb = 0 if g % 2 == 0 else 256
        P.do("sync", lambda e, g=g, rb=rb: e.dma_start(out=rowt[0:8, rb:rb + 256], in_=cc_wk_o[g][:, :]),
             wait=[(cs, cs_chunk[g]), (ts, S["ts_rb"])], inc=[(ld, 16)])
        ld_ag = ld.n
        P.do("tensor", lambda e, rb=rb: e.transpose(psC[:, 0:8], rowt[0:8, rb:rb + 128], idn[0:8, 0:8]),
             wait=[(ld, ld_ag), (vs, vs.n)], inc=[(ts, 1)])
        P.do("tensor", lambda e, rb=rb: e.transpose(psC[:, 128:136], rowt[0:8, rb + 128:rb + 256],
                                                    idn[0:8, 0:8]),
             inc=[(ts, 1)])
        ts_ag = ts.n
        S["ts_rb"] = ts_ag
        P.do("vector", lambda e, aj=aj: e.tensor_copy(aj[:, 0:8], psC[:, 0:8]), wait=[(ts, ts_ag)], inc=[(vs, 1)])
        P.do("vector", lambda e, aj=aj: e.tensor_copy(aj[:, 8:16], psC[:, 128:136]), inc=[(vs, 1)])
        P.do("vector", lambda e, g=g, aj=aj: e.tensor_scalar(out=aj[:, 16:24], in0=e2048bc[:, :],
                                                             scalar1=expv[:, 8 + g:9 + g], scalar2=-87.0,
                                                             op0=ALU.mult, op1=ALU.max), inc=[(vs, 1)])
        P.do("scalar", lambda e, aj=aj: e.activation(out=aj[:, 16:24], in_=aj[:, 16:24], func=AF.Exp),
             wait=[(vs, vs.n)], inc=[(ss, 1)])
        P.do("vector", lambda e, aj=aj: e.tensor_tensor(out=aj[:, 16:24], in0=aj[:, 16:24], in1=mjbc[:, :],
                                                        op=ALU.mult), wait=[(ss, ss.n)], inc=[(vs, 1)])
        P.do("vector", lambda e, aj=aj: e.tensor_tensor(out=aj[:, 0:8], in0=aj[:, 0:8], in1=aj[:, 16:24],
                                                        op=ALU.mult), inc=[(vs, 1)])
        P.do("vector", lambda e, aj=aj: e.tensor_tensor(out=aj[:, 8:16], in0=aj[:, 8:16], in1=aj[:, 16:24],
                                                        op=ALU.mult), inc=[(vs, 1)])
        P.do("vector", lambda e, aj=aj: e.reduce_sum(out=aj[:, 16:17], in_=aj[:, 0:8], axis=AX.X),
             inc=[(vs, 1)])
        P.do("vector", lambda e, aj=aj: e.reduce_sum(out=aj[:, 17:18], in_=aj[:, 8:16], axis=AX.X),
             inc=[(vs, 1)])
        P.do("vector", lambda e, p0a=p0a, et=et, aj=aj: e.scalar_tensor_tensor(
            out=p0a[:, 15:15 + CH], in0=et[:, :], scalar=aj[:, 16:17], in1=p0a[:, 15:15 + CH],
            op0=ALU.mult, op1=ALU.add), wait=[(ss, S["ss_et"])], inc=[(vs, 1)])
        P.do("vector", lambda e, p0b=p0b, et=et, aj=aj: e.scalar_tensor_tensor(
            out=p0b[:, 15:15 + CH], in0=et[:, :], scalar=aj[:, 17:18], in1=p0b[:, 15:15 + CH],
            op0=ALU.mult, op1=ALU.add), inc=[(vs, 1)])
        if wkv_dbg is not None:
            P.do("sync", lambda e, g=g, p0a=p0a: e.dma_start(
                out=wkv_dbg[3, g * 128:(g + 1) * 128, :], in_=p0a[:, 15:15 + CH]),
                wait=[(vs, vs.n)], inc=[(ld, 16)])
        P.do("vector", lambda e, g=g, vf=vf, p0a=p0a: e.scalar_tensor_tensor(
            out=vf[:, :], in0=vf[:, :], scalar=expv[:, 24 + g:25 + g], in1=p0a[:, 15:15 + CH],
            op0=ALU.mult, op1=ALU.add), inc=[(vs, 1)])
        P.do("vector", lambda e, g=g, kf=kf, p0b=p0b: e.scalar_tensor_tensor(
            out=kf[:, :], in0=kf[:, :], scalar=expv[:, 24 + g:25 + g], in1=p0b[:, 15:15 + CH],
            op0=ALU.mult, op1=ALU.add), inc=[(vs, 1)])
        if wkv_dbg is not None:
            P.do("sync", lambda e, g=g, vf=vf: e.dma_start(
                out=wkv_dbg[4, g * 128:(g + 1) * 128, :], in_=vf[:, :]), wait=[(vs, vs.n)], inc=[(ld, 16)])
            P.do("sync", lambda e, g=g, kf=kf: e.dma_start(
                out=wkv_dbg[5, g * 128:(g + 1) * 128, :], in_=kf[:, :]), inc=[(ld, 16)])
        P.do("vector", lambda e, kf=kf: e.reciprocal(out=kf[:, :], in_=kf[:, :]),
             wait=[(ld, ld.n)] if wkv_dbg is not None else [], inc=[(vs, 1)])
        P.do("gpsimd", lambda e, vf=vf, kf=kf: e.tensor_tensor(out=vf[:, :], in0=vf[:, :], in1=kf[:, :],
                                                               op=ALU.mult), wait=[(vs, vs.n)], inc=[(gs, 1)])
        if wkv_dbg is not None:
            P.do("sync", lambda e, g=g, vf=vf: e.dma_start(
                out=wkv_dbg[6, g * 128:(g + 1) * 128, :], in_=vf[:, :]), wait=[(gs, gs.n)], inc=[(ld, 16)])
        P.do("gpsimd", lambda e, g=g, vf=vf, srf=srf: e.tensor_tensor(
            out=wkv_rs[g][:, :], in0=vf[:, :], in1=srf[:, :], op=ALU.mult),
            wait=(_merge_waits(wkvrs_gu, [(ld, ld.n)] if wkv_dbg is not None else []) if g == 0 or wkv_dbg is not None else []),
            inc=[(gs, 1)])
        S["vs_done"] = vs.n
        S["gs_done"] = gs.n

    for S in sets:
        S["ts_rb"] = ts.n
        S["ss_et"] = 0
        S["gs_done"] = 0
    wkv_pre(0)
    for g in range(DC):
        if g + 1 < DC:
            wkv_pre(g + 1)
        wkv_post(g)
    vs_wkv = vs.n
    gs_wkv2 = gs.n
    if dbg == "all":
        for g in range(DC):
            P.do("vector", lambda e, g=g: e.tensor_copy(sets[0]["kf"][:, :], wkv_rs[g][:, :]),
                 wait=[(vs, vs.n), (gs, gs_wkv2)], inc=[(vs, 1)])
            P.do("sync", lambda e, g=g: e.dma_start(out=dbg2[4, g * 128:(g + 1) * 128, 0:CH],
                                                    in_=sets[0]["kf"][:, :]),
                 wait=[(vs, vs.n)], inc=[(ld, 16)])
            P.do("vector", lambda e: e.nop(), wait=[(ld, ld.n)])
    for S in sets:
        for kkey in ("kf", "vf", "p0a", "p0b", "et", "srf", "aj"):
            A.free(S[kkey], [(vs, vs_wkv), (gs, gs_wkv2)])

    wLb6, wp6, ss_wo = wh6
    f1t = []
    x1t = []
    f1_gu = []
    for g in range(DC):
        h, gu = alloc(f"f1t{g}", [128, 512], F32)
        f1t.append(h)
        f1_gu += gu
        h, gu = alloc(f"x1t{g}", [128, 512], F32)
        x1t.append(h)
        f1_gu += gu
    f1_store = [0]
    wo_bank = [0, 0]
    for ni, (c0, w) in enumerate(NTS_CH):
        for g in range(DC):
            P.do("sync", lambda e, g=g, c0=c0, w=w: e.dma_start(
                out=f1t[g][:, 0:w], in_=ffn1_sp[g * 128:(g + 1) * 128, c0:c0 + w]),
                wait=(_merge_waits(f1_gu, [(gd, gd_ffn1), (gd, gd_x1sp), (gd, f1_store[0])])
                      if g == 0 else []), inc=[(ld, 16)])
            P.do("sync", lambda e, g=g, c0=c0, w=w: e.dma_start(
                out=x1t[g][:, 0:w], in_=x1_sp[g * 128:(g + 1) * 128, PAD + c0:PAD + c0 + w]),
                inc=[(ld, 16)])
        ld_fin = ld.n
        for m in range(DC):
            pb = psD if m % 2 == 0 else psE
            P.do("tensor", lambda e, kk=0, m=m, c0=c0, w=w, pb=pb: e.matmul(
                pb[:, 0:w], wLb6[0][:, m * 128:(m + 1) * 128], wkv_rs[0][:, c0:c0 + w],
                start=True, stop=False),
                wait=_merge_waits([(ss, ss_wo), (gs, gs_wkv2)],
                                  [(vs, wo_bank[m % 2])] if not (ni == 0 and m < 2) else [(vs, vs.n)]))
            for kk in range(1, DC):
                P.do("tensor", lambda e, kk=kk, m=m, c0=c0, w=w, pb=pb: e.matmul(
                    pb[:, 0:w], wLb6[kk][:, m * 128:(m + 1) * 128], wkv_rs[kk][:, c0:c0 + w],
                    start=False, stop=(kk == DC - 1)),
                    inc=[(ts, 1)] if kk == DC - 1 else [])
            P.do("vector", lambda e, m=m, w=w, pb=pb: e.tensor_tensor(
                out=f1t[m][:, 0:w], in0=pb[:, 0:w], in1=f1t[m][:, 0:w], op=ALU.add),
                wait=[(ts, ts.n), (ld, ld_fin)], inc=[(vs, 1)])
            wo_bank[m % 2] = vs.n
            P.do("vector", lambda e, m=m, w=w: e.tensor_tensor(
                out=f1t[m][:, 0:w], in0=f1t[m][:, 0:w], in1=x1t[m][:, 0:w], op=ALU.add),
                inc=[(vs, 1)])
            P.do("gpsimd", lambda e, m=m, c0=c0, w=w: e.dma_start(
                out=out_ext[m * 128:(m + 1) * 128, c0:c0 + w], in_=f1t[m][:, 0:w]),
                wait=[(vs, vs.n)], inc=[(gd, 16)])
        f1_store[0] = gd.n

    P.do("gpsimd", lambda e: e.nop(), wait=[(gd, gd.n)])
    P.do("sync", lambda e: e.nop(), wait=[(ld, ld.n)])


_BUILD_CACHE = {}


def _get_program(dbg=None):
    key = dbg
    if key not in _BUILD_CACHE:
        _BUILD_CACHE[key] = build_program(dbg)
    return _BUILD_CACHE[key]


def _stage_inputs(inputs, dbg=None):
    f32 = np.float32
    x = np.asarray(inputs["L0_attn_x"], f32)
    cat = np.ascontiguousarray

    wqkvT = cat(np.asarray(inputs["L0_attn_w_qkv"], f32).T)
    wfcT = cat(np.asarray(inputs["L0_attn_w_fc"], f32).T)
    w1T = cat(np.asarray(inputs["L0_ffn_w1"], f32).T)
    w2T = cat(np.asarray(inputs["L0_ffn_w2"], f32).T)
    b1v = cat(np.asarray(inputs["L0_ffn_b1"], f32))
    wl1T = cat(np.stack([
        np.asarray(inputs["L1_attn_Wr"], f32).T,
        np.asarray(inputs["L1_attn_Wk"], f32).T,
        np.asarray(inputs["L1_attn_Wv"], f32).T,
        np.asarray(inputs["L1_ffn_W_r"], f32).T,
        np.asarray(inputs["L1_ffn_W_k"], f32).T,
        np.asarray(inputs["L1_ffn_W_v"], f32).T,
        np.asarray(inputs["L1_attn_Wo"], f32).T,
    ]))

    in_maps = []
    for cix in range(NCORE):
        b = cix // 4
        t0 = (cix % 4) * CH
        xs = np.zeros((WQ, D), f32)
        lo = t0 - PAD
        if lo < 0:
            xs[PAD:, :] = x[b, t0:t0 + CH]
        else:
            xs[:, :] = x[b, lo:t0 + CH]
        vecs = np.zeros((16, D), f32)
        vecs[0] = np.asarray(inputs["L0_attn_b_fc"], f32)
        vecs[1] = np.asarray(inputs["L0_attn_gamma"], f32)
        vecs[2] = np.asarray(inputs["L0_attn_beta"], f32)
        vecs[3] = np.asarray(inputs["L0_ffn_b2"], f32)
        vecs[4] = np.asarray(inputs["L1_attn_time_decay"], f32)
        vecs[5] = np.asarray(inputs["L1_attn_time_first"], f32)
        vecs[6] = np.asarray(inputs["L1_attn_time_mix_k"], f32)
        vecs[7] = np.asarray(inputs["L1_attn_time_mix_v"], f32)
        vecs[8] = np.asarray(inputs["L1_attn_time_mix_r"], f32)
        vecs[9] = np.asarray(inputs["L1_ffn_mu_r"], f32)
        vecs[10] = np.asarray(inputs["L1_ffn_mu_k"], f32)
        ctl = np.zeros(D, f32)
        ctl[0] = 0.0 if (cix % 4 == 0) else 1.0
        for j in range(8):
            same = (j // 4) == b
            if same and (j % 4) < (cix % 4):
                ctl[1 + j] = 1.0
                ctl[9 + j] = float(CH * ((cix % 4) - 1 - (j % 4)))
            if same:
                ctl[17 + j] = 1.0
        vecs[11] = ctl
        wmx = np.concatenate([w1T[cix * 128:(cix + 1) * 128, :],
                              w2T[cix * 512:(cix + 1) * 512, :].reshape(128, 4096)], axis=1)
        in_maps.append({
            "x_ext": cat(xs.T), "wqkvT": wqkvT, "wfcT": wfcT, "w1T": w1T, "w2T": w2T,
            "vecs": cat(vecs), "b1v": b1v, "wl1T": wl1T, "wmx": cat(wmx),
        })
    return in_maps


def kernel(**inputs):
    dbg = inputs.pop("_dbg", None)
    nc = _get_program(dbg)
    in_maps = _stage_inputs(inputs, dbg)
    res = run_bass_kernel_spmd(nc, in_maps, core_ids=list(range(NCORE)))
    out = np.zeros((B, T, D), np.float32)
    for cix in range(NCORE):
        b = cix // 4
        t0 = (cix % 4) * CH
        out[b, t0:t0 + CH] = res.results[cix]["out_ext"].T
    if dbg:
        kernel._dbg_results = [dict(res.results[cix]) for cix in range(NCORE)]
    return out


# version 13 (final)
# speedup vs baseline: 1.0990x; 1.0990x over previous
import numpy as np
import concourse.bass as bass
import concourse.mybir as mybir
from concourse.bass_utils import run_bass_kernel_spmd

F32 = mybir.dt.float32
BF16 = mybir.dt.bfloat16
AF = mybir.ActivationFunctionType
ALU = mybir.AluOpType
AX = mybir.AxisListType

MAGIC = 12582912.0
SM_SCALE = 0.03125
LN_EPS = 1e-5
QP = 127.0
MIN_RANGE = 1e-6

B, T, D, HH = 2, 8192, 1024, 4096
NCORE = 8
CH = 2048
PAD = 128
WQ = PAD + CH
DC = D // 128
HC = HH // 128
QT = WQ // 128
KB = 1024
NKB = T // KB
NTS_W = [(0, 512), (512, 512), (1024, 512), (1536, 512), (2048, 128)]
NTS_OWN = [(128, 512), (640, 512), (1152, 512), (1664, 512)]
NTS_CH = [(0, 512), (512, 512), (1024, 512), (1536, 512)]

ARENA_LO = 28672
ARENA_HI = 229344


class Sem:
    def __init__(self, nc, name, owner):
        self.h = nc.semaphore(name).__enter__()
        self.owner = owner
        self.dbg_name = name
        self.n = 0


class QSem:

    R = 4

    def __init__(self, nc, name, owner):
        self.sems = [Sem(nc, f"{name}{i}", owner) for i in range(self.R)]
        self.owner = owner
        self.dbg_name = name
        self.k = 0

    @property
    def n(self):
        return 16 * self.k

    def expand_wait(self, v):
        k = (v + 15) // 16
        out = []
        for p in range(self.R):
            c = ((k - 1 - p) // self.R + 1) if k > p else 0
            if c > 0:
                out.append((self.sems[p], 16 * c))
        return out

    def next_issue(self):
        p = self.k % self.R
        c = self.k // self.R
        gate = [(self.sems[p], 16 * c)] if c > 0 else []
        self.k += 1
        return gate, self.sems[p]


class Prog:
    ENGINES = ("sync", "scalar", "vector", "tensor", "gpsimd")

    def __init__(self, nc):
        self.nc = nc
        self.ops = {e: [] for e in self.ENGINES}
        self.chain = {}

    def do(self, eng, fn, wait=(), inc=()):
        import traceback
        wait_x = []
        for s, v in wait:
            if v <= 0:
                continue
            if isinstance(s, QSem):
                wait_x.extend(s.expand_wait(v))
            else:
                wait_x.append((s, v))
        inc_x = []
        for s, amt in inc:
            assert s.owner == eng, (s.owner, eng)
            if isinstance(s, QSem):
                assert amt == 16
                gate, tgt = s.next_issue()
                wait_x = _merge_waits(wait_x, gate)
                inc_x.append((tgt, 16))
            else:
                inc_x.append((s, amt))
        csem = self.chain.get(eng)
        if csem is not None:
            wait_x = _merge_waits(wait_x, [(csem, csem.n)])
            if not inc_x:
                inc_x.append((csem, 1))
        wait = _merge_waits(wait_x)
        inc = inc_x

        stk = traceback.extract_stack()
        fr = next((f for f in reversed(stk[:-1]) if f.name not in ("do",)), stk[-2])
        tag = f"{fr.name}:{fr.lineno}"
        is_dma = any(sm.owner in ("sync", "gpsimd") and amt % 16 == 0 and amt >= 16 for sm, amt in inc)

        def run(e):
            for s, v in wait:
                e.wait_ge(s.h, v)
            n0 = self.nc.n_instructions()
            instr = fn(e)
            if is_dma and self.nc.n_instructions() - n0 != 1:
                raise RuntimeError(f"DMA at {tag} lowered to {self.nc.n_instructions() - n0} "
                                   f"instructions; semaphore accounting assumes 1")
            for s, amt in inc:
                instr = instr.then_inc(s.h, amt)

        run.meta = (eng, wait, inc, tag)
        self.ops[eng].append(run)
        for s, amt in inc:
            s.n += amt

    def verify_no_deadlock(self):
        heads = {e: 0 for e in self.ENGINES}
        cur = {}
        total = sum(len(v) for v in self.ops.values())
        done = 0
        progress = True
        while done < total and progress:
            progress = False
            for e in self.ENGINES:
                while heads[e] < len(self.ops[e]):
                    eng, wait, inc, tag = self.ops[e][heads[e]].meta
                    if any(cur.get(id(sm), 0) < v for sm, v in wait):
                        break
                    for sm, amt in inc:
                        cur[id(sm)] = cur.get(id(sm), 0) + amt
                    heads[e] += 1
                    done += 1
                    progress = True
        if done < total:
            lines = []
            for e in self.ENGINES:
                if heads[e] < len(self.ops[e]):
                    eng, wait, inc, tag = self.ops[e][heads[e]].meta
                    blocked = [(getattr(sm, 'dbg_name', '?'), v, cur.get(id(sm), 0)) for sm, v in wait
                               if cur.get(id(sm), 0) < v]
                    lines.append(f"  {e}[{heads[e]}/{len(self.ops[e])}] at {tag} blocked on {blocked}")
            raise RuntimeError("DEADLOCK in planned program:\n" + "\n".join(lines))

    def emit(self):
        with self.nc.Block() as block:
            for eng in self.ENGINES:
                ops = self.ops[eng]

                def body(e, ops=ops):
                    for op in ops:
                        op(e)

                getattr(block, eng)(body)


class Arena:

    def __init__(self, nc):
        self.nc = nc
        self.free_list = [(ARENA_LO, ARENA_HI, [])]
        self.live = {}
        self.uid = 0
        self.peak = 0

    def alloc(self, name, shape, dtype, align=64):
        nbytes = int(np.prod(shape[1:])) * (2 if dtype == BF16 else 4)
        order = sorted(range(len(self.free_list)),
                       key=lambda i: self.free_list[i][1] - self.free_list[i][0])
        for i in order:
            lo, hi, gu = self.free_list[i]
            start = (lo + align - 1) // align * align
            if start + nbytes <= hi:
                segs = []
                if start > lo:
                    segs.append((lo, start, gu))
                if start + nbytes < hi:
                    segs.append((start + nbytes, hi, gu))
                self.free_list[i:i + 1] = segs
                self.uid += 1
                h = self.nc.alloc_sbuf_tensor_at(f"{name}_{self.uid}", list(shape), dtype, offset=start)
                self.live[id(h)] = (start, start + nbytes)
                self.peak = max(self.peak, start + nbytes)
                return h, list(gu)
        raise MemoryError(f"SBUF arena exhausted for {name} {shape}; free={[(a, b) for a, b, _ in self.free_list]}")

    def free(self, h, guard):
        lo, hi = self.live.pop(id(h))
        self.free_list.append((lo, hi, list(guard)))
        self.free_list.sort(key=lambda s: s[0])
        merged = []
        for seg in self.free_list:
            if merged and merged[-1][1] >= seg[0]:
                prev = merged[-1]
                merged[-1] = (prev[0], max(prev[1], seg[1]), prev[2] + seg[2])
            else:
                merged.append(seg)
        self.free_list = merged


def _merge_waits(*wait_lists):
    best = {}
    for wl in wait_lists:
        for s, v in wl:
            best[id(s)] = (s, max(v, best.get(id(s), (s, 0))[1]))
    return list(best.values())


def build_program(dbg=None):
    nc = bass.Bass("TRN2", target_bir_lowering=False)

    x_ext = nc.dram_tensor("x_ext", [D, WQ], F32, kind="ExternalInput").ap()
    wqkvT = nc.dram_tensor("wqkvT", [D, 3 * D], F32, kind="ExternalInput").ap()
    wfcT = nc.dram_tensor("wfcT", [D, D], F32, kind="ExternalInput").ap()
    w1T = nc.dram_tensor("w1T", [D, HH], F32, kind="ExternalInput").ap()
    w2T = nc.dram_tensor("w2T", [HH, D], F32, kind="ExternalInput").ap()
    vecs = nc.dram_tensor("vecs", [16, D], F32, kind="ExternalInput").ap()
    b1v = nc.dram_tensor("b1v", [HH], F32, kind="ExternalInput").ap()
    wl1T = nc.dram_tensor("wl1T", [7, D, D], F32, kind="ExternalInput").ap()
    wmx = nc.dram_tensor("wmx", [128, 8192], F32, kind="ExternalInput").ap()
    out_ext = nc.dram_tensor("out_ext", [D, CH], F32, kind="ExternalOutput").ap()
    dbg_ext = nc.dram_tensor("dbgout", [D, WQ], F32, kind="ExternalOutput").ap() if dbg else None
    dbg2 = nc.dram_tensor("dbg2", [6, D, WQ], F32, kind="ExternalOutput").ap() if dbg == "all" else None
    dbg3 = nc.dram_tensor("dbg3", [128, 128], F32, kind="ExternalOutput").ap() if dbg == "all" else None

    xn_sp = nc.dram_tensor("xn_sp", [D, WQ], F32).ap()
    h_sp = nc.dram_tensor("h_sp", [HH, WQ], F32).ap()
    x1_sp = nc.dram_tensor("x1_sp", [D, WQ], F32).ap()
    k_sp = nc.dram_tensor("k_sp", [D, CH], F32).ap()
    v_sp = nc.dram_tensor("v_sp", [D, CH], F32).ap()
    sr_sp = nc.dram_tensor("sr_sp", [D, CH], F32).ap()
    r2_sp = nc.dram_tensor("r2_sp", [D, CH], F32).ap()
    cc_m1_i = nc.dram_tensor("cc_m1_i", [1, 8], F32).ap()
    cc_m1_o = nc.dram_tensor("cc_m1_o", [8, 8], F32).ap()
    cc_m2_i = nc.dram_tensor("cc_m2_i", [1, 8], F32).ap()
    cc_m2_o = nc.dram_tensor("cc_m2_o", [8, 8], F32).ap()
    cc_vs_i = nc.dram_tensor("cc_vs_i", [1, D], F32).ap()
    cc_vs_o = nc.dram_tensor("cc_vs_o", [8, D], F32).ap()
    cc_wk_i = [nc.dram_tensor(f"cc_wk_i{g}", [1, 256], F32).ap() for g in range(DC)]
    cc_wk_o = [nc.dram_tensor(f"cc_wk_o{g}", [8, 256], F32).ap() for g in range(DC)]
    wkv_dbg = nc.dram_tensor("wkv_dbg", [7, D, CH], F32).ap() if dbg == "wkv" else None

    P = Prog(nc)
    ld = QSem(nc, "ld", "sync")
    gd = QSem(nc, "gd", "gpsimd")
    vs = Sem(nc, "vs", "vector")
    ss = Sem(nc, "ss", "scalar")
    ts = Sem(nc, "ts", "tensor")
    gs = Sem(nc, "gs", "gpsimd")
    cs = Sem(nc, "cs", "gpsimd")
    P.chain["vector"] = vs
    P.chain["gpsimd"] = gs
    P.chain["scalar"] = ss

    A = Arena(nc)

    def alloc(name, shape, dtype):
        h, gu = A.alloc(name, shape, dtype)
        return h, gu

    psA = nc.psum_tensor("psA", [128, 1024], F32).__enter__()
    psB = nc.psum_tensor("psB", [128, 1024], F32).__enter__()
    psC = nc.psum_tensor("psC", [128, 1024], F32).__enter__()
    psD = nc.psum_tensor("psD", [128, 512], F32).__enter__()
    psE = nc.psum_tensor("psE", [128, 512], F32).__enter__()

    idn, _ = alloc("idn", [128, 128], F32)
    idnb, _ = alloc("idnb", [128, 128], BF16)
    ones128b, _ = alloc("ones128b", [128, 128], BF16)
    onescol, _ = alloc("onescol", [1, 128], F32)
    epsc, _ = alloc("epsc", [128, 1], F32)
    magicc, _ = alloc("magicc", [128, 1], F32)
    smc, _ = alloc("smc", [128, 1], F32)
    neghalfc, _ = alloc("neghalfc", [128, 1], F32)
    invdc, _ = alloc("invdc", [128, 1], F32)
    negmagicc, _ = alloc("negmagicc", [128, 1], F32)
    nc.const_aps.register(F32, -MAGIC, negmagicc[:, :])
    nc.const_aps.register(F32, MAGIC, magicc[:, :])
    nc.const_aps.register(F32, SM_SCALE, smc[:, :])
    nc.const_aps.register(F32, -0.5, neghalfc[:, :])

    vcm, _ = alloc("vcm", [128, 16 * 8], F32)
    VEC = {n: i for i, n in enumerate(
        ["bfc", "gamma", "beta", "b2", "td", "tf", "tmk", "tmv", "tmr", "mur", "muk"])}

    def vc(name):
        i = VEC[name]
        return vcm[:, i * 8:(i + 1) * 8]

    b1cm, _ = alloc("b1cm", [128, HC], F32)
    b1ss, _ = alloc("b1ss", [128, HC], F32)
    scl, _ = alloc("scl", [128, 32], F32)
    e2048bc, _ = alloc("e2048bc", [128, 8], F32)
    mjbc, _ = alloc("mjbc", [128, 8], F32)
    mbbc, _ = alloc("mbbc", [128, 8], F32)
    expv, _ = alloc("expv", [128, 32], F32)
    wmaxp, _ = alloc("wmaxp", [128, 2], F32)
    rowt, _ = alloc("rowt", [32, 512], F32)
    rgA = []
    rgB = []
    rgA_busy = [[], [], [], []]
    rgB_busy = [[], [], [], []]
    for i_ in range(4):
        h, _ = alloc(f"rgA{i_}", [128, 512], F32)
        rgA.append(h)
        h, _ = alloc(f"rgB{i_}", [128, 512], F32)
        rgB.append(h)
    rg_ix = [0]

    P.do("gpsimd", lambda e: e.memset(idn[:, :], 0.0), inc=[(gs, 1)])
    P.do("gpsimd", lambda e: e.affine_select(out=idn[:, :], in_=idn[:, :], compare_op=ALU.not_equal,
                                             fill=1.0, base=0, pattern=[[-1, 128]], channel_multiplier=1),
         wait=[(gs, gs.n)], inc=[(gs, 1)])
    P.do("gpsimd", lambda e: e.memset(ones128b[:, :], 1.0))
    P.do("gpsimd", lambda e: e.memset(onescol[:, :], 1.0))
    P.do("gpsimd", lambda e: e.memset(epsc[:, :], LN_EPS))
    P.do("gpsimd", lambda e: e.memset(magicc[:, :], MAGIC))
    P.do("gpsimd", lambda e: e.memset(smc[:, :], SM_SCALE))
    P.do("gpsimd", lambda e: e.memset(neghalfc[:, :], -0.5))
    P.do("gpsimd", lambda e: e.memset(invdc[:, :], 1.0 / D))
    P.do("gpsimd", lambda e: e.memset(negmagicc[:, :], -MAGIC))
    P.do("gpsimd", lambda e: e.memset(wmaxp[:, :], 0.0))
    P.do("gpsimd", lambda e: e.memset(scl[:, :], 0.0))
    P.do("gpsimd", lambda e: e.memset(rowt[:, :], 0.0), inc=[(gs, 1)])
    gs_pro = gs.n
    P.do("vector", lambda e: e.tensor_copy(idnb[:, :], idn[:, :]), wait=[(gs, gs_pro)], inc=[(vs, 1)])

    for i, name in enumerate(VEC):
        P.do("sync", lambda e, i=i: e.dma_start(
            out=rowt[0:8, 0:128],
            in_=vecs.rearrange("v (j p) -> (v j) p", p=128)[i * 8:(i + 1) * 8, :]),
            wait=[(ts, ts.n), (gs, gs_pro)], inc=[(ld, 16)])
        P.do("tensor", lambda e: e.transpose(psD[:, 0:8], rowt[0:8, 0:128], idn[0:8, 0:8]),
             wait=[(ld, ld.n), (vs, vs.n)], inc=[(ts, 1)])
        P.do("vector", lambda e, i=i: e.tensor_copy(vcm[:, i * 8:(i + 1) * 8], psD[:, 0:8]),
             wait=[(ts, ts.n)], inc=[(vs, 1)])
    P.do("sync", lambda e: e.dma_start(out=rowt[0:32, 0:128], in_=b1v.rearrange("(j p) -> j p", p=128)),
         wait=[(ts, ts.n)], inc=[(ld, 16)])
    P.do("tensor", lambda e: e.transpose(psD[:, 0:32], rowt[0:32, 0:128], idn[0:32, 0:32]),
         wait=[(ld, ld.n), (vs, vs.n)], inc=[(ts, 1)])
    P.do("vector", lambda e: e.tensor_copy(b1cm[:, :], psD[:, 0:32]), wait=[(ts, ts.n)], inc=[(vs, 1)])
    P.do("sync", lambda e: e.dma_start(out=rowt[0:1, 128:160], in_=vecs[11:12, 0:32]), inc=[(ld, 16)])
    ld_ctl = ld.n

    def bcast_row(dst, src_row, n, wait):
        P.do("tensor", lambda e: e.matmul(psE[:, 0:n], onescol[0:1, :], src_row, start=True, stop=True),
             wait=_merge_waits(wait, [(gs, gs_pro), (ss, ss.n), (vs, vs.n)]), inc=[(ts, 1)])
        P.do("vector", lambda e: e.tensor_copy(dst, psE[:, 0:n]), wait=[(ts, ts.n)], inc=[(vs, 1)])

    bcast_row(scl[:, 4:5], rowt[0:1, 128:129], 1, [(ld, ld_ctl), (vs, vs.n)])
    bcast_row(mjbc[:, :], rowt[0:1, 129:137], 8, [(vs, vs.n)])
    bcast_row(e2048bc[:, :], rowt[0:1, 137:145], 8, [(vs, vs.n)])
    bcast_row(mbbc[:, :], rowt[0:1, 145:153], 8, [(vs, vs.n)])
    vs_ctl = vs.n

    P.do("scalar", lambda e: e.activation(out=expv[:, 0:8], in_=vc("td"), func=AF.Exp),
         wait=[(vs, vs_ctl)], inc=[(ss, 1)])
    P.do("vector", lambda e: e.tensor_scalar(out=expv[:, 8:16], in0=expv[:, 0:8], scalar1=-1.0,
                                             scalar2=None, op0=ALU.mult), wait=[(ss, ss.n)], inc=[(vs, 1)])
    P.do("vector", lambda e: e.tensor_scalar(out=expv[:, 8:16], in0=expv[:, 8:16], scalar1=-87.0,
                                             scalar2=None, op0=ALU.max), wait=[(vs, vs.n)], inc=[(vs, 1)])
    P.do("scalar", lambda e: e.activation(out=expv[:, 16:24], in_=expv[:, 8:16], func=AF.Exp),
         wait=[(vs, vs.n)], inc=[(ss, 1)])
    P.do("scalar", lambda e: e.activation(out=expv[:, 24:32], in_=vc("tf"), func=AF.Exp), inc=[(ss, 1)])

    wm_first = [True, True]
    for j in range(16):
        col = 0 if j < 8 else 1
        i_ = rg_ix[0] % 4
        rg_ix[0] += 1
        bA = rgA[i_]
        P.do("sync", lambda e, j=j, bA=bA: e.dma_start(out=bA[:, :], in_=wmx[:, j * 512:(j + 1) * 512]),
             wait=_merge_waits(rgA_busy[i_], [(gs, gs_pro)] if j == 0 else []), inc=[(ld, 16)])
        ldn = ld.n
        if wm_first[col]:
            P.do("vector", lambda e, col=col, bA=bA: e.reduce_max(out=wmaxp[:, col:col + 1], in_=bA[:, :],
                                                                  axis=AX.X, apply_absolute_value=True),
                 wait=[(ld, ldn)], inc=[(vs, 1)])
            wm_first[col] = False
        else:
            P.do("vector", lambda e, bA=bA: e.reduce_max(out=scl[:, 11:12], in_=bA[:, :], axis=AX.X,
                                                         apply_absolute_value=True),
                 wait=[(ld, ldn)], inc=[(vs, 1)])
            P.do("vector", lambda e, col=col: e.tensor_tensor(out=wmaxp[:, col:col + 1],
                                                              in0=wmaxp[:, col:col + 1],
                                                              in1=scl[:, 11:12], op=ALU.max), inc=[(vs, 1)])
        rgA_busy[i_] = [(vs, vs.n)]
    P.do("tensor", lambda e: e.transpose(psE[0:2, 0:128], wmaxp[:, 0:2], idn[:, :]),
         wait=[(vs, vs.n), (ss, ss.n)], inc=[(ts, 1)])
    P.do("vector", lambda e: e.reduce_max(out=rowt[0:2, 152:153], in_=psE[0:2, 0:128], axis=AX.X),
         wait=[(ts, ts.n)], inc=[(vs, 1)])
    P.do("tensor", lambda e: e.transpose(psE[0:1, 0:2], rowt[0:2, 152:153], idn[0:2, 0:2]),
         wait=[(vs, vs.n)], inc=[(ts, 1)])
    P.do("vector", lambda e: e.tensor_copy(scl[0:1, 13:15], psE[0:1, 0:2]), wait=[(ts, ts.n)], inc=[(vs, 1)])
    vs_wmx = vs.n

    x_fm = []
    xfm_gu = []
    for g in range(DC):
        h, gu = alloc(f"x_fm{g}", [128, WQ], F32)
        x_fm.append(h)
        xfm_gu += gu
    for g in range(DC):
        P.do("sync", lambda e, g=g: e.dma_start(out=x_fm[g][:, :], in_=x_ext[g * 128:(g + 1) * 128, :]),
             wait=(_merge_waits(xfm_gu) if g == 0 else []), inc=[(ld, 16)])
    ld_xfm = ld.n

    def dump2(slot, chunks, wait, width=WQ):
        if dbg != "all":
            return
        for g in range(DC):
            P.do("sync", lambda e, g=g: e.dma_start(
                out=dbg2[slot, g * 128:(g + 1) * 128, 0:width], in_=chunks[g][:, 0:width]),
                wait=wait if g == 0 else [], inc=[(ld, 16)])

    ln_sc, _ = alloc("ln_sc", [128, 512], F32)
    castp = []
    for i_ in range(2):
        h0_, _ = alloc(f"cast0_{i_}", [128, 512], BF16)
        h1_, _ = alloc(f"cast1_{i_}", [128, 512], BF16)
        castp.append((h0_, h1_))
    cast_busy = [0, 0]

    def emit_ln(src, mu_t, rstd_t, first_wait, width_tiles=NTS_W):
        for ti, (c0, w) in enumerate(width_tiles):
            for g in range(DC):
                p_ = g % 2
                c0b, c1b = castp[p_]
                wcast = _merge_waits(list(first_wait) if (ti == 0 and g == 0) else [],
                                     [(ts, cast_busy[p_])])
                P.do("vector", lambda e, g=g, c0=c0, w=w, c0b=c0b: e.tensor_copy(
                    c0b[:, 0:w], src[g][:, c0:c0 + w]), wait=wcast, inc=[(vs, 1)])
                P.do("vector", lambda e, g=g, c0=c0, w=w, c1b=c1b: e.tensor_tensor(
                    out=c1b[:, 0:w], in0=src[g][:, c0:c0 + w], in1=src[g][:, c0:c0 + w], op=ALU.mult),
                    inc=[(vs, 1)])
                vsn = vs.n
                P.do("tensor", lambda e, g=g, w=w, c0b=c0b: e.matmul(psD[:, 0:w], ones128b[:, :], c0b[:, 0:w],
                                                                     start=(g == 0), stop=(g == DC - 1)),
                     wait=[(vs, vsn)])
                P.do("tensor", lambda e, g=g, w=w, c1b=c1b: e.matmul(psE[:, 0:w], ones128b[:, :], c1b[:, 0:w],
                                                                     start=(g == 0), stop=(g == DC - 1)),
                     inc=[(ts, 1)])
                cast_busy[p_] = ts.n
            tsn = ts.n
            P.do("vector", lambda e, c0=c0, w=w: e.tensor_scalar(
                out=mu_t[:, c0:c0 + w], in0=psD[:, 0:w], scalar1=invdc[:, 0:1], scalar2=None, op0=ALU.mult),
                wait=[(ts, tsn)], inc=[(vs, 1)])
            P.do("vector", lambda e, c0=c0, w=w: e.tensor_scalar(
                out=ln_sc[:, 0:w], in0=psE[:, 0:w], scalar1=invdc[:, 0:1], scalar2=None, op0=ALU.mult),
                wait=[(ss, ss.n)], inc=[(vs, 1)])
            P.do("vector", lambda e, c0=c0, w=w: e.tensor_tensor(
                out=rstd_t[:, c0:c0 + w], in0=mu_t[:, c0:c0 + w], in1=mu_t[:, c0:c0 + w], op=ALU.mult),
                inc=[(vs, 1)])
            P.do("vector", lambda e, c0=c0, w=w: e.tensor_tensor(
                out=ln_sc[:, 0:w], in0=ln_sc[:, 0:w], in1=rstd_t[:, c0:c0 + w], op=ALU.subtract),
                inc=[(vs, 1)])
            vsn = vs.n
            P.do("scalar", lambda e, c0=c0, w=w: e.activation(out=ln_sc[:, 0:w], in_=ln_sc[:, 0:w],
                                                              func=AF.Ln, bias=epsc[:, 0:1], scale=1.0),
                 wait=[(vs, vsn)], inc=[(ss, 1)])
            P.do("scalar", lambda e, c0=c0, w=w: e.activation(out=rstd_t[:, c0:c0 + w], in_=ln_sc[:, 0:w],
                                                              func=AF.Exp, scale=-0.5), inc=[(ss, 1)])
            P.do("tensor", lambda e: e.nop(), wait=[(vs, vsn)])

    def emit_ln_norm(src, dst, dst_guard, mu_t, rstd_t, gamma=None, beta=None):
        ss_stats = ss.n
        for g in range(DC):
            P.do("vector", lambda e, g=g: e.tensor_tensor(
                out=dst[g][:, :], in0=src[g][:, :], in1=mu_t[:, :], op=ALU.subtract),
                wait=([(ss, ss_stats)] + dst_guard) if g == 0 else [], inc=[(vs, 1)])
            P.do("vector", lambda e, g=g: e.tensor_tensor(
                out=dst[g][:, :], in0=dst[g][:, :], in1=rstd_t[:, :], op=ALU.mult), inc=[(vs, 1)])
            if gamma is not None:
                P.do("vector", lambda e, g=g: e.tensor_scalar(
                    out=dst[g][:, :], in0=dst[g][:, :], scalar1=gamma[:, g:g + 1], scalar2=beta[:, g:g + 1],
                    op0=ALU.mult, op1=ALU.add), inc=[(vs, 1)])

    mu_t, mu_gu = alloc("mu_t", [128, WQ], F32)
    rstd_t, rs_gu = alloc("rstd_t", [128, WQ], F32)
    emit_ln(x_fm, mu_t, rstd_t, _merge_waits([(ld, ld_xfm)], mu_gu, rs_gu))
    xn = []
    xn_gu = []
    for g in range(DC):
        h, gu = alloc(f"xn{g}", [128, WQ], F32)
        xn.append(h)
        xn_gu += gu
    emit_ln_norm(x_fm, xn, xn_gu, mu_t, rstd_t)
    gs_xn = vs.n
    for g in range(DC):
        A.free(x_fm[g], [(vs, gs_xn)])
    A.free(mu_t, [(vs, gs_xn)])
    A.free(rstd_t, [(vs, gs_xn)])

    dump2(0, xn, [(vs, gs_xn)])
    def absmax_chunks(chunks, cslice, dst_acc, dst_tmp, first_wait):
        for g in range(DC):
            if g == 0:
                P.do("vector", lambda e, g=g: e.reduce_max(out=dst_acc, in_=chunks[g][:, cslice[0]:cslice[1]],
                                                           axis=AX.X, apply_absolute_value=True),
                     wait=first_wait, inc=[(vs, 1)])
            else:
                P.do("vector", lambda e, g=g: e.reduce_max(out=dst_tmp, in_=chunks[g][:, cslice[0]:cslice[1]],
                                                           axis=AX.X, apply_absolute_value=True), inc=[(vs, 1)])
                P.do("vector", lambda e: e.tensor_tensor(out=dst_acc, in0=dst_acc, in1=dst_tmp, op=ALU.max),
                     inc=[(vs, 1)])

    absmax_chunks(xn, (PAD, WQ), scl[:, 10:11], scl[:, 11:12], [(vs, gs_xn)])
    P.do("tensor", lambda e: e.transpose(psE[0:1, 0:128], scl[:, 10:11], idn[:, :]),
         wait=[(vs, vs.n), (ss, ss.n)], inc=[(ts, 1)])
    P.do("vector", lambda e: e.reduce_max(out=scl[0:1, 12:13], in_=psE[0:1, 0:128], axis=AX.X),
         wait=[(ts, ts.n)], inc=[(vs, 1)])
    vs_m1 = vs.n
    P.do("gpsimd", lambda e: e.dma_start(out=cc_m1_i[:, 0:8], in_=scl[0:1, 12:20]),
         wait=[(vs, vs_m1)], inc=[(gd, 16)])
    P.do("gpsimd", lambda e: e.collective_compute(
        "AllGather", ALU.bypass, replica_groups=[list(range(8))],
        ins=[cc_m1_i.opt()], outs=[cc_m1_o.opt()]), wait=[(gd, gd.n)], inc=[(cs, 1)])
    cs_m1 = cs.n

    def load_wT_bf16(dram_rows_cols, dst, first_wait=()):
        for kk in range(DC):
            for cq in range(0, D, 512):
                i_ = rg_ix[0] % 4
                rg_ix[0] += 1
                bB = rgB[i_]
                P.do("sync", lambda e, kk=kk, cq=cq, bB=bB: e.dma_start(
                    out=bB[:, :], in_=dram_rows_cols[kk * 128:(kk + 1) * 128, cq:cq + 512]),
                    wait=_merge_waits(rgB_busy[i_],
                                      list(first_wait) if (kk == 0 and cq == 0) else []), inc=[(ld, 16)])
                P.do("scalar", lambda e, kk=kk, cq=cq, dst=dst, bB=bB: e.activation(
                    out=dst[kk][:, cq:cq + 512], in_=bB[:, :], func=AF.Copy),
                    wait=[(ld, ld.n)], inc=[(ss, 1)])
                rgB_busy[i_] = [(ss, ss.n)]
        return ss.n

    wvb = []
    wv_gu = []
    for k in range(DC):
        h, gu = alloc(f"wvb{k}", [128, D], BF16)
        wvb.append(h)
        wv_gu += gu
    ss_wvb = load_wT_bf16(wqkvT[:, 2 * D:3 * D], wvb, first_wait=wv_gu)

    xsl = []
    xsl_gu = []
    for k in range(DC):
        h, gu = alloc(f"xsl{k}", [128, 512], BF16)
        xsl.append(h)
        xsl_gu += gu
    vsum_cm, vsg_gu0 = alloc("vsum_cm", [128, 16], F32)
    P.do("vector", lambda e: e.memset(vsum_cm[:, :], 0.0), wait=_merge_waits(vsg_gu0), inc=[(vs, 1)])
    vs_first = [True]
    for ni, (c0, w) in enumerate(NTS_OWN):
        for kk in range(DC):
            P.do("vector", lambda e, kk=kk, c0=c0, w=w: e.tensor_copy(xsl[kk][:, 0:w], xn[kk][:, c0:c0 + w]),
                 wait=(_merge_waits([(ss, ss_wvb), (vs, gs_xn)], xsl_gu) if (ni == 0 and kk == 0) else ([(ts, ts.n)] if kk == 0 else [])),
                 inc=[(vs, 1)] if kk == DC - 1 else [])
        vs_sl = vs.n
        for m in range(DC):
            pb = psD if m % 2 == 0 else psE
            for kk in range(DC):
                P.do("tensor", lambda e, kk=kk, m=m, w=w, pb=pb: e.matmul(
                    pb[:, 0:w], wvb[kk][:, m * 128:(m + 1) * 128], xsl[kk][:, 0:w],
                    start=(kk == 0), stop=(kk == DC - 1)),
                    wait=[(vs, vs_sl), (vs, vs.n)] if kk == 0 else [],
                    inc=[(ts, 1)] if kk == DC - 1 else [])
            if vs_first[0]:
                P.do("vector", lambda e, m=m, w=w, pb=pb: e.reduce_sum(
                    out=vsum_cm[:, m:m + 1], in_=pb[:, 0:w], axis=AX.X), wait=[(ts, ts.n)], inc=[(vs, 1)])
            else:
                P.do("vector", lambda e, m=m, w=w, pb=pb: e.reduce_sum(
                    out=vsum_cm[:, 8:9], in_=pb[:, 0:w], axis=AX.X), wait=[(ts, ts.n)], inc=[(vs, 1)])
                P.do("vector", lambda e, m=m: e.tensor_tensor(
                    out=vsum_cm[:, m:m + 1], in0=vsum_cm[:, m:m + 1], in1=vsum_cm[:, 8:9], op=ALU.add),
                    wait=[(vs, vs.n)], inc=[(vs, 1)])
        vs_first[0] = False
    vs_vsum = vs.n
    for g in range(DC):
        P.do("gpsimd", lambda e, g=g: e.dma_start(out=cc_vs_i[0:1, g * 128:(g + 1) * 128],
                                                  in_=vsum_cm[:, g:g + 1]),
             wait=[(vs, vs_vsum)] if g == 0 else [], inc=[(gd, 16)])
    P.do("gpsimd", lambda e: e.collective_compute(
        "AllGather", ALU.bypass, replica_groups=[list(range(8))],
        ins=[cc_vs_i.opt()], outs=[cc_vs_o.opt()]), wait=[(gd, gd.n)], inc=[(cs, 1)])
    cs_vs = cs.n
    ts_kv_proj = ts.n
    for k in range(DC):
        A.free(wvb[k], [(ts, ts_kv_proj)])

    P.do("sync", lambda e: e.dma_start(out=rowt[0:8, 160:168], in_=cc_m1_o[:, :]),
         wait=[(cs, cs_m1)], inc=[(ld, 16)])
    P.do("tensor", lambda e: e.transpose(psE[0:8, 0:8], rowt[0:8, 160:168], idn[0:8, 0:8]),
         wait=[(ld, ld.n), (vs, vs.n), (ss, ss.n)], inc=[(ts, 1)])
    P.do("vector", lambda e: e.reduce_max(out=rowt[0:3, 155:156], in_=psE[0:3, 0:8], axis=AX.X),
         wait=[(ts, ts.n)], inc=[(vs, 1)])
    P.do("tensor", lambda e: e.transpose(psE[0:1, 0:3], rowt[0:3, 155:156], idn[0:3, 0:3]),
         wait=[(vs, vs.n)], inc=[(ts, 1)])
    P.do("vector", lambda e: e.tensor_copy(rowt[0:1, 152:155], psE[0:1, 0:3]), wait=[(ts, ts.n)],
         inc=[(vs, 1)])
    P.do("vector", lambda e: e.tensor_scalar(out=rowt[0:1, 156:157], in0=rowt[0:1, 155:156],
                                             scalar1=MIN_RANGE, scalar2=1.0 / QP, op0=ALU.max, op1=ALU.mult),
         inc=[(vs, 1)])
    P.do("vector", lambda e: e.reciprocal(out=rowt[0:1, 157:158], in_=rowt[0:1, 156:157]), inc=[(vs, 1)])
    P.do("vector", lambda e: e.tensor_scalar(out=rowt[0:1, 158:159], in0=rowt[0:1, 153:154],
                                             scalar1=MIN_RANGE, scalar2=1.0 / QP, op0=ALU.max, op1=ALU.mult),
         inc=[(vs, 1)])
    P.do("vector", lambda e: e.reciprocal(out=rowt[0:1, 159:160], in_=rowt[0:1, 158:159]), inc=[(vs, 1)])
    P.do("vector", lambda e: e.tensor_tensor(out=rowt[0:1, 168:169], in0=rowt[0:1, 158:159],
                                             in1=rowt[0:1, 156:157], op=ALU.mult), inc=[(vs, 1)])
    vs_s1e = vs.n
    bcast_row(scl[:, 13:15], rowt[0:1, 156:158], 2, [(vs, vs_s1e)])
    bcast_row(scl[:, 15:17], rowt[0:1, 158:160], 2, [(vs, vs.n)])
    bcast_row(scl[:, 17:18], rowt[0:1, 168:169], 1, [(vs, vs.n)])
    vs_s1bc = vs.n
    P.do("vector", lambda e: e.tensor_scalar(out=b1ss[:, :], in0=b1cm[:, :], scalar1=scl[:, 17:18],
                                             scalar2=None, op0=ALU.mult), wait=[(vs, vs_s1bc)], inc=[(vs, 1)])
    vs_ss1 = vs.n

    w1qb = []
    w1_gu = []
    for k in range(DC):
        h, gu = alloc(f"w1qb{k}", [128, HH], BF16)
        w1qb.append(h)
        w1_gu += gu
    for kk in range(DC):
        for c in range(0, HH, 512):
            i_ = rg_ix[0] % 4
            rg_ix[0] += 1
            bB, bA = rgB[i_], rgA[i_]
            P.do("sync", lambda e, kk=kk, c=c, bB=bB: e.dma_start(
                out=bB[:, :], in_=w1T[kk * 128:(kk + 1) * 128, c:c + 512]),
                wait=rgB_busy[i_] + (w1_gu + [(vs, vs_ss1)] if (kk == 0 and c == 0) else []),
                inc=[(ld, 16)])
            P.do("scalar", lambda e, bB=bB, bA=bA: e.activation(out=bA[:, :], in_=bB[:, :], func=AF.Identity,
                                                                bias=MAGIC, scale=scl[:, 16:17]),
                 wait=_merge_waits([(ld, ld.n)], rgA_busy[i_]), inc=[(ss, 1)])
            rgB_busy[i_] = [(ss, ss.n)]
            P.do("vector", lambda e, kk=kk, c=c, bA=bA: e.tensor_scalar(
                out=w1qb[kk][:, c:c + 512], in0=bA[:, :], scalar1=-MAGIC, scalar2=None, op0=ALU.add),
                wait=[(ss, ss.n)], inc=[(vs, 1)])
            rgA_busy[i_] = [(vs, vs.n)]
    vs_w1q = vs.n

    h_tile = []
    ht_gu = []
    for i_ in range(4):
        h, gu = alloc(f"ht{i_}", [128, 512], F32)
        h_tile.append(h)
        ht_gu += gu
    h_busy = [0] * 4
    h_vbusy = [0] * 4
    m2_first = [True]
    bank_ss1 = [0, 0]
    xslB = []
    xslB_gu = []
    for k in range(DC):
        h, gu = alloc(f"xslB{k}", [128, 512], BF16)
        xslB.append(h)
        xslB_gu += gu
    xsl2 = [xsl, xslB]
    xsl2_busy = [[(ts, ts_kv_proj)] + xslB_gu, list(xslB_gu)]
    vs_feed = [0] * len(NTS_W)

    def m1_feed(t):
        c0, w = NTS_W[t]
        dst = xsl2[t % 2]
        for kk in range(DC):
            i_ = rg_ix[0] % 4
            rg_ix[0] += 1
            bA = rgA[i_]
            P.do("scalar", lambda e, kk=kk, c0=c0, w=w, bA=bA: e.activation(
                out=bA[:, 0:w], in_=xn[kk][:, c0:c0 + w], func=AF.Identity, bias=MAGIC,
                scale=scl[:, 14:15]),
                wait=_merge_waits(rgA_busy[i_],
                                  ([(vs, vs_s1bc)] if t == 0 else []) if kk == 0 else []),
                inc=[(ss, 1)])
            P.do("vector", lambda e, kk=kk, w=w, bA=bA, dst=dst: e.tensor_scalar(
                out=dst[kk][:, 0:w], in0=bA[:, 0:w], scalar1=-MAGIC, scalar2=None, op0=ALU.add),
                wait=_merge_waits([(ss, ss.n)], xsl2_busy[t % 2] if kk == 0 else []), inc=[(vs, 1)])
            rgA_busy[i_] = [(vs, vs.n)]
        vs_feed[t] = vs.n

    m1_feed(0)
    for ni, (c0, w) in enumerate(NTS_W):
        if ni + 1 < len(NTS_W):
            m1_feed(ni + 1)
        rhs = xsl2[ni % 2]
        for m in range(HC):
            pb = psD if m % 2 == 0 else psE
            for kk in range(DC):
                P.do("tensor", lambda e, kk=kk, m=m, w=w, pb=pb, rhs=rhs: e.matmul(
                    pb[:, 0:w], w1qb[kk][:, m * 128:(m + 1) * 128], rhs[kk][:, 0:w],
                    start=(kk == 0), stop=(kk == DC - 1)),
                    wait=(_merge_waits([(ss, bank_ss1[m % 2]), (vs, vs_feed[ni])],
                                       [(vs, vs_w1q)] if (ni == 0 and m < 2) else [])
                          if kk == 0 else []),
                    inc=[(ts, 1)] if kk == DC - 1 else [])
            tile = h_tile[m % 4]
            P.do("scalar", lambda e, m=m, w=w, pb=pb, tile=tile: e.activation(
                out=tile[:, 0:w], in_=pb[:, 0:w], func=AF.Gelu, bias=b1ss[:, m:m + 1], scale=scl[:, 17:18]),
                wait=_merge_waits([(ts, ts.n), (gd, h_busy[m % 4]), (vs, h_vbusy[m % 4])],
                                  ht_gu if (ni == 0 and m < 4) else []), inc=[(ss, 1)])
            ssn = ss.n
            bank_ss1[m % 2] = ssn
            o0 = max(PAD - c0, 0)
            if o0 < w:
                if m2_first[0]:
                    P.do("vector", lambda e, w=w, o0=o0, tile=tile: e.reduce_max(
                        out=scl[:, 18:19], in_=tile[:, o0:w], axis=AX.X, apply_absolute_value=True),
                        wait=[(ss, ssn)], inc=[(vs, 1)])
                    m2_first[0] = False
                else:
                    P.do("vector", lambda e, w=w, o0=o0, tile=tile: e.reduce_max(
                        out=scl[:, 19:20], in_=tile[:, o0:w], axis=AX.X, apply_absolute_value=True),
                        wait=[(ss, ssn)], inc=[(vs, 1)])
                    P.do("vector", lambda e: e.tensor_tensor(out=scl[:, 18:19], in0=scl[:, 18:19],
                                                             in1=scl[:, 19:20], op=ALU.max), inc=[(vs, 1)])
            P.do("gpsimd", lambda e, m=m, c0=c0, w=w, tile=tile: e.dma_start(
                out=h_sp[m * 128:(m + 1) * 128, c0:c0 + w], in_=tile[:, 0:w]),
                wait=[(ss, ssn)], inc=[(gd, 16)])
            h_busy[m % 4] = gd.n
            h_vbusy[m % 4] = vs.n
        xsl2_busy[ni % 2] = [(ts, ts.n)]
    gd_hsp = gd.n
    ts_m1 = ts.n
    for k in range(DC):
        A.free(w1qb[k], [(ts, ts_m1)])
        A.free(xslB[k], [(ts, ts_m1)])
    for i_ in range(4):
        A.free(h_tile[i_], [(gd, h_busy[i_]), (vs, h_vbusy[i_])])

    P.do("tensor", lambda e: e.transpose(psE[0:1, 0:128], scl[:, 18:19], idn[:, :]),
         wait=[(vs, vs.n), (ss, ss.n)], inc=[(ts, 1)])
    P.do("vector", lambda e: e.reduce_max(out=scl[0:1, 19:20], in_=psE[0:1, 0:128], axis=AX.X),
         wait=[(ts, ts.n)], inc=[(vs, 1)])
    vs_m2 = vs.n
    P.do("gpsimd", lambda e: e.dma_start(out=cc_m2_i[:, 0:8], in_=scl[0:1, 19:27]),
         wait=[(vs, vs_m2)], inc=[(gd, 16)])
    P.do("gpsimd", lambda e: e.collective_compute(
        "AllGather", ALU.bypass, replica_groups=[list(range(8))],
        ins=[cc_m2_i.opt()], outs=[cc_m2_o.opt()]), wait=[(gd, gd.n)], inc=[(cs, 1)])
    cs_m2 = cs.n

    P.do("sync", lambda e: e.dma_start(out=rowt[0:8, 160:168], in_=cc_m2_o[:, :]),
         wait=[(cs, cs_m2)], inc=[(ld, 16)])
    P.do("tensor", lambda e: e.transpose(psE[0:8, 0:8], rowt[0:8, 160:168], idn[0:8, 0:8]),
         wait=[(ld, ld.n), (vs, vs.n), (ss, ss.n)], inc=[(ts, 1)])
    P.do("vector", lambda e: e.reduce_max(out=rowt[0:1, 169:170], in_=psE[0:1, 0:8], axis=AX.X),
         wait=[(ts, ts.n)], inc=[(vs, 1)])
    P.do("vector", lambda e: e.tensor_scalar(out=rowt[0:1, 170:171], in0=rowt[0:1, 169:170],
                                             scalar1=MIN_RANGE, scalar2=1.0 / QP, op0=ALU.max, op1=ALU.mult),
         inc=[(vs, 1)])
    P.do("vector", lambda e: e.reciprocal(out=rowt[0:1, 171:172], in_=rowt[0:1, 170:171]), inc=[(vs, 1)])
    P.do("vector", lambda e: e.tensor_scalar(out=rowt[0:1, 172:173], in0=rowt[0:1, 154:155],
                                             scalar1=MIN_RANGE, scalar2=1.0 / QP, op0=ALU.max, op1=ALU.mult),
         inc=[(vs, 1)])
    P.do("vector", lambda e: e.reciprocal(out=rowt[0:1, 173:174], in_=rowt[0:1, 172:173]), inc=[(vs, 1)])
    P.do("vector", lambda e: e.tensor_tensor(out=rowt[0:1, 174:175], in0=rowt[0:1, 172:173],
                                             in1=rowt[0:1, 170:171], op=ALU.mult), inc=[(vs, 1)])
    vs_s2 = vs.n
    bcast_row(scl[:, 20:25], rowt[0:1, 170:175], 5, [(vs, vs_s2)])
    vs_ss2 = vs.n
    P.do("vector", lambda e: e.tensor_scalar(out=vcm[:, 88:96], in0=vc("b2"), scalar1=scl[:, 24:25],
                                             scalar2=None, op0=ALU.mult), wait=[(vs, vs_ss2)], inc=[(vs, 1)])
    vs_b2ss = vs.n

    for k in range(DC):
        A.free(xsl[k], [(ts, ts_m1)])
    w2qb = []
    w2_gu = []
    for k in range(HC):
        h, gu = alloc(f"w2qb{k}", [128, D], BF16)
        w2qb.append(h)
        w2_gu += gu
    for kk in range(HC):
        for cq in range(0, D, 512):
            i_ = rg_ix[0] % 4
            rg_ix[0] += 1
            bB, bA = rgB[i_], rgA[i_]
            P.do("sync", lambda e, kk=kk, cq=cq, bB=bB: e.dma_start(
                out=bB[:, :], in_=w2T[kk * 128:(kk + 1) * 128, cq:cq + 512]),
                wait=rgB_busy[i_] + (_merge_waits(w2_gu, [(vs, vs_ss2)]) if (kk == 0 and cq == 0) else []),
                inc=[(ld, 16)])
            P.do("scalar", lambda e, bB=bB, bA=bA: e.activation(out=bA[:, :], in_=bB[:, :], func=AF.Identity,
                                                                bias=MAGIC, scale=scl[:, 23:24]),
                 wait=_merge_waits([(ld, ld.n)], rgA_busy[i_]), inc=[(ss, 1)])
            rgB_busy[i_] = [(ss, ss.n)]
            P.do("scalar", lambda e, kk=kk, cq=cq, bA=bA: e.activation(
                out=w2qb[kk][:, cq:cq + 512], in_=bA[:, :], func=AF.Identity, bias=-MAGIC, scale=1.0),
                inc=[(ss, 1)])
            rgA_busy[i_] = [(ss, ss.n)]
    vs_w2q = ss.n

    vsg, vsg_gu = alloc("vsg", [8, D], F32)
    P.do("sync", lambda e: e.dma_start(out=vsg[0:8, :], in_=cc_vs_o[:, :]),
         wait=_merge_waits(vsg_gu, [(cs, cs_vs)]), inc=[(ld, 16)])
    ld_vsg = ld.n
    vstot, vst_gu = alloc("vstot", [128, 8], F32)
    for g in range(DC):
        P.do("tensor", lambda e, g=g: e.transpose(psE[0:128, 0:8], vsg[0:8, g * 128:(g + 1) * 128],
                                                  idn[0:8, 0:8]),
             wait=[(ld, ld_vsg), (vs, vs.n), (ss, ss.n)], inc=[(ts, 1)])
        P.do("vector", lambda e, g=g: e.tensor_tensor(out=vsum_cm[:, 8:16], in0=psE[0:128, 0:8],
                                                      in1=mbbc[:, :], op=ALU.mult),
             wait=[(ts, ts.n)], inc=[(vs, 1)])
        P.do("vector", lambda e, g=g: e.reduce_sum(out=vstot[:, g:g + 1], in_=vsum_cm[:, 8:16], axis=AX.X),
             wait=(_merge_waits(vst_gu) if g == 0 else []), inc=[(vs, 1)])
    vs_vstot = vs.n
    for m in range(DC):
        for kk in range(DC):
            i_ = rg_ix[0] % 4
            rg_ix[0] += 1
            bB = rgB[i_]
            P.do("sync", lambda e, m=m, kk=kk, bB=bB: e.dma_start(
                out=bB[:, 0:128], in_=wfcT[kk * 128:(kk + 1) * 128, m * 128:(m + 1) * 128]),
                wait=rgB_busy[i_], inc=[(ld, 16)])
            ldw = ld.n
            P.do("tensor", lambda e, m=m, kk=kk, bB=bB: e.matmul(
                psD[:, m:m + 1], bB[:, 0:128], vstot[:, kk:kk + 1],
                start=(kk == 0), stop=(kk == DC - 1)),
                wait=_merge_waits([(ld, ldw)],
                                  [(vs, vs_vstot), (ss, ss.n)] if (m == 0 and kk == 0) else []),
                inc=[(ts, 1)])
            rgB_busy[i_] = [(ts, ts.n)]
    y2cm, y2_gu = alloc("y2cm", [128, 8], F32)
    P.do("vector", lambda e: e.tensor_tensor(out=y2cm[:, :], in0=psD[:, 0:8], in1=vc("bfc"), op=ALU.add),
         wait=_merge_waits([(ts, ts.n)], y2_gu), inc=[(vs, 1)])
    vs_y2 = vs.n
    ts_y2 = ts.n
    A.free(vsg, [(ts, ts_y2)])

    if dbg == "xn":
        for g in range(DC):
            P.do("sync", lambda e, g=g: e.dma_start(out=dbg_ext[g * 128:(g + 1) * 128, :], in_=xn[g][:, :]),
                 wait=[(vs, vs.n), (ss, ss.n)], inc=[(ld, 16)])
    dbg_ld0 = ld.n

    for g in range(DC):
        P.do("vector", lambda e, g=g: e.tensor_scalar(
            out=xn[g][:, :], in0=xn[g][:, :], scalar1=y2cm[:, g:g + 1], scalar2=None, op0=ALU.add),
            wait=[(vs, vs_y2), (ss, ss.n), (ld, dbg_ld0)] if g == 0 else [], inc=[(vs, 1)])
    vs_h0 = vs.n
    mu_t, mu_gu = A.alloc("mu2_t", [128, WQ], F32)
    rstd_t, rs_gu = A.alloc("rstd2_t", [128, WQ], F32)
    emit_ln(xn, mu_t, rstd_t, _merge_waits([(vs, vs_h0)], mu_gu, rs_gu))
    vs_ln2st = vs.n
    emit_ln_norm(xn, xn, [], mu_t, rstd_t, gamma=vc("gamma"), beta=vc("beta"))
    gs_attn0 = vs.n
    A.free(mu_t, [(vs, gs_attn0)])
    A.free(rstd_t, [(vs, gs_attn0)])
    if dbg == "attn0":
        for g in range(DC):
            P.do("sync", lambda e, g=g: e.dma_start(out=dbg_ext[g * 128:(g + 1) * 128, :], in_=xn[g][:, :]),
                 wait=[(vs, gs_attn0)], inc=[(ld, 16)])
    dbg_ld = ld.n

    dump2(1, xn, [(vs, gs_attn0)])
    ld_dmp1 = ld.n
    x1 = xn
    xrt = []
    xrt_gu = []
    for i_ in range(2):
        h, gu = alloc(f"xrt{i_}", [128, WQ], F32)
        xrt.append(h)
        xrt_gu += gu
    xrt_busy = [0, 0]
    for g in range(DC):
        buf = xrt[g % 2]
        P.do("sync", lambda e, g=g, buf=buf: e.dma_start(out=buf[:, :], in_=x_ext[g * 128:(g + 1) * 128, :]),
             wait=_merge_waits(xrt_gu if g < 2 else [], [(vs, xrt_busy[g % 2])]), inc=[(ld, 16)])
        ldx = ld.n
        P.do("vector", lambda e, g=g, buf=buf: e.tensor_tensor(out=x1[g][:, :], in0=x1[g][:, :],
                                                               in1=buf[:, :], op=ALU.add),
             wait=_merge_waits([(ld, ldx)], [(vs, gs_attn0), (ld, ld_dmp1)] if g == 0 else []),
             inc=[(vs, 1)])
        xrt_busy[g % 2] = vs.n
    gd_xacc = vs.n
    for i_ in range(2):
        A.free(xrt[i_], [(vs, gd_xacc)])
    vs_x1a = vs.n
    gs_attn0_ctx = gs_attn0
    dump2(2, x1, [(vs, gd_xacc)])
    if dbg == "all":
        P.do("sync", lambda e: e.dma_start(out=dbg3[:, 0:32], in_=scl[:, 0:32]),
             wait=[(vs, vs.n), (ss, ss.n)], inc=[(ld, 16)])
        P.do("sync", lambda e: e.dma_start(out=dbg3[:, 32:48], in_=vsum_cm[:, 0:16]), inc=[(ld, 16)])
        P.do("sync", lambda e: e.dma_start(out=dbg3[:, 48:56], in_=y2cm[:, 0:8]), inc=[(ld, 16)])
        P.do("sync", lambda e: e.dma_start(out=dbg3[:, 56:88], in_=expv[:, 0:32]), inc=[(ld, 16)])
        P.do("sync", lambda e: e.dma_start(out=dbg3[:, 88:96], in_=vstot[:, 0:8]), inc=[(ld, 16)])
        P.do("sync", lambda e: e.dma_start(out=dbg3[:, 96:104], in_=mbbc[:, 0:8]), inc=[(ld, 16)])
        P.do("sync", lambda e: e.dma_start(out=dbg3[:, 104:112], in_=b1ss[:, 0:8]), inc=[(ld, 16)])


    ctx = dict(
        nc=nc, P=P, A=A, alloc=alloc, dbg=dbg, dbg_ext=dbg_ext,

        ld=ld, gd=gd, vs=vs, ss=ss, ts=ts, gs=gs, cs=cs,
        psA=psA, psB=psB, psC=psC, psD=psD, psE=psE,
        idn=idn, idnb=idnb, ones128b=ones128b, onescol=onescol,
        epsc=epsc, scl=scl, vcm=vcm, vc=vc, b1cm=b1cm, b1ss=b1ss, e2048bc=e2048bc, mjbc=mjbc, expv=expv,
        rowt=rowt, xsl=xsl,
        ln_sc=ln_sc, castp=castp, rgA=rgA, rgB=rgB, rgA_busy=rgA_busy, rgB_busy=rgB_busy, rg_ix=rg_ix,
        emit_ln=emit_ln, emit_ln_norm=emit_ln_norm, absmax_chunks=absmax_chunks,
        bcast_row=bcast_row, load_wT_bf16=load_wT_bf16,
        wfcT=wfcT, w2T=w2T, wl1T=wl1T, vecs=vecs, out_ext=out_ext,
        xn_sp=xn_sp, h_sp=h_sp, x1_sp=x1_sp, k_sp=k_sp, v_sp=v_sp, sr_sp=sr_sp, r2_sp=r2_sp,
        cc_m2_o=cc_m2_o, cc_wk_i=cc_wk_i, cc_wk_o=cc_wk_o,
        gd_hsp=gd_hsp, cs_m2=cs_m2,
        x1=x1, vs_x1a=vs_x1a, wkv_dbg=wkv_dbg, dump2=dump2, dbg2=dbg2, dbg3=dbg3,
        w2qb=w2qb, vs_w2q=vs_w2q, gd_xacc=gd_xacc, vs_b2ss=vs_b2ss, vs_ln2st=vs_ln2st,
        gs_attn0=gs_attn0,
    )
    _build_part2(ctx)
    P.verify_no_deadlock()
    P.emit()
    return nc




def _build_part2(c):
    nc = c["nc"]; P = c["P"]; A = c["A"]; alloc = c["alloc"]
    ld = c["ld"]; gd = c["gd"]; vs = c["vs"]; ss = c["ss"]; ts = c["ts"]; gs = c["gs"]; cs = c["cs"]
    psA = c["psA"]; psB = c["psB"]; psC = c["psC"]; psD = c["psD"]; psE = c["psE"]
    idn = c["idn"]; idnb = c["idnb"]; ones128b = c["ones128b"]; onescol = c["onescol"]
    epsc = c["epsc"]; scl = c["scl"]
    vc = c["vc"]; vcm = c["vcm"]; b1cm = c["b1cm"]; e2048bc = c["e2048bc"]; mjbc = c["mjbc"]; expv = c["expv"]
    rowt = c["rowt"]; xsl = c["xsl"]
    ln_sc = c["ln_sc"]; castp = c["castp"]
    rgA = c["rgA"]; rgB = c["rgB"]; rgA_busy = c["rgA_busy"]; rgB_busy = c["rgB_busy"]; rg_ix = c["rg_ix"]
    emit_ln = c["emit_ln"]; emit_ln_norm = c["emit_ln_norm"]
    bcast_row = c["bcast_row"]; load_wT_bf16 = c["load_wT_bf16"]
    wfcT = c["wfcT"]; w2T = c["w2T"]; wl1T = c["wl1T"]; out_ext = c["out_ext"]
    h_sp = c["h_sp"]; x1_sp = c["x1_sp"]
    k_sp = c["k_sp"]; v_sp = c["v_sp"]; sr_sp = c["sr_sp"]; r2_sp = c["r2_sp"]
    cc_m2_o = c["cc_m2_o"]; cc_wk_i = c["cc_wk_i"]; cc_wk_o = c["cc_wk_o"]
    gd_hsp = c["gd_hsp"]; cs_m2 = c["cs_m2"]
    x1 = c["x1"]; vs_x1a = c["vs_x1a"]
    dbg = c["dbg"]; dbg_ext = c["dbg_ext"]; wkv_dbg = c["wkv_dbg"]; dump2 = c["dump2"]
    dbg2 = c["dbg2"]; dbg3 = c["dbg3"]

    vs_x1 = vs_x1a


    w2qb = c["w2qb"]
    vs_w2q = c["vs_w2q"]
    gd_xacc = c["gd_xacc"]
    vs_b2ss = c["vs_b2ss"]
    vs_ln2st = c["vs_ln2st"]
    gs_attn0 = c["gs_attn0"]

    hqsl2 = []
    hq_gu = []
    for p_ in range(2):
        row = []
        for k in range(HC):
            h, gu = alloc(f"hqsl{p_}_{k}", [128, 256], BF16)
            row.append(h)
            hq_gu += gu
        hqsl2.append(row)
    etl = []
    etl_gu = []
    for i_ in range(4):
        h, gu = alloc(f"etl{i_}", [128, 256], F32)
        etl.append(h)
        etl_gu += gu
    etl_busy = [0, 0, 0, 0]
    NTS_M2 = [(c, 256) for c in range(0, CH, 256)] + [(CH, 128)]
    hq2_busy = [_merge_waits(hq_gu, [(ss, vs_w2q)]), list(hq_gu)]
    gs_feed2 = [0] * len(NTS_M2)
    bank_ss2 = [0, 0]

    def m2_feed(t, k0=0, k1=HC):
        c0, w = NTS_M2[t]
        dst = hqsl2[t % 2]
        for kk in range(k0, k1):
            i_ = rg_ix[0] % 4
            rg_ix[0] += 1
            bB, bA = rgB[i_], rgA[i_]
            P.do("sync", lambda e, kk=kk, c0=c0, w=w, bB=bB: e.dma_start(
                out=bB[:, 0:w], in_=h_sp[kk * 128:(kk + 1) * 128, c0:c0 + w]),
                wait=rgB_busy[i_] + ([(gd, gd_hsp)] + hq2_busy[t % 2] if kk == 0 else []),
                inc=[(ld, 16)])
            P.do("scalar", lambda e, w=w, bB=bB, bA=bA: e.activation(out=bA[:, 0:w], in_=bB[:, 0:w],
                                                                     func=AF.Identity, bias=MAGIC,
                                                                     scale=scl[:, 21:22]),
                 wait=_merge_waits([(ld, ld.n)], rgA_busy[i_]), inc=[(ss, 1)])
            rgB_busy[i_] = [(ss, ss.n)]
            P.do("vector", lambda e, kk=kk, w=w, bA=bA, dst=dst: e.tensor_scalar(
                out=dst[kk][:, 0:w], in0=bA[:, 0:w], scalar1=-MAGIC, scalar2=None, op0=ALU.add),
                wait=[(ss, ss.n)], inc=[(vs, 1)])
            rgA_busy[i_] = [(vs, vs.n)]
        if k1 == HC:
            gs_feed2[t] = vs.n

    m2_feed(0)
    for t, (c0, w) in enumerate(NTS_M2):
        rhs = hqsl2[t % 2]
        for m in range(DC):
            if t + 1 < len(NTS_M2):
                m2_feed(t + 1, m * 4, m * 4 + 4)
            pb = psD if m % 2 == 0 else psE
            for kk in range(HC):
                P.do("tensor", lambda e, kk=kk, m=m, w=w, pb=pb, rhs=rhs: e.matmul(
                    pb[:, 0:w], w2qb[kk][:, m * 128:(m + 1) * 128], rhs[kk][:, 0:w],
                    start=(kk == 0), stop=(kk == HC - 1)),
                    wait=(_merge_waits([(vs, gs_feed2[t]), (ss, bank_ss2[m % 2])],
                                       [(vs, vs_ln2st)] if (t == 0 and m < 2) else [])
                          if kk == 0 else []),
                    inc=[(ts, 1)] if kk == HC - 1 else [])
            tix = (t * DC + m) % 4
            tile = etl[tix]
            P.do("scalar", lambda e, m=m, w=w, pb=pb, tile=tile: e.activation(
                out=tile[:, 0:w], in_=pb[:, 0:w], func=AF.Identity, bias=vcm[:, 88 + m:89 + m],
                scale=scl[:, 24:25]),
                wait=_merge_waits([(ts, ts.n), (vs, etl_busy[tix])],
                                  _merge_waits(etl_gu, [(vs, vs_b2ss)]) if (t == 0 and m < 4) else []),
                inc=[(ss, 1)])
            bank_ss2[m % 2] = ss.n
            P.do("vector", lambda e, m=m, c0=c0, w=w, tile=tile: e.tensor_tensor(
                out=x1[m][:, c0:c0 + w], in0=x1[m][:, c0:c0 + w], in1=tile[:, 0:w], op=ALU.add),
                wait=_merge_waits([(ss, ss.n)],
                                  [(vs, gd_xacc)] if (t == 0 and m == 0) else []),
                inc=[(vs, 1)])
            etl_busy[tix] = vs.n
        hq2_busy[t % 2] = [(ts, ts.n)]
    vs_x1 = vs.n
    gs_m2 = vs.n
    ts_m2 = ts.n
    for k in range(HC):
        A.free(w2qb[k], [(ts, ts_m2)])
        A.free(hqsl2[0][k], [(ts, ts_m2)])
        A.free(hqsl2[1][k], [(ts, ts_m2)])
    for i_ in range(4):
        A.free(etl[i_], [(vs, gs_m2)])

    for i_ in range(4):
        A.free(rgA[i_], rgA_busy[i_] + [(ss, ss.n), (vs, gs_m2)])
    for g in range(DC):
        P.do("gpsimd", lambda e, g=g: e.dma_start(out=x1_sp[g * 128:(g + 1) * 128, :], in_=x1[g][:, :]),
             wait=[(vs, gs_m2)] if g == 0 else [], inc=[(gd, 16)])
    gd_x1sp = gd.n
    if dbg == "x1":
        P.do("sync", lambda e: e.dma_start(out=dbg_ext[:, :], in_=x1_sp[:, :]), wait=[(gd, gd_x1sp)],
             inc=[(ld, 16)])

    mu_t, mu_gu = A.alloc("mu3_t", [128, WQ], F32)
    rstd_t, rs_gu = A.alloc("rstd3_t", [128, WQ], F32)
    emit_ln(x1, mu_t, rstd_t, _merge_waits([(vs, vs_x1)], mu_gu, rs_gu))
    xn1 = []
    xn1_gu = []
    for g in range(DC):
        h, gu = alloc(f"xn1_{g}", [128, WQ], F32)
        xn1.append(h)
        xn1_gu += gu
    emit_ln_norm(x1, xn1, xn1_gu, mu_t, rstd_t)
    gs_xn1 = vs.n
    for g in range(DC):
        P.do("vector", lambda e, g=g: e.tensor_scalar(
            out=xn1[g][:, PAD - 1:PAD], in0=xn1[g][:, PAD - 1:PAD], scalar1=scl[:, 4:5], scalar2=None,
            op0=ALU.mult), wait=[(vs, gs_xn1)] if g == 0 else [], inc=[(vs, 1)])
    vs_xn1 = vs.n
    for g in range(DC):
        A.free(x1[g], [(vs, gs_xn1), (gd, gd_x1sp)])
    A.free(mu_t, [(vs, gs_xn1)])
    A.free(rstd_t, [(vs, gs_xn1)])
    if dbg == "xn1":
        for g in range(DC):
            P.do("sync", lambda e, g=g: e.dma_start(out=dbg_ext[g * 128:(g + 1) * 128, :], in_=xn1[g][:, :]),
                 wait=[(vs, vs_xn1)], inc=[(ld, 16)])
    dump2(3, xn1, [(vs, vs_xn1)])

    xslq = []
    xsl_gu2 = []
    for p_ in range(2):
        row = []
        for k in range(DC):
            h, gu = alloc(f"xsl3_{p_}_{k}", [128, 512], BF16)
            row.append(h)
            xsl_gu2 += gu
        xslq.append(row)
    xslq_busy = [list(xsl_gu2), list(xsl_gu2)]
    wLbq = []
    wL_gu = []
    for p_ in range(2):
        row = []
        for k in range(DC):
            h, gu = alloc(f"wLb{p_}_{k}", [128, D], BF16)
            row.append(h)
            wL_gu += gu
        wLbq.append(row)
    wLbq_busy = [list(wL_gu), list(wL_gu)]
    wLb_ix = [0]

    def next_wLb():
        p_ = wLb_ix[0] % 2
        wLb_ix[0] += 1
        return wLbq[p_], p_

    kct = []
    kct_gu = []
    for g in range(DC):
        h, gu = alloc(f"kct{g}", [128, CH], BF16)
        kct.append(h)
        kct_gu += gu
    pt0, pt_gu0 = alloc("pt0", [128, 512], F32)
    pt1, pt_gu1 = alloc("pt1", [128, 512], F32)
    ptile = [pt0, pt1]
    ptile_busy = [list(pt_gu0), list(pt_gu1)]
    bank_busy = [0, 0]

    def l1_load(plane):
        wLb, wp = next_wLb()
        ss_w = load_wT_bf16(wl1T[plane, :, :], wLb, first_wait=wLbq_busy[wp])
        return (wLb, wp, ss_w)

    def l1_proj(wh, tmname, mode, dst_sp=None, dst_res=None, extra_first=()):
        wLb, wp, ss_w = wh
        tmv = vc(tmname)
        for ni, (c0, w) in enumerate(NTS_CH):
            a0 = PAD + c0
            xp = ni % 2
            xsl = xslq[xp]
            for kk in range(DC):
                P.do("vector", lambda e, kk=kk, a0=a0, w=w: e.tensor_tensor(
                    out=ln_sc[:, 0:w], in0=xn1[kk][:, a0:a0 + w], in1=xn1[kk][:, a0 - 1:a0 - 1 + w],
                    op=ALU.subtract),
                    wait=(_merge_waits([(vs, vs_xn1)], xslq_busy[xp],
                                       list(extra_first) if ni == 0 else []) if kk == 0 else []),
                    inc=[(vs, 1)])
                P.do("vector", lambda e, kk=kk, a0=a0, w=w, tmv=tmv, xsl=xsl: e.scalar_tensor_tensor(
                    out=xsl[kk][:, 0:w], in0=ln_sc[:, 0:w], scalar=tmv[:, kk:kk + 1],
                    in1=xn1[kk][:, a0 - 1:a0 - 1 + w], op0=ALU.mult, op1=ALU.add), inc=[(vs, 1)])
            vs_sl = vs.n
            for m in range(DC):
                pb = psD if m % 2 == 0 else psE
                for kk in range(DC):
                    P.do("tensor", lambda e, kk=kk, m=m, w=w, pb=pb, xsl=xsl, wLb=wLb: e.matmul(
                        pb[:, 0:w], wLb[kk][:, m * 128:(m + 1) * 128], xsl[kk][:, 0:w],
                        start=(kk == 0), stop=(kk == DC - 1)),
                        wait=(_merge_waits([(vs, vs_sl), (ss, bank_busy[m % 2])],
                                           [(ss, ss_w)] if (ni == 0 and m == 0) else [])
                              if kk == 0 else []),
                        inc=[(ts, 1)] if kk == DC - 1 else [])
                if mode == "sig_spill":
                    tile = ptile[m % 2]
                    P.do("scalar", lambda e, w=w, pb=pb, tile=tile: e.activation(
                        out=tile[:, 0:w], in_=pb[:, 0:w], func=AF.Sigmoid),
                        wait=_merge_waits([(ts, ts.n)], ptile_busy[m % 2]), inc=[(ss, 1)])
                    P.do("gpsimd", lambda e, m=m, c0=c0, w=w, tile=tile: e.dma_start(
                        out=dst_sp[m * 128:(m + 1) * 128, c0:c0 + w], in_=tile[:, 0:w]),
                        wait=[(ss, ss.n)], inc=[(gd, 16)])
                    ptile_busy[m % 2] = [(gd, gd.n)]
                    bank_busy[m % 2] = ss.n
                elif mode == "copy_spill":
                    tile = ptile[m % 2]
                    P.do("scalar", lambda e, w=w, pb=pb, tile=tile: e.activation(
                        out=tile[:, 0:w], in_=pb[:, 0:w], func=AF.Copy),
                        wait=_merge_waits([(ts, ts.n)], ptile_busy[m % 2]), inc=[(ss, 1)])
                    P.do("gpsimd", lambda e, m=m, c0=c0, w=w, tile=tile: e.dma_start(
                        out=dst_sp[m * 128:(m + 1) * 128, c0:c0 + w], in_=tile[:, 0:w]),
                        wait=[(ss, ss.n)], inc=[(gd, 16)])
                    ptile_busy[m % 2] = [(gd, gd.n)]
                    bank_busy[m % 2] = ss.n
                elif mode == "sig_res":
                    P.do("scalar", lambda e, m=m, c0=c0, w=w, pb=pb: e.activation(
                        out=dst_res[m][:, c0:c0 + w], in_=pb[:, 0:w], func=AF.Sigmoid),
                        wait=[(ts, ts.n)], inc=[(ss, 1)])
                    bank_busy[m % 2] = ss.n
                elif mode == "relusq_res":
                    tile = ptile[m % 2]
                    P.do("scalar", lambda e, w=w, pb=pb, tile=tile: e.activation(
                        out=tile[:, 0:w], in_=pb[:, 0:w], func=AF.Relu),
                        wait=_merge_waits([(ts, ts.n), (vs, vs.n)], ptile_busy[m % 2]), inc=[(ss, 1)])
                    P.do("vector", lambda e, m=m, c0=c0, w=w, tile=tile: e.tensor_tensor(
                        out=dst_res[m][:, c0:c0 + w], in0=tile[:, 0:w], in1=tile[:, 0:w], op=ALU.mult),
                        wait=[(ss, ss.n)], inc=[(vs, 1)])
                    ptile_busy[m % 2] = [(vs, vs.n)]
                    bank_busy[m % 2] = ss.n
            xslq_busy[xp] = [(ts, ts.n)]
        wLbq_busy[wp] = [(ts, ts.n)]

    wh0 = l1_load(0)
    wh1 = l1_load(1)
    l1_proj(wh0, "tmr", "sig_spill", dst_sp=sr_sp)
    gd_sr = gd.n
    wh2 = l1_load(2)
    l1_proj(wh1, "tmk", "copy_spill", dst_sp=k_sp)
    gd_k = gd.n
    wh3 = l1_load(3)
    l1_proj(wh2, "tmv", "copy_spill", dst_sp=v_sp)
    gd_v = gd.n
    wh4 = l1_load(4)
    l1_proj(wh3, "mur", "sig_spill", dst_sp=r2_sp)
    gd_r2 = gd.n
    wh5 = l1_load(5)
    l1_proj(wh4, "muk", "relusq_res", dst_res=kct, extra_first=kct_gu)
    vs_l1p = vs.n
    ss_l1p = ss.n

    wLb, wp5, ss_wv2 = wh5
    wh6 = l1_load(6)
    ffn1_sp = nc.dram_tensor("ffn1_sp", [D, CH], F32).ap()
    rt_busy = [0, 0]
    rt_vs = [0, 0]
    f1_bank = [0, 0]
    for ni, (c0, w) in enumerate(NTS_CH):
        for m in range(DC):
            pb = psD if m % 2 == 0 else psE
            tile = ptile[m % 2]
            P.do("sync", lambda e, m=m, c0=c0, w=w, tile=tile: e.dma_start(
                out=tile[:, 0:w], in_=r2_sp[m * 128:(m + 1) * 128, c0:c0 + w]),
                wait=_merge_waits([(gd, gd_r2), (gd, rt_busy[m % 2]), (vs, rt_vs[m % 2])],
                                  ptile_busy[m % 2] if (ni == 0 and m < 2) else []), inc=[(ld, 16)])
            ld_r2 = ld.n
            for kk in range(DC):
                P.do("tensor", lambda e, kk=kk, m=m, c0=c0, w=w, pb=pb: e.matmul(
                    pb[:, 0:w], wLb[kk][:, m * 128:(m + 1) * 128], kct[kk][:, c0:c0 + w],
                    start=(kk == 0), stop=(kk == DC - 1)),
                    wait=([(ss, ss_wv2), (ss, ss_l1p), (vs, vs_l1p)] if (ni == 0 and m == 0 and kk == 0)
                          else ([(vs, f1_bank[m % 2])] if kk == 0 else [])),
                    inc=[(ts, 1)] if kk == DC - 1 else [])
            P.do("vector", lambda e, m=m, w=w, pb=pb, tile=tile: e.tensor_tensor(
                out=tile[:, 0:w], in0=pb[:, 0:w], in1=tile[:, 0:w], op=ALU.mult),
                wait=[(ts, ts.n), (ld, ld_r2)], inc=[(vs, 1)])
            f1_bank[m % 2] = vs.n
            rt_vs[m % 2] = vs.n
            P.do("gpsimd", lambda e, m=m, c0=c0, w=w, tile=tile: e.dma_start(
                out=ffn1_sp[m * 128:(m + 1) * 128, c0:c0 + w], in_=tile[:, 0:w]),
                wait=[(vs, vs.n)], inc=[(gd, 16)])
            rt_busy[m % 2] = gd.n
    gd_ffn1 = gd.n
    ts_ffn1 = ts.n
    wLbq_busy[wp5] = [(ts, ts_ffn1)]
    for k in range(DC):
        A.free(wLbq[wp5][k], [(ts, ts_ffn1)])
        A.free(xslq[0][k], [(ts, ts_ffn1)])
        A.free(xslq[1][k], [(ts, ts_ffn1)])
    for g in range(DC):
        A.free(kct[g], [(ts, ts_ffn1)])
        A.free(xn1[g], [(vs, vs_l1p)])

    for cp_ in castp:
        A.free(cp_[0], [(ts, ts.n)])
        A.free(cp_[1], [(ts, ts.n)])
    A.free(ln_sc, [(vs, vs.n), (ss, ss.n)])
    ones2048, o2_gu = A.alloc("ones2048b", [128, 2048], F32)
    jrow, jr_gu = A.alloc("jrowb", [128, 2048], F32)
    P.do("gpsimd", lambda e: e.memset(ones2048[:, :], 1.0), wait=_merge_waits(o2_gu, jr_gu), inc=[(gs, 1)])
    P.do("gpsimd", lambda e: e.iota(jrow[:, :], pattern=[[1, 2048]], base=0, channel_multiplier=0,
                                    allow_small_or_imprecise_dtypes=True), inc=[(gs, 1)])
    gs_wkv = gs.n
    sets = []
    for si in range(2):
        kf, gu1 = alloc(f"wkv_kf{si}", [128, CH], F32)
        vf, gu2 = alloc(f"wkv_vf{si}", [128, CH], F32)
        p0a, gu3 = alloc(f"wkv_p0a{si}", [128, CH + 16], F32)
        p0b, gu4 = alloc(f"wkv_p0b{si}", [128, CH + 16], F32)
        et, gu5 = alloc(f"wkv_et{si}", [128, CH], F32)
        srf, gu6 = alloc(f"wkv_sr{si}", [128, CH], F32)
        aj, gu7 = alloc(f"wkv_aj{si}", [128, 24], F32)
        sets.append(dict(kf=kf, vf=vf, p0a=p0a, p0b=p0b, et=et, srf=srf, aj=aj,
                         gu=gu1 + gu2 + gu3 + gu4 + gu5 + gu6 + gu7, vs_done=0, ld_done=0))
    wkv_rs = []
    wkvrs_gu = []
    for g in range(DC):
        h, gu = alloc(f"wkv_rs{g}", [128, CH], BF16)
        wkv_rs.append(h)
        wkvrs_gu += gu

    cs_chunk = [0] * DC

    def wkv_pre(g):
        S = sets[g % 2]
        kf, vf, p0a, p0b, et, srf = S["kf"], S["vf"], S["p0a"], S["p0b"], S["et"], S["srf"]
        wload = _merge_waits(S["gu"] if g < 2 else [], [(vs, S["vs_done"]), (gs, S["gs_done"]),
                                                        (gd, gd_v), (gd, gd_sr)])
        P.do("sync", lambda e, g=g, kf=kf: e.dma_start(out=kf[:, :], in_=k_sp[g * 128:(g + 1) * 128, :]),
             wait=wload, inc=[(ld, 16)])
        P.do("sync", lambda e, g=g, vf=vf: e.dma_start(out=vf[:, :], in_=v_sp[g * 128:(g + 1) * 128, :]),
             inc=[(ld, 16)])
        P.do("sync", lambda e, g=g, srf=srf: e.dma_start(out=srf[:, :], in_=sr_sp[g * 128:(g + 1) * 128, :]),
             inc=[(ld, 16)])
        ld_kv = ld.n
        P.do("scalar", lambda e, kf=kf: e.activation(out=kf[:, :], in_=kf[:, :], func=AF.Exp),
             wait=[(ld, ld_kv)], inc=[(ss, 1)])
        P.do("gpsimd", lambda e, kf=kf, vf=vf: e.tensor_tensor(out=vf[:, :], in0=kf[:, :], in1=vf[:, :],
                                                               op=ALU.mult),
             wait=[(ss, ss.n)], inc=[(gs, 1)])
        gs_ekv = gs.n
        if wkv_dbg is not None:
            P.do("sync", lambda e, g=g, kf=kf: e.dma_start(
                out=wkv_dbg[0, g * 128:(g + 1) * 128, :], in_=kf[:, :]), wait=[(vs, vs.n)], inc=[(ld, 16)])
            P.do("sync", lambda e, g=g, vf=vf: e.dma_start(
                out=wkv_dbg[1, g * 128:(g + 1) * 128, :], in_=vf[:, :]), inc=[(ld, 16)])
        P.do("scalar", lambda e, g=g, et=et: e.activation(out=et[:, :], in_=ones2048[:, :], func=AF.Copy,
                                                          scale=expv[:, 16 + g:17 + g]),
             wait=[(gs, gs_wkv)] if g == 0 else [], inc=[(ss, 1)])
        ss_bc = ss.n
        P.do("vector", lambda e, p0a=p0a: e.memset(p0a[:, 15:16], 0.0), inc=[(vs, 1)])
        P.do("vector", lambda e, p0b=p0b: e.memset(p0b[:, 15:16], 0.0), inc=[(vs, 1)])
        P.do("vector", lambda e, p0a=p0a, et=et, vf=vf: e.tensor_tensor_scan(
            out=p0a[:, 16:16 + CH], data0=et[:, :], data1=vf[:, :], initial=0.0,
            op0=ALU.mult, op1=ALU.add), wait=[(ss, ss_bc), (gs, gs_ekv)], inc=[(vs, 1)])
        P.do("vector", lambda e, p0b=p0b, et=et, kf=kf: e.tensor_tensor_scan(
            out=p0b[:, 16:16 + CH], data0=et[:, :], data1=kf[:, :], initial=0.0,
            op0=ALU.mult, op1=ALU.add), inc=[(vs, 1)])
        vs_scan = vs.n
        if wkv_dbg is not None:
            P.do("sync", lambda e, g=g, p0a=p0a: e.dma_start(
                out=wkv_dbg[2, g * 128:(g + 1) * 128, :], in_=p0a[:, 16:16 + CH]),
                wait=[(vs, vs_scan)], inc=[(ld, 16)])
        P.do("gpsimd", lambda e, g=g, p0a=p0a: e.dma_start(out=cc_wk_i[g][0:1, 0:128],
                                                           in_=p0a[:, 15 + CH:16 + CH]),
             wait=[(vs, vs_scan)], inc=[(gd, 16)])
        P.do("gpsimd", lambda e, g=g, p0b=p0b: e.dma_start(out=cc_wk_i[g][0:1, 128:256],
                                                           in_=p0b[:, 15 + CH:16 + CH]),
             inc=[(gd, 16)])
        P.do("gpsimd", lambda e, g=g: e.collective_compute(
            "AllGather", ALU.bypass, replica_groups=[list(range(8))],
            ins=[cc_wk_i[g].opt()], outs=[cc_wk_o[g].opt()]), wait=[(gd, gd.n)], inc=[(cs, 1)])
        cs_chunk[g] = cs.n
        P.do("vector", lambda e, g=g, et=et: e.tensor_scalar(out=et[:, :], in0=jrow[:, :],
                                                             scalar1=expv[:, 8 + g:9 + g], scalar2=-87.0,
                                                             op0=ALU.mult, op1=ALU.max), inc=[(vs, 1)])
        P.do("scalar", lambda e, et=et: e.activation(out=et[:, :], in_=et[:, :], func=AF.Exp),
             wait=[(vs, vs.n)], inc=[(ss, 1)])
        S["ss_et"] = ss.n

    def wkv_post(g):
        S = sets[g % 2]
        kf, vf, p0a, p0b, et, srf, aj = S["kf"], S["vf"], S["p0a"], S["p0b"], S["et"], S["srf"], S["aj"]
        rb = 0 if g % 2 == 0 else 256
        P.do("sync", lambda e, g=g, rb=rb: e.dma_start(out=rowt[0:8, rb:rb + 256], in_=cc_wk_o[g][:, :]),
             wait=[(cs, cs_chunk[g]), (ts, S["ts_rb"])], inc=[(ld, 16)])
        ld_ag = ld.n
        P.do("tensor", lambda e, rb=rb: e.transpose(psC[:, 0:8], rowt[0:8, rb:rb + 128], idn[0:8, 0:8]),
             wait=[(ld, ld_ag), (vs, vs.n)], inc=[(ts, 1)])
        P.do("tensor", lambda e, rb=rb: e.transpose(psC[:, 128:136], rowt[0:8, rb + 128:rb + 256],
                                                    idn[0:8, 0:8]),
             inc=[(ts, 1)])
        ts_ag = ts.n
        S["ts_rb"] = ts_ag
        P.do("vector", lambda e, aj=aj: e.tensor_copy(aj[:, 0:8], psC[:, 0:8]), wait=[(ts, ts_ag)], inc=[(vs, 1)])
        P.do("vector", lambda e, aj=aj: e.tensor_copy(aj[:, 8:16], psC[:, 128:136]), inc=[(vs, 1)])
        P.do("vector", lambda e, g=g, aj=aj: e.tensor_scalar(out=aj[:, 16:24], in0=e2048bc[:, :],
                                                             scalar1=expv[:, 8 + g:9 + g], scalar2=-87.0,
                                                             op0=ALU.mult, op1=ALU.max), inc=[(vs, 1)])
        P.do("scalar", lambda e, aj=aj: e.activation(out=aj[:, 16:24], in_=aj[:, 16:24], func=AF.Exp),
             wait=[(vs, vs.n)], inc=[(ss, 1)])
        P.do("vector", lambda e, aj=aj: e.tensor_tensor(out=aj[:, 16:24], in0=aj[:, 16:24], in1=mjbc[:, :],
                                                        op=ALU.mult), wait=[(ss, ss.n)], inc=[(vs, 1)])
        P.do("vector", lambda e, aj=aj: e.tensor_tensor(out=aj[:, 0:8], in0=aj[:, 0:8], in1=aj[:, 16:24],
                                                        op=ALU.mult), inc=[(vs, 1)])
        P.do("vector", lambda e, aj=aj: e.tensor_tensor(out=aj[:, 8:16], in0=aj[:, 8:16], in1=aj[:, 16:24],
                                                        op=ALU.mult), inc=[(vs, 1)])
        P.do("vector", lambda e, aj=aj: e.reduce_sum(out=aj[:, 16:17], in_=aj[:, 0:8], axis=AX.X),
             inc=[(vs, 1)])
        P.do("vector", lambda e, aj=aj: e.reduce_sum(out=aj[:, 17:18], in_=aj[:, 8:16], axis=AX.X),
             inc=[(vs, 1)])
        P.do("vector", lambda e, p0a=p0a, et=et, aj=aj: e.scalar_tensor_tensor(
            out=p0a[:, 15:15 + CH], in0=et[:, :], scalar=aj[:, 16:17], in1=p0a[:, 15:15 + CH],
            op0=ALU.mult, op1=ALU.add), wait=[(ss, S["ss_et"])], inc=[(vs, 1)])
        P.do("vector", lambda e, p0b=p0b, et=et, aj=aj: e.scalar_tensor_tensor(
            out=p0b[:, 15:15 + CH], in0=et[:, :], scalar=aj[:, 17:18], in1=p0b[:, 15:15 + CH],
            op0=ALU.mult, op1=ALU.add), inc=[(vs, 1)])
        if wkv_dbg is not None:
            P.do("sync", lambda e, g=g, p0a=p0a: e.dma_start(
                out=wkv_dbg[3, g * 128:(g + 1) * 128, :], in_=p0a[:, 15:15 + CH]),
                wait=[(vs, vs.n)], inc=[(ld, 16)])
        P.do("vector", lambda e, g=g, vf=vf, p0a=p0a: e.scalar_tensor_tensor(
            out=vf[:, :], in0=vf[:, :], scalar=expv[:, 24 + g:25 + g], in1=p0a[:, 15:15 + CH],
            op0=ALU.mult, op1=ALU.add), inc=[(vs, 1)])
        P.do("vector", lambda e, g=g, kf=kf, p0b=p0b: e.scalar_tensor_tensor(
            out=kf[:, :], in0=kf[:, :], scalar=expv[:, 24 + g:25 + g], in1=p0b[:, 15:15 + CH],
            op0=ALU.mult, op1=ALU.add), inc=[(vs, 1)])
        if wkv_dbg is not None:
            P.do("sync", lambda e, g=g, vf=vf: e.dma_start(
                out=wkv_dbg[4, g * 128:(g + 1) * 128, :], in_=vf[:, :]), wait=[(vs, vs.n)], inc=[(ld, 16)])
            P.do("sync", lambda e, g=g, kf=kf: e.dma_start(
                out=wkv_dbg[5, g * 128:(g + 1) * 128, :], in_=kf[:, :]), inc=[(ld, 16)])
        P.do("vector", lambda e, kf=kf: e.reciprocal(out=kf[:, :], in_=kf[:, :]),
             wait=[(ld, ld.n)] if wkv_dbg is not None else [], inc=[(vs, 1)])
        P.do("gpsimd", lambda e, vf=vf, kf=kf: e.tensor_tensor(out=vf[:, :], in0=vf[:, :], in1=kf[:, :],
                                                               op=ALU.mult), wait=[(vs, vs.n)], inc=[(gs, 1)])
        if wkv_dbg is not None:
            P.do("sync", lambda e, g=g, vf=vf: e.dma_start(
                out=wkv_dbg[6, g * 128:(g + 1) * 128, :], in_=vf[:, :]), wait=[(gs, gs.n)], inc=[(ld, 16)])
        P.do("gpsimd", lambda e, g=g, vf=vf, srf=srf: e.tensor_tensor(
            out=wkv_rs[g][:, :], in0=vf[:, :], in1=srf[:, :], op=ALU.mult),
            wait=(_merge_waits(wkvrs_gu, [(ld, ld.n)] if wkv_dbg is not None else []) if g == 0 or wkv_dbg is not None else []),
            inc=[(gs, 1)])
        S["vs_done"] = vs.n
        S["gs_done"] = gs.n

    for S in sets:
        S["ts_rb"] = ts.n
        S["ss_et"] = 0
        S["gs_done"] = 0
    wkv_pre(0)
    for g in range(DC):
        if g + 1 < DC:
            wkv_pre(g + 1)
        wkv_post(g)
    vs_wkv = vs.n
    gs_wkv2 = gs.n
    if dbg == "all":
        for g in range(DC):
            P.do("vector", lambda e, g=g: e.tensor_copy(sets[0]["kf"][:, :], wkv_rs[g][:, :]),
                 wait=[(vs, vs.n), (gs, gs_wkv2)], inc=[(vs, 1)])
            P.do("sync", lambda e, g=g: e.dma_start(out=dbg2[4, g * 128:(g + 1) * 128, 0:CH],
                                                    in_=sets[0]["kf"][:, :]),
                 wait=[(vs, vs.n)], inc=[(ld, 16)])
            P.do("vector", lambda e: e.nop(), wait=[(ld, ld.n)])
    for S in sets:
        for kkey in ("kf", "vf", "p0a", "p0b", "et", "srf", "aj"):
            A.free(S[kkey], [(vs, vs_wkv), (gs, gs_wkv2)])

    wLb6, wp6, ss_wo = wh6
    f1t = []
    x1t = []
    f1_gu = []
    for g in range(DC):
        h, gu = alloc(f"f1t{g}", [128, 512], F32)
        f1t.append(h)
        f1_gu += gu
        h, gu = alloc(f"x1t{g}", [128, 512], F32)
        x1t.append(h)
        f1_gu += gu
    f1_store = [0]
    wo_bank = [0, 0]
    for ni, (c0, w) in enumerate(NTS_CH):
        for g in range(DC):
            P.do("sync", lambda e, g=g, c0=c0, w=w: e.dma_start(
                out=f1t[g][:, 0:w], in_=ffn1_sp[g * 128:(g + 1) * 128, c0:c0 + w]),
                wait=(_merge_waits(f1_gu, [(gd, gd_ffn1), (gd, gd_x1sp), (gd, f1_store[0])])
                      if g == 0 else []), inc=[(ld, 16)])
            P.do("sync", lambda e, g=g, c0=c0, w=w: e.dma_start(
                out=x1t[g][:, 0:w], in_=x1_sp[g * 128:(g + 1) * 128, PAD + c0:PAD + c0 + w]),
                inc=[(ld, 16)])
        ld_fin = ld.n
        for m in range(DC):
            pb = psD if m % 2 == 0 else psE
            P.do("tensor", lambda e, kk=0, m=m, c0=c0, w=w, pb=pb: e.matmul(
                pb[:, 0:w], wLb6[0][:, m * 128:(m + 1) * 128], wkv_rs[0][:, c0:c0 + w],
                start=True, stop=False),
                wait=_merge_waits([(ss, ss_wo), (gs, gs_wkv2)],
                                  [(vs, wo_bank[m % 2])] if not (ni == 0 and m < 2) else [(vs, vs.n)]))
            for kk in range(1, DC):
                P.do("tensor", lambda e, kk=kk, m=m, c0=c0, w=w, pb=pb: e.matmul(
                    pb[:, 0:w], wLb6[kk][:, m * 128:(m + 1) * 128], wkv_rs[kk][:, c0:c0 + w],
                    start=False, stop=(kk == DC - 1)),
                    inc=[(ts, 1)] if kk == DC - 1 else [])
            P.do("vector", lambda e, m=m, w=w, pb=pb: e.tensor_tensor(
                out=f1t[m][:, 0:w], in0=pb[:, 0:w], in1=f1t[m][:, 0:w], op=ALU.add),
                wait=[(ts, ts.n), (ld, ld_fin)], inc=[(vs, 1)])
            wo_bank[m % 2] = vs.n
            P.do("vector", lambda e, m=m, w=w: e.tensor_tensor(
                out=f1t[m][:, 0:w], in0=f1t[m][:, 0:w], in1=x1t[m][:, 0:w], op=ALU.add),
                inc=[(vs, 1)])
            P.do("gpsimd", lambda e, m=m, c0=c0, w=w: e.dma_start(
                out=out_ext[m * 128:(m + 1) * 128, c0:c0 + w], in_=f1t[m][:, 0:w]),
                wait=[(vs, vs.n)], inc=[(gd, 16)])
        f1_store[0] = gd.n

    P.do("gpsimd", lambda e: e.nop(), wait=[(gd, gd.n)])
    P.do("sync", lambda e: e.nop(), wait=[(ld, ld.n)])


_BUILD_CACHE = {}


def _get_program(dbg=None):
    key = dbg
    if key not in _BUILD_CACHE:
        _BUILD_CACHE[key] = build_program(dbg)
    return _BUILD_CACHE[key]


def _stage_inputs(inputs, dbg=None):
    f32 = np.float32
    x = np.asarray(inputs["L0_attn_x"], f32)
    cat = np.ascontiguousarray

    wqkvT = cat(np.asarray(inputs["L0_attn_w_qkv"], f32).T)
    wfcT = cat(np.asarray(inputs["L0_attn_w_fc"], f32).T)
    w1T = cat(np.asarray(inputs["L0_ffn_w1"], f32).T)
    w2T = cat(np.asarray(inputs["L0_ffn_w2"], f32).T)
    b1v = cat(np.asarray(inputs["L0_ffn_b1"], f32))
    wl1T = cat(np.stack([
        np.asarray(inputs["L1_attn_Wr"], f32).T,
        np.asarray(inputs["L1_attn_Wk"], f32).T,
        np.asarray(inputs["L1_attn_Wv"], f32).T,
        np.asarray(inputs["L1_ffn_W_r"], f32).T,
        np.asarray(inputs["L1_ffn_W_k"], f32).T,
        np.asarray(inputs["L1_ffn_W_v"], f32).T,
        np.asarray(inputs["L1_attn_Wo"], f32).T,
    ]))

    in_maps = []
    for cix in range(NCORE):
        b = cix // 4
        t0 = (cix % 4) * CH
        xs = np.zeros((WQ, D), f32)
        lo = t0 - PAD
        if lo < 0:
            xs[PAD:, :] = x[b, t0:t0 + CH]
        else:
            xs[:, :] = x[b, lo:t0 + CH]
        vecs = np.zeros((16, D), f32)
        vecs[0] = np.asarray(inputs["L0_attn_b_fc"], f32)
        vecs[1] = np.asarray(inputs["L0_attn_gamma"], f32)
        vecs[2] = np.asarray(inputs["L0_attn_beta"], f32)
        vecs[3] = np.asarray(inputs["L0_ffn_b2"], f32)
        vecs[4] = np.asarray(inputs["L1_attn_time_decay"], f32)
        vecs[5] = np.asarray(inputs["L1_attn_time_first"], f32)
        vecs[6] = np.asarray(inputs["L1_attn_time_mix_k"], f32)
        vecs[7] = np.asarray(inputs["L1_attn_time_mix_v"], f32)
        vecs[8] = np.asarray(inputs["L1_attn_time_mix_r"], f32)
        vecs[9] = np.asarray(inputs["L1_ffn_mu_r"], f32)
        vecs[10] = np.asarray(inputs["L1_ffn_mu_k"], f32)
        ctl = np.zeros(D, f32)
        ctl[0] = 0.0 if (cix % 4 == 0) else 1.0
        for j in range(8):
            same = (j // 4) == b
            if same and (j % 4) < (cix % 4):
                ctl[1 + j] = 1.0
                ctl[9 + j] = float(CH * ((cix % 4) - 1 - (j % 4)))
            if same:
                ctl[17 + j] = 1.0
        vecs[11] = ctl
        wmx = np.concatenate([w1T[cix * 128:(cix + 1) * 128, :],
                              w2T[cix * 512:(cix + 1) * 512, :].reshape(128, 4096)], axis=1)
        in_maps.append({
            "x_ext": cat(xs.T), "wqkvT": wqkvT, "wfcT": wfcT, "w1T": w1T, "w2T": w2T,
            "vecs": cat(vecs), "b1v": b1v, "wl1T": wl1T, "wmx": cat(wmx),
        })
    return in_maps


def kernel(**inputs):
    dbg = inputs.pop("_dbg", None)
    nc = _get_program(dbg)
    in_maps = _stage_inputs(inputs, dbg)
    res = run_bass_kernel_spmd(nc, in_maps, core_ids=list(range(NCORE)))
    out = np.zeros((B, T, D), np.float32)
    for cix in range(NCORE):
        b = cix // 4
        t0 = (cix % 4) * CH
        out[b, t0:t0 + CH] = res.results[cix]["out_ext"].T
    if dbg:
        kernel._dbg_results = [dict(res.results[cix]) for cix in range(NCORE)]
    return out


# version 15
# speedup vs baseline: 1.1048x; 1.0053x over previous
import numpy as np
import concourse.bass as bass
import concourse.mybir as mybir
from concourse.bass_utils import run_bass_kernel_spmd

F32 = mybir.dt.float32
BF16 = mybir.dt.bfloat16
AF = mybir.ActivationFunctionType
ALU = mybir.AluOpType
AX = mybir.AxisListType

MAGIC = 12582912.0
SM_SCALE = 0.03125
LN_EPS = 1e-5
QP = 127.0
MIN_RANGE = 1e-6

B, T, D, HH = 2, 8192, 1024, 4096
NCORE = 8
CH = 2048
PAD = 128
WQ = PAD + CH
DC = D // 128
HC = HH // 128
QT = WQ // 128
KB = 1024
NKB = T // KB
NTS_W = [(0, 512), (512, 512), (1024, 512), (1536, 512), (2048, 128)]
NTS_OWN = [(128, 512), (640, 512), (1152, 512), (1664, 512)]
NTS_CH = [(0, 512), (512, 512), (1024, 512), (1536, 512)]

ARENA_LO = 28672
ARENA_HI = 229344


class Sem:
    def __init__(self, nc, name, owner):
        self.h = nc.semaphore(name).__enter__()
        self.owner = owner
        self.dbg_name = name
        self.n = 0


class QSem:

    R = 4

    def __init__(self, nc, name, owner):
        self.sems = [Sem(nc, f"{name}{i}", owner) for i in range(self.R)]
        self.owner = owner
        self.dbg_name = name
        self.k = 0

    @property
    def n(self):
        return 16 * self.k

    def expand_wait(self, v):
        k = (v + 15) // 16
        out = []
        for p in range(self.R):
            c = ((k - 1 - p) // self.R + 1) if k > p else 0
            if c > 0:
                out.append((self.sems[p], 16 * c))
        return out

    def next_issue(self):
        p = self.k % self.R
        c = self.k // self.R
        gate = [(self.sems[p], 16 * c)] if c > 0 else []
        self.k += 1
        return gate, self.sems[p]


class Prog:
    ENGINES = ("sync", "scalar", "vector", "tensor", "gpsimd")

    def __init__(self, nc):
        self.nc = nc
        self.ops = {e: [] for e in self.ENGINES}
        self.chain = {}

    def do(self, eng, fn, wait=(), inc=()):
        import traceback
        wait_x = []
        for s, v in wait:
            if v <= 0:
                continue
            if isinstance(s, QSem):
                wait_x.extend(s.expand_wait(v))
            else:
                wait_x.append((s, v))
        inc_x = []
        for s, amt in inc:
            assert s.owner == eng, (s.owner, eng)
            if isinstance(s, QSem):
                assert amt == 16
                gate, tgt = s.next_issue()
                wait_x = _merge_waits(wait_x, gate)
                inc_x.append((tgt, 16))
            else:
                inc_x.append((s, amt))
        csem = self.chain.get(eng)
        if csem is not None:
            wait_x = _merge_waits(wait_x, [(csem, csem.n)])
            if not inc_x:
                inc_x.append((csem, 1))
        wait = _merge_waits(wait_x)
        inc = inc_x

        stk = traceback.extract_stack()
        fr = next((f for f in reversed(stk[:-1]) if f.name not in ("do",)), stk[-2])
        tag = f"{fr.name}:{fr.lineno}"
        is_dma = any(sm.owner in ("sync", "gpsimd") and amt % 16 == 0 and amt >= 16 for sm, amt in inc)

        def run(e):
            for s, v in wait:
                e.wait_ge(s.h, v)
            n0 = self.nc.n_instructions()
            instr = fn(e)
            if is_dma and self.nc.n_instructions() - n0 != 1:
                raise RuntimeError(f"DMA at {tag} lowered to {self.nc.n_instructions() - n0} "
                                   f"instructions; semaphore accounting assumes 1")
            for s, amt in inc:
                instr = instr.then_inc(s.h, amt)

        run.meta = (eng, wait, inc, tag)
        self.ops[eng].append(run)
        for s, amt in inc:
            s.n += amt

    def verify_no_deadlock(self):
        heads = {e: 0 for e in self.ENGINES}
        cur = {}
        total = sum(len(v) for v in self.ops.values())
        done = 0
        progress = True
        while done < total and progress:
            progress = False
            for e in self.ENGINES:
                while heads[e] < len(self.ops[e]):
                    eng, wait, inc, tag = self.ops[e][heads[e]].meta
                    if any(cur.get(id(sm), 0) < v for sm, v in wait):
                        break
                    for sm, amt in inc:
                        cur[id(sm)] = cur.get(id(sm), 0) + amt
                    heads[e] += 1
                    done += 1
                    progress = True
        if done < total:
            lines = []
            for e in self.ENGINES:
                if heads[e] < len(self.ops[e]):
                    eng, wait, inc, tag = self.ops[e][heads[e]].meta
                    blocked = [(getattr(sm, 'dbg_name', '?'), v, cur.get(id(sm), 0)) for sm, v in wait
                               if cur.get(id(sm), 0) < v]
                    lines.append(f"  {e}[{heads[e]}/{len(self.ops[e])}] at {tag} blocked on {blocked}")
            raise RuntimeError("DEADLOCK in planned program:\n" + "\n".join(lines))

    def emit(self):
        with self.nc.Block() as block:
            for eng in self.ENGINES:
                ops = self.ops[eng]

                def body(e, ops=ops):
                    for op in ops:
                        op(e)

                getattr(block, eng)(body)


class Arena:

    def __init__(self, nc):
        self.nc = nc
        self.free_list = [(ARENA_LO, ARENA_HI, [])]
        self.live = {}
        self.uid = 0
        self.peak = 0

    def alloc(self, name, shape, dtype, align=64):
        nbytes = int(np.prod(shape[1:])) * (2 if dtype == BF16 else 4)
        order = sorted(range(len(self.free_list)),
                       key=lambda i: self.free_list[i][1] - self.free_list[i][0])
        for i in order:
            lo, hi, gu = self.free_list[i]
            start = (lo + align - 1) // align * align
            if start + nbytes <= hi:
                segs = []
                if start > lo:
                    segs.append((lo, start, gu))
                if start + nbytes < hi:
                    segs.append((start + nbytes, hi, gu))
                self.free_list[i:i + 1] = segs
                self.uid += 1
                h = self.nc.alloc_sbuf_tensor_at(f"{name}_{self.uid}", list(shape), dtype, offset=start)
                self.live[id(h)] = (start, start + nbytes)
                self.peak = max(self.peak, start + nbytes)
                return h, list(gu)
        raise MemoryError(f"SBUF arena exhausted for {name} {shape}; free={[(a, b) for a, b, _ in self.free_list]}")

    def free(self, h, guard):
        lo, hi = self.live.pop(id(h))
        self.free_list.append((lo, hi, list(guard)))
        self.free_list.sort(key=lambda s: s[0])
        merged = []
        for seg in self.free_list:
            if merged and merged[-1][1] >= seg[0]:
                prev = merged[-1]
                merged[-1] = (prev[0], max(prev[1], seg[1]), prev[2] + seg[2])
            else:
                merged.append(seg)
        self.free_list = merged


def _merge_waits(*wait_lists):
    best = {}
    for wl in wait_lists:
        for s, v in wl:
            best[id(s)] = (s, max(v, best.get(id(s), (s, 0))[1]))
    return list(best.values())


def build_program(dbg=None):
    nc = bass.Bass("TRN2", target_bir_lowering=False)

    x_ext = nc.dram_tensor("x_ext", [D, WQ], F32, kind="ExternalInput").ap()
    wqkvT = nc.dram_tensor("wqkvT", [D, 3 * D], F32, kind="ExternalInput").ap()
    wfcT = nc.dram_tensor("wfcT", [D, D], F32, kind="ExternalInput").ap()
    w1T = nc.dram_tensor("w1T", [D, HH], F32, kind="ExternalInput").ap()
    w2T = nc.dram_tensor("w2T", [HH, D], F32, kind="ExternalInput").ap()
    vecs = nc.dram_tensor("vecs", [16, D], F32, kind="ExternalInput").ap()
    b1v = nc.dram_tensor("b1v", [HH], F32, kind="ExternalInput").ap()
    wl1T = nc.dram_tensor("wl1T", [7, D, D], F32, kind="ExternalInput").ap()
    wmx = nc.dram_tensor("wmx", [128, 8192], F32, kind="ExternalInput").ap()
    out_ext = nc.dram_tensor("out_ext", [D, CH], F32, kind="ExternalOutput").ap()
    dbg_ext = nc.dram_tensor("dbgout", [D, WQ], F32, kind="ExternalOutput").ap() if dbg else None
    dbg2 = nc.dram_tensor("dbg2", [6, D, WQ], F32, kind="ExternalOutput").ap() if dbg == "all" else None
    dbg3 = nc.dram_tensor("dbg3", [128, 128], F32, kind="ExternalOutput").ap() if dbg == "all" else None

    xn_sp = nc.dram_tensor("xn_sp", [D, WQ], F32).ap()
    h_sp = nc.dram_tensor("h_sp", [HH, WQ], F32).ap()
    x1_sp = nc.dram_tensor("x1_sp", [D, WQ], F32).ap()
    k_sp = nc.dram_tensor("k_sp", [D, CH], F32).ap()
    v_sp = nc.dram_tensor("v_sp", [D, CH], F32).ap()
    sr_sp = nc.dram_tensor("sr_sp", [D, CH], F32).ap()
    r2_sp = nc.dram_tensor("r2_sp", [D, CH], F32).ap()
    cc_m1_i = nc.dram_tensor("cc_m1_i", [1, 8], F32).ap()
    cc_m1_o = nc.dram_tensor("cc_m1_o", [8, 8], F32).ap()
    cc_m2_i = nc.dram_tensor("cc_m2_i", [1, 8], F32).ap()
    cc_m2_o = nc.dram_tensor("cc_m2_o", [8, 8], F32).ap()
    cc_vs_i = nc.dram_tensor("cc_vs_i", [1, D], F32).ap()
    cc_vs_o = nc.dram_tensor("cc_vs_o", [8, D], F32).ap()
    cc_wk_i = [nc.dram_tensor(f"cc_wk_i{g}", [1, 256], F32).ap() for g in range(DC)]
    cc_wk_o = [nc.dram_tensor(f"cc_wk_o{g}", [8, 256], F32).ap() for g in range(DC)]
    wkv_dbg = nc.dram_tensor("wkv_dbg", [7, D, CH], F32).ap() if dbg == "wkv" else None

    P = Prog(nc)
    ld = QSem(nc, "ld", "sync")
    gd = QSem(nc, "gd", "gpsimd")
    vs = Sem(nc, "vs", "vector")
    ss = Sem(nc, "ss", "scalar")
    ts = Sem(nc, "ts", "tensor")
    gs = Sem(nc, "gs", "gpsimd")
    cs = Sem(nc, "cs", "gpsimd")
    P.chain["vector"] = vs
    P.chain["gpsimd"] = gs
    P.chain["scalar"] = ss

    A = Arena(nc)

    def alloc(name, shape, dtype):
        h, gu = A.alloc(name, shape, dtype)
        return h, gu

    psA = nc.psum_tensor("psA", [128, 1024], F32).__enter__()
    psB = nc.psum_tensor("psB", [128, 1024], F32).__enter__()
    psC = nc.psum_tensor("psC", [128, 1024], F32).__enter__()
    psD = nc.psum_tensor("psD", [128, 512], F32).__enter__()
    psE = nc.psum_tensor("psE", [128, 512], F32).__enter__()

    idn, _ = alloc("idn", [128, 128], F32)
    idnb, _ = alloc("idnb", [128, 128], BF16)
    ones128b, _ = alloc("ones128b", [128, 128], BF16)
    onescol, _ = alloc("onescol", [1, 128], F32)
    epsc, _ = alloc("epsc", [128, 1], F32)
    magicc, _ = alloc("magicc", [128, 1], F32)
    smc, _ = alloc("smc", [128, 1], F32)
    neghalfc, _ = alloc("neghalfc", [128, 1], F32)
    invdc, _ = alloc("invdc", [128, 1], F32)
    negmagicc, _ = alloc("negmagicc", [128, 1], F32)
    nc.const_aps.register(F32, -MAGIC, negmagicc[:, :])
    nc.const_aps.register(F32, MAGIC, magicc[:, :])
    nc.const_aps.register(F32, SM_SCALE, smc[:, :])
    nc.const_aps.register(F32, -0.5, neghalfc[:, :])

    vcm, _ = alloc("vcm", [128, 16 * 8], F32)
    VEC = {n: i for i, n in enumerate(
        ["bfc", "gamma", "beta", "b2", "td", "tf", "tmk", "tmv", "tmr", "mur", "muk"])}

    def vc(name):
        i = VEC[name]
        return vcm[:, i * 8:(i + 1) * 8]

    b1cm, _ = alloc("b1cm", [128, HC], F32)
    b1ss, _ = alloc("b1ss", [128, HC], F32)
    scl, _ = alloc("scl", [128, 32], F32)
    e2048bc, _ = alloc("e2048bc", [128, 8], F32)
    mjbc, _ = alloc("mjbc", [128, 8], F32)
    mbbc, _ = alloc("mbbc", [128, 8], F32)
    expv, _ = alloc("expv", [128, 32], F32)
    wmaxp, _ = alloc("wmaxp", [128, 2], F32)
    rowt, _ = alloc("rowt", [32, 512], F32)
    rgA = []
    rgB = []
    rgA_busy = [[], [], [], []]
    rgB_busy = [[], [], [], []]
    for i_ in range(4):
        h, _ = alloc(f"rgA{i_}", [128, 512], F32)
        rgA.append(h)
        h, _ = alloc(f"rgB{i_}", [128, 512], F32)
        rgB.append(h)
    rg_ix = [0]

    P.do("gpsimd", lambda e: e.memset(idn[:, :], 0.0), inc=[(gs, 1)])
    P.do("gpsimd", lambda e: e.affine_select(out=idn[:, :], in_=idn[:, :], compare_op=ALU.not_equal,
                                             fill=1.0, base=0, pattern=[[-1, 128]], channel_multiplier=1),
         wait=[(gs, gs.n)], inc=[(gs, 1)])
    P.do("gpsimd", lambda e: e.memset(ones128b[:, :], 1.0))
    P.do("gpsimd", lambda e: e.memset(onescol[:, :], 1.0))
    P.do("gpsimd", lambda e: e.memset(epsc[:, :], LN_EPS))
    P.do("gpsimd", lambda e: e.memset(magicc[:, :], MAGIC))
    P.do("gpsimd", lambda e: e.memset(smc[:, :], SM_SCALE))
    P.do("gpsimd", lambda e: e.memset(neghalfc[:, :], -0.5))
    P.do("gpsimd", lambda e: e.memset(invdc[:, :], 1.0 / D))
    P.do("gpsimd", lambda e: e.memset(negmagicc[:, :], -MAGIC))
    P.do("gpsimd", lambda e: e.memset(wmaxp[:, :], 0.0))
    P.do("gpsimd", lambda e: e.memset(scl[:, :], 0.0))
    P.do("gpsimd", lambda e: e.memset(rowt[:, :], 0.0), inc=[(gs, 1)])
    gs_pro = gs.n
    P.do("vector", lambda e: e.tensor_copy(idnb[:, :], idn[:, :]), wait=[(gs, gs_pro)], inc=[(vs, 1)])

    for i, name in enumerate(VEC):
        P.do("sync", lambda e, i=i: e.dma_start(
            out=rowt[0:8, 0:128],
            in_=vecs.rearrange("v (j p) -> (v j) p", p=128)[i * 8:(i + 1) * 8, :]),
            wait=[(ts, ts.n), (gs, gs_pro)], inc=[(ld, 16)])
        P.do("tensor", lambda e: e.transpose(psD[:, 0:8], rowt[0:8, 0:128], idn[0:8, 0:8]),
             wait=[(ld, ld.n), (vs, vs.n)], inc=[(ts, 1)])
        P.do("vector", lambda e, i=i: e.tensor_copy(vcm[:, i * 8:(i + 1) * 8], psD[:, 0:8]),
             wait=[(ts, ts.n)], inc=[(vs, 1)])
    P.do("sync", lambda e: e.dma_start(out=rowt[0:32, 0:128], in_=b1v.rearrange("(j p) -> j p", p=128)),
         wait=[(ts, ts.n)], inc=[(ld, 16)])
    P.do("tensor", lambda e: e.transpose(psD[:, 0:32], rowt[0:32, 0:128], idn[0:32, 0:32]),
         wait=[(ld, ld.n), (vs, vs.n)], inc=[(ts, 1)])
    P.do("vector", lambda e: e.tensor_copy(b1cm[:, :], psD[:, 0:32]), wait=[(ts, ts.n)], inc=[(vs, 1)])
    P.do("sync", lambda e: e.dma_start(out=rowt[0:1, 128:160], in_=vecs[11:12, 0:32]), inc=[(ld, 16)])
    ld_ctl = ld.n

    def bcast_row(dst, src_row, n, wait):
        P.do("tensor", lambda e: e.matmul(psE[:, 0:n], onescol[0:1, :], src_row, start=True, stop=True),
             wait=_merge_waits(wait, [(gs, gs_pro), (ss, ss.n), (vs, vs.n)]), inc=[(ts, 1)])
        P.do("vector", lambda e: e.tensor_copy(dst, psE[:, 0:n]), wait=[(ts, ts.n)], inc=[(vs, 1)])

    bcast_row(scl[:, 4:5], rowt[0:1, 128:129], 1, [(ld, ld_ctl), (vs, vs.n)])
    bcast_row(mjbc[:, :], rowt[0:1, 129:137], 8, [(vs, vs.n)])
    bcast_row(e2048bc[:, :], rowt[0:1, 137:145], 8, [(vs, vs.n)])
    bcast_row(mbbc[:, :], rowt[0:1, 145:153], 8, [(vs, vs.n)])
    vs_ctl = vs.n

    P.do("scalar", lambda e: e.activation(out=expv[:, 0:8], in_=vc("td"), func=AF.Exp),
         wait=[(vs, vs_ctl)], inc=[(ss, 1)])
    P.do("vector", lambda e: e.tensor_scalar(out=expv[:, 8:16], in0=expv[:, 0:8], scalar1=-1.0,
                                             scalar2=None, op0=ALU.mult), wait=[(ss, ss.n)], inc=[(vs, 1)])
    P.do("vector", lambda e: e.tensor_scalar(out=expv[:, 8:16], in0=expv[:, 8:16], scalar1=-87.0,
                                             scalar2=None, op0=ALU.max), wait=[(vs, vs.n)], inc=[(vs, 1)])
    P.do("scalar", lambda e: e.activation(out=expv[:, 16:24], in_=expv[:, 8:16], func=AF.Exp),
         wait=[(vs, vs.n)], inc=[(ss, 1)])
    P.do("scalar", lambda e: e.activation(out=expv[:, 24:32], in_=vc("tf"), func=AF.Exp), inc=[(ss, 1)])

    wm_first = [True, True]
    for j in range(16):
        col = 0 if j < 8 else 1
        i_ = rg_ix[0] % 4
        rg_ix[0] += 1
        bA = rgA[i_]
        P.do("sync", lambda e, j=j, bA=bA: e.dma_start(out=bA[:, :], in_=wmx[:, j * 512:(j + 1) * 512]),
             wait=_merge_waits(rgA_busy[i_], [(gs, gs_pro)] if j == 0 else []), inc=[(ld, 16)])
        ldn = ld.n
        if wm_first[col]:
            P.do("vector", lambda e, col=col, bA=bA: e.reduce_max(out=wmaxp[:, col:col + 1], in_=bA[:, :],
                                                                  axis=AX.X, apply_absolute_value=True),
                 wait=[(ld, ldn)], inc=[(vs, 1)])
            wm_first[col] = False
        else:
            P.do("vector", lambda e, bA=bA: e.reduce_max(out=scl[:, 11:12], in_=bA[:, :], axis=AX.X,
                                                         apply_absolute_value=True),
                 wait=[(ld, ldn)], inc=[(vs, 1)])
            P.do("vector", lambda e, col=col: e.tensor_tensor(out=wmaxp[:, col:col + 1],
                                                              in0=wmaxp[:, col:col + 1],
                                                              in1=scl[:, 11:12], op=ALU.max), inc=[(vs, 1)])
        rgA_busy[i_] = [(vs, vs.n)]
    P.do("tensor", lambda e: e.transpose(psE[0:2, 0:128], wmaxp[:, 0:2], idn[:, :]),
         wait=[(vs, vs.n), (ss, ss.n)], inc=[(ts, 1)])
    P.do("vector", lambda e: e.reduce_max(out=rowt[0:2, 152:153], in_=psE[0:2, 0:128], axis=AX.X),
         wait=[(ts, ts.n)], inc=[(vs, 1)])
    P.do("tensor", lambda e: e.transpose(psE[0:1, 0:2], rowt[0:2, 152:153], idn[0:2, 0:2]),
         wait=[(vs, vs.n)], inc=[(ts, 1)])
    P.do("vector", lambda e: e.tensor_copy(scl[0:1, 13:15], psE[0:1, 0:2]), wait=[(ts, ts.n)], inc=[(vs, 1)])
    vs_wmx = vs.n

    x_fm = []
    xfm_gu = []
    for g in range(DC):
        h, gu = alloc(f"x_fm{g}", [128, WQ], F32)
        x_fm.append(h)
        xfm_gu += gu
    for g in range(DC):
        P.do("sync", lambda e, g=g: e.dma_start(out=x_fm[g][:, :], in_=x_ext[g * 128:(g + 1) * 128, :]),
             wait=(_merge_waits(xfm_gu) if g == 0 else []), inc=[(ld, 16)])
    ld_xfm = ld.n

    def dump2(slot, chunks, wait, width=WQ):
        if dbg != "all":
            return
        for g in range(DC):
            P.do("sync", lambda e, g=g: e.dma_start(
                out=dbg2[slot, g * 128:(g + 1) * 128, 0:width], in_=chunks[g][:, 0:width]),
                wait=wait if g == 0 else [], inc=[(ld, 16)])

    ln_sc, _ = alloc("ln_sc", [128, 512], F32)
    castp = []
    for i_ in range(2):
        h0_, _ = alloc(f"cast0_{i_}", [128, 512], BF16)
        h1_, _ = alloc(f"cast1_{i_}", [128, 512], BF16)
        castp.append((h0_, h1_))
    cast_busy = [0, 0]

    def emit_ln(src, mu_t, rstd_t, first_wait, width_tiles=NTS_W):
        for ti, (c0, w) in enumerate(width_tiles):
            for g in range(DC):
                p_ = g % 2
                c0b, c1b = castp[p_]
                wcast = _merge_waits(list(first_wait) if (ti == 0 and g == 0) else [],
                                     [(ts, cast_busy[p_])])
                P.do("vector", lambda e, g=g, c0=c0, w=w, c0b=c0b: e.tensor_copy(
                    c0b[:, 0:w], src[g][:, c0:c0 + w]), wait=wcast, inc=[(vs, 1)])
                P.do("vector", lambda e, g=g, c0=c0, w=w, c1b=c1b: e.tensor_tensor(
                    out=c1b[:, 0:w], in0=src[g][:, c0:c0 + w], in1=src[g][:, c0:c0 + w], op=ALU.mult),
                    inc=[(vs, 1)])
                vsn = vs.n
                P.do("tensor", lambda e, g=g, w=w, c0b=c0b: e.matmul(psD[:, 0:w], ones128b[:, :], c0b[:, 0:w],
                                                                     start=(g == 0), stop=(g == DC - 1)),
                     wait=[(vs, vsn)])
                P.do("tensor", lambda e, g=g, w=w, c1b=c1b: e.matmul(psE[:, 0:w], ones128b[:, :], c1b[:, 0:w],
                                                                     start=(g == 0), stop=(g == DC - 1)),
                     inc=[(ts, 1)])
                cast_busy[p_] = ts.n
            tsn = ts.n
            P.do("vector", lambda e, c0=c0, w=w: e.tensor_scalar(
                out=mu_t[:, c0:c0 + w], in0=psD[:, 0:w], scalar1=invdc[:, 0:1], scalar2=None, op0=ALU.mult),
                wait=[(ts, tsn)], inc=[(vs, 1)])
            P.do("vector", lambda e, c0=c0, w=w: e.tensor_scalar(
                out=ln_sc[:, 0:w], in0=psE[:, 0:w], scalar1=invdc[:, 0:1], scalar2=None, op0=ALU.mult),
                wait=[(ss, ss.n)], inc=[(vs, 1)])
            P.do("vector", lambda e, c0=c0, w=w: e.tensor_tensor(
                out=rstd_t[:, c0:c0 + w], in0=mu_t[:, c0:c0 + w], in1=mu_t[:, c0:c0 + w], op=ALU.mult),
                inc=[(vs, 1)])
            P.do("vector", lambda e, c0=c0, w=w: e.tensor_tensor(
                out=ln_sc[:, 0:w], in0=ln_sc[:, 0:w], in1=rstd_t[:, c0:c0 + w], op=ALU.subtract),
                inc=[(vs, 1)])
            vsn = vs.n
            P.do("scalar", lambda e, c0=c0, w=w: e.activation(out=ln_sc[:, 0:w], in_=ln_sc[:, 0:w],
                                                              func=AF.Ln, bias=epsc[:, 0:1], scale=1.0),
                 wait=[(vs, vsn)], inc=[(ss, 1)])
            P.do("scalar", lambda e, c0=c0, w=w: e.activation(out=rstd_t[:, c0:c0 + w], in_=ln_sc[:, 0:w],
                                                              func=AF.Exp, scale=-0.5), inc=[(ss, 1)])
            P.do("tensor", lambda e: e.nop(), wait=[(vs, vsn)])

    def emit_ln_norm(src, dst, dst_guard, mu_t, rstd_t, gamma=None, beta=None):
        ss_stats = ss.n
        for g in range(DC):
            P.do("vector", lambda e, g=g: e.tensor_tensor(
                out=dst[g][:, :], in0=src[g][:, :], in1=mu_t[:, :], op=ALU.subtract),
                wait=([(ss, ss_stats)] + dst_guard) if g == 0 else [], inc=[(vs, 1)])
            P.do("vector", lambda e, g=g: e.tensor_tensor(
                out=dst[g][:, :], in0=dst[g][:, :], in1=rstd_t[:, :], op=ALU.mult), inc=[(vs, 1)])
            if gamma is not None:
                P.do("vector", lambda e, g=g: e.tensor_scalar(
                    out=dst[g][:, :], in0=dst[g][:, :], scalar1=gamma[:, g:g + 1], scalar2=beta[:, g:g + 1],
                    op0=ALU.mult, op1=ALU.add), inc=[(vs, 1)])

    mu_t, mu_gu = alloc("mu_t", [128, WQ], F32)
    rstd_t, rs_gu = alloc("rstd_t", [128, WQ], F32)
    emit_ln(x_fm, mu_t, rstd_t, _merge_waits([(ld, ld_xfm)], mu_gu, rs_gu))
    xn = []
    xn_gu = []
    for g in range(DC):
        h, gu = alloc(f"xn{g}", [128, WQ], F32)
        xn.append(h)
        xn_gu += gu
    emit_ln_norm(x_fm, xn, xn_gu, mu_t, rstd_t)
    gs_xn = vs.n
    for g in range(DC):
        A.free(x_fm[g], [(vs, gs_xn)])
    A.free(mu_t, [(vs, gs_xn)])
    A.free(rstd_t, [(vs, gs_xn)])

    dump2(0, xn, [(vs, gs_xn)])
    def absmax_chunks(chunks, cslice, dst_acc, dst_tmp, first_wait):
        for g in range(DC):
            if g == 0:
                P.do("vector", lambda e, g=g: e.reduce_max(out=dst_acc, in_=chunks[g][:, cslice[0]:cslice[1]],
                                                           axis=AX.X, apply_absolute_value=True),
                     wait=first_wait, inc=[(vs, 1)])
            else:
                P.do("vector", lambda e, g=g: e.reduce_max(out=dst_tmp, in_=chunks[g][:, cslice[0]:cslice[1]],
                                                           axis=AX.X, apply_absolute_value=True), inc=[(vs, 1)])
                P.do("vector", lambda e: e.tensor_tensor(out=dst_acc, in0=dst_acc, in1=dst_tmp, op=ALU.max),
                     inc=[(vs, 1)])

    absmax_chunks(xn, (PAD, WQ), scl[:, 10:11], scl[:, 11:12], [(vs, gs_xn)])
    P.do("tensor", lambda e: e.transpose(psE[0:1, 0:128], scl[:, 10:11], idn[:, :]),
         wait=[(vs, vs.n), (ss, ss.n)], inc=[(ts, 1)])
    P.do("vector", lambda e: e.reduce_max(out=scl[0:1, 12:13], in_=psE[0:1, 0:128], axis=AX.X),
         wait=[(ts, ts.n)], inc=[(vs, 1)])
    vs_m1 = vs.n
    P.do("gpsimd", lambda e: e.dma_start(out=cc_m1_i[:, 0:8], in_=scl[0:1, 12:20]),
         wait=[(vs, vs_m1)], inc=[(gd, 16)])
    P.do("gpsimd", lambda e: e.collective_compute(
        "AllGather", ALU.bypass, replica_groups=[list(range(8))],
        ins=[cc_m1_i.opt()], outs=[cc_m1_o.opt()]), wait=[(gd, gd.n)], inc=[(cs, 1)])
    cs_m1 = cs.n

    def load_wT_bf16(dram_rows_cols, dst, first_wait=()):
        for kk in range(DC):
            for cq in range(0, D, 512):
                i_ = rg_ix[0] % 4
                rg_ix[0] += 1
                bB = rgB[i_]
                P.do("sync", lambda e, kk=kk, cq=cq, bB=bB: e.dma_start(
                    out=bB[:, :], in_=dram_rows_cols[kk * 128:(kk + 1) * 128, cq:cq + 512]),
                    wait=_merge_waits(rgB_busy[i_],
                                      list(first_wait) if (kk == 0 and cq == 0) else []), inc=[(ld, 16)])
                P.do("scalar", lambda e, kk=kk, cq=cq, dst=dst, bB=bB: e.activation(
                    out=dst[kk][:, cq:cq + 512], in_=bB[:, :], func=AF.Copy),
                    wait=[(ld, ld.n)], inc=[(ss, 1)])
                rgB_busy[i_] = [(ss, ss.n)]
        return ss.n

    wvb = []
    wv_gu = []
    for k in range(DC):
        h, gu = alloc(f"wvb{k}", [128, D], BF16)
        wvb.append(h)
        wv_gu += gu
    ss_wvb = load_wT_bf16(wqkvT[:, 2 * D:3 * D], wvb, first_wait=wv_gu)

    xsl = []
    xsl_gu = []
    for k in range(DC):
        h, gu = alloc(f"xsl{k}", [128, 512], BF16)
        xsl.append(h)
        xsl_gu += gu
    vsum_cm, vsg_gu0 = alloc("vsum_cm", [128, 16], F32)
    P.do("vector", lambda e: e.memset(vsum_cm[:, :], 0.0), wait=_merge_waits(vsg_gu0), inc=[(vs, 1)])
    vs_first = [True]
    for ni, (c0, w) in enumerate(NTS_OWN):
        for kk in range(DC):
            P.do("vector", lambda e, kk=kk, c0=c0, w=w: e.tensor_copy(xsl[kk][:, 0:w], xn[kk][:, c0:c0 + w]),
                 wait=(_merge_waits([(ss, ss_wvb), (vs, gs_xn)], xsl_gu) if (ni == 0 and kk == 0) else ([(ts, ts.n)] if kk == 0 else [])),
                 inc=[(vs, 1)] if kk == DC - 1 else [])
        vs_sl = vs.n
        for m in range(DC):
            pb = psD if m % 2 == 0 else psE
            for kk in range(DC):
                P.do("tensor", lambda e, kk=kk, m=m, w=w, pb=pb: e.matmul(
                    pb[:, 0:w], wvb[kk][:, m * 128:(m + 1) * 128], xsl[kk][:, 0:w],
                    start=(kk == 0), stop=(kk == DC - 1)),
                    wait=[(vs, vs_sl), (vs, vs.n)] if kk == 0 else [],
                    inc=[(ts, 1)] if kk == DC - 1 else [])
            if vs_first[0]:
                P.do("vector", lambda e, m=m, w=w, pb=pb: e.reduce_sum(
                    out=vsum_cm[:, m:m + 1], in_=pb[:, 0:w], axis=AX.X), wait=[(ts, ts.n)], inc=[(vs, 1)])
            else:
                P.do("vector", lambda e, m=m, w=w, pb=pb: e.reduce_sum(
                    out=vsum_cm[:, 8:9], in_=pb[:, 0:w], axis=AX.X), wait=[(ts, ts.n)], inc=[(vs, 1)])
                P.do("vector", lambda e, m=m: e.tensor_tensor(
                    out=vsum_cm[:, m:m + 1], in0=vsum_cm[:, m:m + 1], in1=vsum_cm[:, 8:9], op=ALU.add),
                    wait=[(vs, vs.n)], inc=[(vs, 1)])
        vs_first[0] = False
    vs_vsum = vs.n
    for g in range(DC):
        P.do("gpsimd", lambda e, g=g: e.dma_start(out=cc_vs_i[0:1, g * 128:(g + 1) * 128],
                                                  in_=vsum_cm[:, g:g + 1]),
             wait=[(vs, vs_vsum)] if g == 0 else [], inc=[(gd, 16)])
    P.do("gpsimd", lambda e: e.collective_compute(
        "AllGather", ALU.bypass, replica_groups=[list(range(8))],
        ins=[cc_vs_i.opt()], outs=[cc_vs_o.opt()]), wait=[(gd, gd.n)], inc=[(cs, 1)])
    cs_vs = cs.n
    ts_kv_proj = ts.n
    for k in range(DC):
        A.free(wvb[k], [(ts, ts_kv_proj)])

    P.do("sync", lambda e: e.dma_start(out=rowt[0:8, 160:168], in_=cc_m1_o[:, :]),
         wait=[(cs, cs_m1)], inc=[(ld, 16)])
    P.do("tensor", lambda e: e.transpose(psE[0:8, 0:8], rowt[0:8, 160:168], idn[0:8, 0:8]),
         wait=[(ld, ld.n), (vs, vs.n), (ss, ss.n)], inc=[(ts, 1)])
    P.do("vector", lambda e: e.reduce_max(out=rowt[0:3, 155:156], in_=psE[0:3, 0:8], axis=AX.X),
         wait=[(ts, ts.n)], inc=[(vs, 1)])
    P.do("tensor", lambda e: e.transpose(psE[0:1, 0:3], rowt[0:3, 155:156], idn[0:3, 0:3]),
         wait=[(vs, vs.n)], inc=[(ts, 1)])
    P.do("vector", lambda e: e.tensor_copy(rowt[0:1, 152:155], psE[0:1, 0:3]), wait=[(ts, ts.n)],
         inc=[(vs, 1)])
    P.do("vector", lambda e: e.tensor_scalar(out=rowt[0:1, 156:157], in0=rowt[0:1, 155:156],
                                             scalar1=MIN_RANGE, scalar2=1.0 / QP, op0=ALU.max, op1=ALU.mult),
         inc=[(vs, 1)])
    P.do("vector", lambda e: e.reciprocal(out=rowt[0:1, 157:158], in_=rowt[0:1, 156:157]), inc=[(vs, 1)])
    P.do("vector", lambda e: e.tensor_scalar(out=rowt[0:1, 158:159], in0=rowt[0:1, 153:154],
                                             scalar1=MIN_RANGE, scalar2=1.0 / QP, op0=ALU.max, op1=ALU.mult),
         inc=[(vs, 1)])
    P.do("vector", lambda e: e.reciprocal(out=rowt[0:1, 159:160], in_=rowt[0:1, 158:159]), inc=[(vs, 1)])
    P.do("vector", lambda e: e.tensor_tensor(out=rowt[0:1, 168:169], in0=rowt[0:1, 158:159],
                                             in1=rowt[0:1, 156:157], op=ALU.mult), inc=[(vs, 1)])
    vs_s1e = vs.n
    bcast_row(scl[:, 13:15], rowt[0:1, 156:158], 2, [(vs, vs_s1e)])
    bcast_row(scl[:, 15:17], rowt[0:1, 158:160], 2, [(vs, vs.n)])
    bcast_row(scl[:, 17:18], rowt[0:1, 168:169], 1, [(vs, vs.n)])
    vs_s1bc = vs.n
    P.do("vector", lambda e: e.tensor_scalar(out=b1ss[:, :], in0=b1cm[:, :], scalar1=scl[:, 17:18],
                                             scalar2=None, op0=ALU.mult), wait=[(vs, vs_s1bc)], inc=[(vs, 1)])
    vs_ss1 = vs.n

    w1qb = []
    w1_gu = []
    for k in range(DC):
        h, gu = alloc(f"w1qb{k}", [128, HH], BF16)
        w1qb.append(h)
        w1_gu += gu
    for kk in range(DC):
        for c in range(0, HH, 512):
            i_ = rg_ix[0] % 4
            rg_ix[0] += 1
            bB, bA = rgB[i_], rgA[i_]
            P.do("sync", lambda e, kk=kk, c=c, bB=bB: e.dma_start(
                out=bB[:, :], in_=w1T[kk * 128:(kk + 1) * 128, c:c + 512]),
                wait=rgB_busy[i_] + (w1_gu + [(vs, vs_ss1)] if (kk == 0 and c == 0) else []),
                inc=[(ld, 16)])
            P.do("scalar", lambda e, bB=bB, bA=bA: e.activation(out=bA[:, :], in_=bB[:, :], func=AF.Identity,
                                                                bias=MAGIC, scale=scl[:, 16:17]),
                 wait=_merge_waits([(ld, ld.n)], rgA_busy[i_]), inc=[(ss, 1)])
            rgB_busy[i_] = [(ss, ss.n)]
            P.do("vector", lambda e, kk=kk, c=c, bA=bA: e.tensor_scalar(
                out=w1qb[kk][:, c:c + 512], in0=bA[:, :], scalar1=-MAGIC, scalar2=None, op0=ALU.add),
                wait=[(ss, ss.n)], inc=[(vs, 1)])
            rgA_busy[i_] = [(vs, vs.n)]
    vs_w1q = vs.n

    h_tile = []
    ht_gu = []
    for i_ in range(4):
        h, gu = alloc(f"ht{i_}", [128, 512], F32)
        h_tile.append(h)
        ht_gu += gu
    h_busy = [0] * 4
    h_vbusy = [0] * 4
    m2_first = [True]
    bank_ss1 = [0, 0, 0, 0]
    xslB = []
    xslB_gu = []
    for k in range(DC):
        h, gu = alloc(f"xslB{k}", [128, 512], BF16)
        xslB.append(h)
        xslB_gu += gu
    xsl2 = [xsl, xslB]
    xsl2_busy = [[(ts, ts_kv_proj)] + xslB_gu, list(xslB_gu)]
    vs_feed = [0] * len(NTS_W)

    def m1_feed(t):
        c0, w = NTS_W[t]
        dst = xsl2[t % 2]
        for kk in range(DC):
            i_ = rg_ix[0] % 4
            rg_ix[0] += 1
            bA = rgA[i_]
            P.do("scalar", lambda e, kk=kk, c0=c0, w=w, bA=bA: e.activation(
                out=bA[:, 0:w], in_=xn[kk][:, c0:c0 + w], func=AF.Identity, bias=MAGIC,
                scale=scl[:, 14:15]),
                wait=_merge_waits(rgA_busy[i_],
                                  ([(vs, vs_s1bc)] if t == 0 else []) if kk == 0 else []),
                inc=[(ss, 1)])
            P.do("vector", lambda e, kk=kk, w=w, bA=bA, dst=dst: e.tensor_scalar(
                out=dst[kk][:, 0:w], in0=bA[:, 0:w], scalar1=-MAGIC, scalar2=None, op0=ALU.add),
                wait=_merge_waits([(ss, ss.n)], xsl2_busy[t % 2] if kk == 0 else []), inc=[(vs, 1)])
            rgA_busy[i_] = [(vs, vs.n)]
        vs_feed[t] = vs.n

    m1_feed(0)
    for ni, (c0, w) in enumerate(NTS_W):
        if ni + 1 < len(NTS_W):
            m1_feed(ni + 1)
        rhs = xsl2[ni % 2]
        for m in range(HC):
            pb, cof = [(psD, 0), (psE, 0), (psA, 0), (psA, 512)][m % 4]
            for kk in range(DC):
                P.do("tensor", lambda e, kk=kk, m=m, w=w, pb=pb, cof=cof, rhs=rhs: e.matmul(
                    pb[:, cof:cof + w], w1qb[kk][:, m * 128:(m + 1) * 128], rhs[kk][:, 0:w],
                    start=(kk == 0), stop=(kk == DC - 1)),
                    wait=(_merge_waits([(ss, bank_ss1[m % 4]), (vs, vs_feed[ni])],
                                       [(vs, vs_w1q)] if (ni == 0 and m < 4) else [])
                          if kk == 0 else []),
                    inc=[(ts, 1)] if kk == DC - 1 else [])
            tile = h_tile[m % 4]
            P.do("scalar", lambda e, m=m, w=w, pb=pb, cof=cof, tile=tile: e.activation(
                out=tile[:, 0:w], in_=pb[:, cof:cof + w], func=AF.Gelu, bias=b1ss[:, m:m + 1],
                scale=scl[:, 17:18]),
                wait=_merge_waits([(ts, ts.n), (gd, h_busy[m % 4]), (vs, h_vbusy[m % 4])],
                                  ht_gu if (ni == 0 and m < 4) else []), inc=[(ss, 1)])
            ssn = ss.n
            bank_ss1[m % 4] = ssn
            o0 = max(PAD - c0, 0)
            if o0 < w:
                if m2_first[0]:
                    P.do("vector", lambda e, w=w, o0=o0, tile=tile: e.reduce_max(
                        out=scl[:, 18:19], in_=tile[:, o0:w], axis=AX.X, apply_absolute_value=True),
                        wait=[(ss, ssn)], inc=[(vs, 1)])
                    m2_first[0] = False
                else:
                    P.do("vector", lambda e, w=w, o0=o0, tile=tile: e.reduce_max(
                        out=scl[:, 19:20], in_=tile[:, o0:w], axis=AX.X, apply_absolute_value=True),
                        wait=[(ss, ssn)], inc=[(vs, 1)])
                    P.do("vector", lambda e: e.tensor_tensor(out=scl[:, 18:19], in0=scl[:, 18:19],
                                                             in1=scl[:, 19:20], op=ALU.max), inc=[(vs, 1)])
            P.do("gpsimd", lambda e, m=m, c0=c0, w=w, tile=tile: e.dma_start(
                out=h_sp[m * 128:(m + 1) * 128, c0:c0 + w], in_=tile[:, 0:w]),
                wait=[(ss, ssn)], inc=[(gd, 16)])
            h_busy[m % 4] = gd.n
            h_vbusy[m % 4] = vs.n
        xsl2_busy[ni % 2] = [(ts, ts.n)]
    gd_hsp = gd.n
    ts_m1 = ts.n
    for k in range(DC):
        A.free(w1qb[k], [(ts, ts_m1)])
        A.free(xslB[k], [(ts, ts_m1)])
    for i_ in range(4):
        A.free(h_tile[i_], [(gd, h_busy[i_]), (vs, h_vbusy[i_])])

    P.do("tensor", lambda e: e.transpose(psE[0:1, 0:128], scl[:, 18:19], idn[:, :]),
         wait=[(vs, vs.n), (ss, ss.n)], inc=[(ts, 1)])
    P.do("vector", lambda e: e.reduce_max(out=scl[0:1, 19:20], in_=psE[0:1, 0:128], axis=AX.X),
         wait=[(ts, ts.n)], inc=[(vs, 1)])
    vs_m2 = vs.n
    P.do("gpsimd", lambda e: e.dma_start(out=cc_m2_i[:, 0:8], in_=scl[0:1, 19:27]),
         wait=[(vs, vs_m2)], inc=[(gd, 16)])
    P.do("gpsimd", lambda e: e.collective_compute(
        "AllGather", ALU.bypass, replica_groups=[list(range(8))],
        ins=[cc_m2_i.opt()], outs=[cc_m2_o.opt()]), wait=[(gd, gd.n)], inc=[(cs, 1)])
    cs_m2 = cs.n

    P.do("sync", lambda e: e.dma_start(out=rowt[0:8, 160:168], in_=cc_m2_o[:, :]),
         wait=[(cs, cs_m2)], inc=[(ld, 16)])
    P.do("tensor", lambda e: e.transpose(psE[0:8, 0:8], rowt[0:8, 160:168], idn[0:8, 0:8]),
         wait=[(ld, ld.n), (vs, vs.n), (ss, ss.n)], inc=[(ts, 1)])
    P.do("vector", lambda e: e.reduce_max(out=rowt[0:1, 169:170], in_=psE[0:1, 0:8], axis=AX.X),
         wait=[(ts, ts.n)], inc=[(vs, 1)])
    P.do("vector", lambda e: e.tensor_scalar(out=rowt[0:1, 170:171], in0=rowt[0:1, 169:170],
                                             scalar1=MIN_RANGE, scalar2=1.0 / QP, op0=ALU.max, op1=ALU.mult),
         inc=[(vs, 1)])
    P.do("vector", lambda e: e.reciprocal(out=rowt[0:1, 171:172], in_=rowt[0:1, 170:171]), inc=[(vs, 1)])
    P.do("vector", lambda e: e.tensor_scalar(out=rowt[0:1, 172:173], in0=rowt[0:1, 154:155],
                                             scalar1=MIN_RANGE, scalar2=1.0 / QP, op0=ALU.max, op1=ALU.mult),
         inc=[(vs, 1)])
    P.do("vector", lambda e: e.reciprocal(out=rowt[0:1, 173:174], in_=rowt[0:1, 172:173]), inc=[(vs, 1)])
    P.do("vector", lambda e: e.tensor_tensor(out=rowt[0:1, 174:175], in0=rowt[0:1, 172:173],
                                             in1=rowt[0:1, 170:171], op=ALU.mult), inc=[(vs, 1)])
    vs_s2 = vs.n
    bcast_row(scl[:, 20:25], rowt[0:1, 170:175], 5, [(vs, vs_s2)])
    vs_ss2 = vs.n
    P.do("vector", lambda e: e.tensor_scalar(out=vcm[:, 88:96], in0=vc("b2"), scalar1=scl[:, 24:25],
                                             scalar2=None, op0=ALU.mult), wait=[(vs, vs_ss2)], inc=[(vs, 1)])
    vs_b2ss = vs.n

    for k in range(DC):
        A.free(xsl[k], [(ts, ts_m1)])
    w2qb = []
    w2_gu = []
    for k in range(HC):
        h, gu = alloc(f"w2qb{k}", [128, D], BF16)
        w2qb.append(h)
        w2_gu += gu
    for kk in range(HC):
        for cq in range(0, D, 512):
            i_ = rg_ix[0] % 4
            rg_ix[0] += 1
            bB, bA = rgB[i_], rgA[i_]
            P.do("sync", lambda e, kk=kk, cq=cq, bB=bB: e.dma_start(
                out=bB[:, :], in_=w2T[kk * 128:(kk + 1) * 128, cq:cq + 512]),
                wait=rgB_busy[i_] + (_merge_waits(w2_gu, [(vs, vs_ss2)]) if (kk == 0 and cq == 0) else []),
                inc=[(ld, 16)])
            P.do("scalar", lambda e, bB=bB, bA=bA: e.activation(out=bA[:, :], in_=bB[:, :], func=AF.Identity,
                                                                bias=MAGIC, scale=scl[:, 23:24]),
                 wait=_merge_waits([(ld, ld.n)], rgA_busy[i_]), inc=[(ss, 1)])
            rgB_busy[i_] = [(ss, ss.n)]
            P.do("scalar", lambda e, kk=kk, cq=cq, bA=bA: e.activation(
                out=w2qb[kk][:, cq:cq + 512], in_=bA[:, :], func=AF.Identity, bias=-MAGIC, scale=1.0),
                inc=[(ss, 1)])
            rgA_busy[i_] = [(ss, ss.n)]
    vs_w2q = ss.n

    vsg, vsg_gu = alloc("vsg", [8, D], F32)
    P.do("sync", lambda e: e.dma_start(out=vsg[0:8, :], in_=cc_vs_o[:, :]),
         wait=_merge_waits(vsg_gu, [(cs, cs_vs)]), inc=[(ld, 16)])
    ld_vsg = ld.n
    vstot, vst_gu = alloc("vstot", [128, 8], F32)
    for g in range(DC):
        P.do("tensor", lambda e, g=g: e.transpose(psE[0:128, 0:8], vsg[0:8, g * 128:(g + 1) * 128],
                                                  idn[0:8, 0:8]),
             wait=[(ld, ld_vsg), (vs, vs.n), (ss, ss.n)], inc=[(ts, 1)])
        P.do("vector", lambda e, g=g: e.tensor_tensor(out=vsum_cm[:, 8:16], in0=psE[0:128, 0:8],
                                                      in1=mbbc[:, :], op=ALU.mult),
             wait=[(ts, ts.n)], inc=[(vs, 1)])
        P.do("vector", lambda e, g=g: e.reduce_sum(out=vstot[:, g:g + 1], in_=vsum_cm[:, 8:16], axis=AX.X),
             wait=(_merge_waits(vst_gu) if g == 0 else []), inc=[(vs, 1)])
    vs_vstot = vs.n
    for m in range(DC):
        for kk in range(DC):
            i_ = rg_ix[0] % 4
            rg_ix[0] += 1
            bB = rgB[i_]
            P.do("sync", lambda e, m=m, kk=kk, bB=bB: e.dma_start(
                out=bB[:, 0:128], in_=wfcT[kk * 128:(kk + 1) * 128, m * 128:(m + 1) * 128]),
                wait=rgB_busy[i_], inc=[(ld, 16)])
            ldw = ld.n
            P.do("tensor", lambda e, m=m, kk=kk, bB=bB: e.matmul(
                psD[:, m:m + 1], bB[:, 0:128], vstot[:, kk:kk + 1],
                start=(kk == 0), stop=(kk == DC - 1)),
                wait=_merge_waits([(ld, ldw)],
                                  [(vs, vs_vstot), (ss, ss.n)] if (m == 0 and kk == 0) else []),
                inc=[(ts, 1)])
            rgB_busy[i_] = [(ts, ts.n)]
    y2cm, y2_gu = alloc("y2cm", [128, 8], F32)
    P.do("vector", lambda e: e.tensor_tensor(out=y2cm[:, :], in0=psD[:, 0:8], in1=vc("bfc"), op=ALU.add),
         wait=_merge_waits([(ts, ts.n)], y2_gu), inc=[(vs, 1)])
    vs_y2 = vs.n
    ts_y2 = ts.n
    A.free(vsg, [(ts, ts_y2)])

    if dbg == "xn":
        for g in range(DC):
            P.do("sync", lambda e, g=g: e.dma_start(out=dbg_ext[g * 128:(g + 1) * 128, :], in_=xn[g][:, :]),
                 wait=[(vs, vs.n), (ss, ss.n)], inc=[(ld, 16)])
    dbg_ld0 = ld.n

    for g in range(DC):
        P.do("vector", lambda e, g=g: e.tensor_scalar(
            out=xn[g][:, :], in0=xn[g][:, :], scalar1=y2cm[:, g:g + 1], scalar2=None, op0=ALU.add),
            wait=[(vs, vs_y2), (ss, ss.n), (ld, dbg_ld0)] if g == 0 else [], inc=[(vs, 1)])
    vs_h0 = vs.n
    mu_t, mu_gu = A.alloc("mu2_t", [128, WQ], F32)
    rstd_t, rs_gu = A.alloc("rstd2_t", [128, WQ], F32)
    emit_ln(xn, mu_t, rstd_t, _merge_waits([(vs, vs_h0)], mu_gu, rs_gu))
    vs_ln2st = vs.n
    emit_ln_norm(xn, xn, [], mu_t, rstd_t, gamma=vc("gamma"), beta=vc("beta"))
    gs_attn0 = vs.n
    A.free(mu_t, [(vs, gs_attn0)])
    A.free(rstd_t, [(vs, gs_attn0)])
    if dbg == "attn0":
        for g in range(DC):
            P.do("sync", lambda e, g=g: e.dma_start(out=dbg_ext[g * 128:(g + 1) * 128, :], in_=xn[g][:, :]),
                 wait=[(vs, gs_attn0)], inc=[(ld, 16)])
    dbg_ld = ld.n

    dump2(1, xn, [(vs, gs_attn0)])
    ld_dmp1 = ld.n
    x1 = xn
    xrt = []
    xrt_gu = []
    for i_ in range(2):
        h, gu = alloc(f"xrt{i_}", [128, WQ], F32)
        xrt.append(h)
        xrt_gu += gu
    xrt_busy = [0, 0]
    for g in range(DC):
        buf = xrt[g % 2]
        P.do("sync", lambda e, g=g, buf=buf: e.dma_start(out=buf[:, :], in_=x_ext[g * 128:(g + 1) * 128, :]),
             wait=_merge_waits(xrt_gu if g < 2 else [], [(vs, xrt_busy[g % 2])]), inc=[(ld, 16)])
        ldx = ld.n
        P.do("vector", lambda e, g=g, buf=buf: e.tensor_tensor(out=x1[g][:, :], in0=x1[g][:, :],
                                                               in1=buf[:, :], op=ALU.add),
             wait=_merge_waits([(ld, ldx)], [(vs, gs_attn0), (ld, ld_dmp1)] if g == 0 else []),
             inc=[(vs, 1)])
        xrt_busy[g % 2] = vs.n
    gd_xacc = vs.n
    for i_ in range(2):
        A.free(xrt[i_], [(vs, gd_xacc)])
    vs_x1a = vs.n
    gs_attn0_ctx = gs_attn0
    dump2(2, x1, [(vs, gd_xacc)])
    if dbg == "all":
        P.do("sync", lambda e: e.dma_start(out=dbg3[:, 0:32], in_=scl[:, 0:32]),
             wait=[(vs, vs.n), (ss, ss.n)], inc=[(ld, 16)])
        P.do("sync", lambda e: e.dma_start(out=dbg3[:, 32:48], in_=vsum_cm[:, 0:16]), inc=[(ld, 16)])
        P.do("sync", lambda e: e.dma_start(out=dbg3[:, 48:56], in_=y2cm[:, 0:8]), inc=[(ld, 16)])
        P.do("sync", lambda e: e.dma_start(out=dbg3[:, 56:88], in_=expv[:, 0:32]), inc=[(ld, 16)])
        P.do("sync", lambda e: e.dma_start(out=dbg3[:, 88:96], in_=vstot[:, 0:8]), inc=[(ld, 16)])
        P.do("sync", lambda e: e.dma_start(out=dbg3[:, 96:104], in_=mbbc[:, 0:8]), inc=[(ld, 16)])
        P.do("sync", lambda e: e.dma_start(out=dbg3[:, 104:112], in_=b1ss[:, 0:8]), inc=[(ld, 16)])


    ctx = dict(
        nc=nc, P=P, A=A, alloc=alloc, dbg=dbg, dbg_ext=dbg_ext,

        ld=ld, gd=gd, vs=vs, ss=ss, ts=ts, gs=gs, cs=cs,
        psA=psA, psB=psB, psC=psC, psD=psD, psE=psE,
        idn=idn, idnb=idnb, ones128b=ones128b, onescol=onescol,
        epsc=epsc, scl=scl, vcm=vcm, vc=vc, b1cm=b1cm, b1ss=b1ss, e2048bc=e2048bc, mjbc=mjbc, expv=expv,
        rowt=rowt, xsl=xsl,
        ln_sc=ln_sc, castp=castp, rgA=rgA, rgB=rgB, rgA_busy=rgA_busy, rgB_busy=rgB_busy, rg_ix=rg_ix,
        emit_ln=emit_ln, emit_ln_norm=emit_ln_norm, absmax_chunks=absmax_chunks,
        bcast_row=bcast_row, load_wT_bf16=load_wT_bf16,
        wfcT=wfcT, w2T=w2T, wl1T=wl1T, vecs=vecs, out_ext=out_ext,
        xn_sp=xn_sp, h_sp=h_sp, x1_sp=x1_sp, k_sp=k_sp, v_sp=v_sp, sr_sp=sr_sp, r2_sp=r2_sp,
        cc_m2_o=cc_m2_o, cc_wk_i=cc_wk_i, cc_wk_o=cc_wk_o,
        gd_hsp=gd_hsp, cs_m2=cs_m2,
        x1=x1, vs_x1a=vs_x1a, wkv_dbg=wkv_dbg, dump2=dump2, dbg2=dbg2, dbg3=dbg3,
        w2qb=w2qb, vs_w2q=vs_w2q, gd_xacc=gd_xacc, vs_b2ss=vs_b2ss, vs_ln2st=vs_ln2st,
        gs_attn0=gs_attn0,
    )
    _build_part2(ctx)
    P.verify_no_deadlock()
    P.emit()
    return nc




def _build_part2(c):
    nc = c["nc"]; P = c["P"]; A = c["A"]; alloc = c["alloc"]
    ld = c["ld"]; gd = c["gd"]; vs = c["vs"]; ss = c["ss"]; ts = c["ts"]; gs = c["gs"]; cs = c["cs"]
    psA = c["psA"]; psB = c["psB"]; psC = c["psC"]; psD = c["psD"]; psE = c["psE"]
    idn = c["idn"]; idnb = c["idnb"]; ones128b = c["ones128b"]; onescol = c["onescol"]
    epsc = c["epsc"]; scl = c["scl"]
    vc = c["vc"]; vcm = c["vcm"]; b1cm = c["b1cm"]; e2048bc = c["e2048bc"]; mjbc = c["mjbc"]; expv = c["expv"]
    rowt = c["rowt"]; xsl = c["xsl"]
    ln_sc = c["ln_sc"]; castp = c["castp"]
    rgA = c["rgA"]; rgB = c["rgB"]; rgA_busy = c["rgA_busy"]; rgB_busy = c["rgB_busy"]; rg_ix = c["rg_ix"]
    emit_ln = c["emit_ln"]; emit_ln_norm = c["emit_ln_norm"]
    bcast_row = c["bcast_row"]; load_wT_bf16 = c["load_wT_bf16"]
    wfcT = c["wfcT"]; w2T = c["w2T"]; wl1T = c["wl1T"]; out_ext = c["out_ext"]
    h_sp = c["h_sp"]; x1_sp = c["x1_sp"]
    k_sp = c["k_sp"]; v_sp = c["v_sp"]; sr_sp = c["sr_sp"]; r2_sp = c["r2_sp"]
    cc_m2_o = c["cc_m2_o"]; cc_wk_i = c["cc_wk_i"]; cc_wk_o = c["cc_wk_o"]
    gd_hsp = c["gd_hsp"]; cs_m2 = c["cs_m2"]
    x1 = c["x1"]; vs_x1a = c["vs_x1a"]
    dbg = c["dbg"]; dbg_ext = c["dbg_ext"]; wkv_dbg = c["wkv_dbg"]; dump2 = c["dump2"]
    dbg2 = c["dbg2"]; dbg3 = c["dbg3"]

    vs_x1 = vs_x1a


    w2qb = c["w2qb"]
    vs_w2q = c["vs_w2q"]
    gd_xacc = c["gd_xacc"]
    vs_b2ss = c["vs_b2ss"]
    vs_ln2st = c["vs_ln2st"]
    gs_attn0 = c["gs_attn0"]

    hqsl2 = []
    hq_gu = []
    for p_ in range(2):
        row = []
        for k in range(HC):
            h, gu = alloc(f"hqsl{p_}_{k}", [128, 256], BF16)
            row.append(h)
            hq_gu += gu
        hqsl2.append(row)
    etl = []
    etl_gu = []
    for i_ in range(4):
        h, gu = alloc(f"etl{i_}", [128, 256], F32)
        etl.append(h)
        etl_gu += gu
    etl_busy = [0, 0, 0, 0]
    NTS_M2 = [(c, 256) for c in range(0, CH, 256)] + [(CH, 128)]
    hq2_busy = [_merge_waits(hq_gu, [(ss, vs_w2q)]), list(hq_gu)]
    gs_feed2 = [0] * len(NTS_M2)
    bank_ss2 = [0, 0, 0, 0]

    def m2_feed(t, k0=0, k1=HC):
        c0, w = NTS_M2[t]
        dst = hqsl2[t % 2]
        for kk in range(k0, k1):
            i_ = rg_ix[0] % 4
            rg_ix[0] += 1
            bB, bA = rgB[i_], rgA[i_]
            P.do("sync", lambda e, kk=kk, c0=c0, w=w, bB=bB: e.dma_start(
                out=bB[:, 0:w], in_=h_sp[kk * 128:(kk + 1) * 128, c0:c0 + w]),
                wait=rgB_busy[i_] + ([(gd, gd_hsp)] + hq2_busy[t % 2] if kk == 0 else []),
                inc=[(ld, 16)])
            P.do("scalar", lambda e, w=w, bB=bB, bA=bA: e.activation(out=bA[:, 0:w], in_=bB[:, 0:w],
                                                                     func=AF.Identity, bias=MAGIC,
                                                                     scale=scl[:, 21:22]),
                 wait=_merge_waits([(ld, ld.n)], rgA_busy[i_]), inc=[(ss, 1)])
            rgB_busy[i_] = [(ss, ss.n)]
            P.do("vector", lambda e, kk=kk, w=w, bA=bA, dst=dst: e.tensor_scalar(
                out=dst[kk][:, 0:w], in0=bA[:, 0:w], scalar1=-MAGIC, scalar2=None, op0=ALU.add),
                wait=[(ss, ss.n)], inc=[(vs, 1)])
            rgA_busy[i_] = [(vs, vs.n)]
        if k1 == HC:
            gs_feed2[t] = vs.n

    m2_feed(0)
    for t, (c0, w) in enumerate(NTS_M2):
        rhs = hqsl2[t % 2]
        for m in range(DC):
            if t + 1 < len(NTS_M2):
                m2_feed(t + 1, m * 4, m * 4 + 4)
            pb, cof = [(psD, 0), (psE, 0), (psA, 0), (psA, 512)][m % 4]
            for kk in range(HC):
                P.do("tensor", lambda e, kk=kk, m=m, w=w, pb=pb, cof=cof, rhs=rhs: e.matmul(
                    pb[:, cof:cof + w], w2qb[kk][:, m * 128:(m + 1) * 128], rhs[kk][:, 0:w],
                    start=(kk == 0), stop=(kk == HC - 1)),
                    wait=(_merge_waits([(vs, gs_feed2[t]), (ss, bank_ss2[m % 4])],
                                       [(vs, vs_ln2st)] if (t == 0 and m < 2) else [])
                          if kk == 0 else []),
                    inc=[(ts, 1)] if kk == HC - 1 else [])
            tix = (t * DC + m) % 4
            tile = etl[tix]
            P.do("scalar", lambda e, m=m, w=w, pb=pb, cof=cof, tile=tile: e.activation(
                out=tile[:, 0:w], in_=pb[:, cof:cof + w], func=AF.Identity, bias=vcm[:, 88 + m:89 + m],
                scale=scl[:, 24:25]),
                wait=_merge_waits([(ts, ts.n), (vs, etl_busy[tix])],
                                  _merge_waits(etl_gu, [(vs, vs_b2ss)]) if (t == 0 and m < 4) else []),
                inc=[(ss, 1)])
            bank_ss2[m % 4] = ss.n
            P.do("vector", lambda e, m=m, c0=c0, w=w, tile=tile: e.tensor_tensor(
                out=x1[m][:, c0:c0 + w], in0=x1[m][:, c0:c0 + w], in1=tile[:, 0:w], op=ALU.add),
                wait=_merge_waits([(ss, ss.n)],
                                  [(vs, gd_xacc)] if (t == 0 and m == 0) else []),
                inc=[(vs, 1)])
            etl_busy[tix] = vs.n
        hq2_busy[t % 2] = [(ts, ts.n)]
    vs_x1 = vs.n
    gs_m2 = vs.n
    ts_m2 = ts.n
    for k in range(HC):
        A.free(w2qb[k], [(ts, ts_m2)])
        A.free(hqsl2[0][k], [(ts, ts_m2)])
        A.free(hqsl2[1][k], [(ts, ts_m2)])
    for i_ in range(4):
        A.free(etl[i_], [(vs, gs_m2)])

    for i_ in range(4):
        A.free(rgA[i_], rgA_busy[i_] + [(ss, ss.n), (vs, gs_m2)])
    for g in range(DC):
        P.do("gpsimd", lambda e, g=g: e.dma_start(out=x1_sp[g * 128:(g + 1) * 128, :], in_=x1[g][:, :]),
             wait=[(vs, gs_m2)] if g == 0 else [], inc=[(gd, 16)])
    gd_x1sp = gd.n
    if dbg == "x1":
        P.do("sync", lambda e: e.dma_start(out=dbg_ext[:, :], in_=x1_sp[:, :]), wait=[(gd, gd_x1sp)],
             inc=[(ld, 16)])

    mu_t, mu_gu = A.alloc("mu3_t", [128, WQ], F32)
    rstd_t, rs_gu = A.alloc("rstd3_t", [128, WQ], F32)
    emit_ln(x1, mu_t, rstd_t, _merge_waits([(vs, vs_x1)], mu_gu, rs_gu))
    xn1 = []
    xn1_gu = []
    for g in range(DC):
        h, gu = alloc(f"xn1_{g}", [128, WQ], F32)
        xn1.append(h)
        xn1_gu += gu
    emit_ln_norm(x1, xn1, xn1_gu, mu_t, rstd_t)
    gs_xn1 = vs.n
    for g in range(DC):
        P.do("vector", lambda e, g=g: e.tensor_scalar(
            out=xn1[g][:, PAD - 1:PAD], in0=xn1[g][:, PAD - 1:PAD], scalar1=scl[:, 4:5], scalar2=None,
            op0=ALU.mult), wait=[(vs, gs_xn1)] if g == 0 else [], inc=[(vs, 1)])
    vs_xn1 = vs.n
    for g in range(DC):
        A.free(x1[g], [(vs, gs_xn1), (gd, gd_x1sp)])
    A.free(mu_t, [(vs, gs_xn1)])
    A.free(rstd_t, [(vs, gs_xn1)])
    if dbg == "xn1":
        for g in range(DC):
            P.do("sync", lambda e, g=g: e.dma_start(out=dbg_ext[g * 128:(g + 1) * 128, :], in_=xn1[g][:, :]),
                 wait=[(vs, vs_xn1)], inc=[(ld, 16)])
    dump2(3, xn1, [(vs, vs_xn1)])

    xslq = []
    xsl_gu2 = []
    for p_ in range(2):
        row = []
        for k in range(DC):
            h, gu = alloc(f"xsl3_{p_}_{k}", [128, 512], BF16)
            row.append(h)
            xsl_gu2 += gu
        xslq.append(row)
    xslq_busy = [list(xsl_gu2), list(xsl_gu2)]
    wLbq = []
    wL_gu = []
    for p_ in range(2):
        row = []
        for k in range(DC):
            h, gu = alloc(f"wLb{p_}_{k}", [128, D], BF16)
            row.append(h)
            wL_gu += gu
        wLbq.append(row)
    wLbq_busy = [list(wL_gu), list(wL_gu)]
    wLb_ix = [0]

    def next_wLb():
        p_ = wLb_ix[0] % 2
        wLb_ix[0] += 1
        return wLbq[p_], p_

    kct = []
    kct_gu = []
    for g in range(DC):
        h, gu = alloc(f"kct{g}", [128, CH], BF16)
        kct.append(h)
        kct_gu += gu
    pt0, pt_gu0 = alloc("pt0", [128, 512], F32)
    pt1, pt_gu1 = alloc("pt1", [128, 512], F32)
    ptile = [pt0, pt1]
    ptile_busy = [list(pt_gu0), list(pt_gu1)]
    bank_busy = [0, 0, 0, 0]

    def l1_load(plane):
        wLb, wp = next_wLb()
        ss_w = load_wT_bf16(wl1T[plane, :, :], wLb, first_wait=wLbq_busy[wp])
        return (wLb, wp, ss_w)

    def l1_proj(wh, tmname, mode, dst_sp=None, dst_res=None, extra_first=()):
        wLb, wp, ss_w = wh
        tmv = vc(tmname)
        for ni, (c0, w) in enumerate(NTS_CH):
            a0 = PAD + c0
            xp = ni % 2
            xsl = xslq[xp]
            for kk in range(DC):
                P.do("vector", lambda e, kk=kk, a0=a0, w=w: e.tensor_tensor(
                    out=ln_sc[:, 0:w], in0=xn1[kk][:, a0:a0 + w], in1=xn1[kk][:, a0 - 1:a0 - 1 + w],
                    op=ALU.subtract),
                    wait=(_merge_waits([(vs, vs_xn1)], xslq_busy[xp],
                                       list(extra_first) if ni == 0 else []) if kk == 0 else []),
                    inc=[(vs, 1)])
                P.do("vector", lambda e, kk=kk, a0=a0, w=w, tmv=tmv, xsl=xsl: e.scalar_tensor_tensor(
                    out=xsl[kk][:, 0:w], in0=ln_sc[:, 0:w], scalar=tmv[:, kk:kk + 1],
                    in1=xn1[kk][:, a0 - 1:a0 - 1 + w], op0=ALU.mult, op1=ALU.add), inc=[(vs, 1)])
            vs_sl = vs.n
            for m in range(DC):
                pb, cof = [(psD, 0), (psE, 0), (psA, 0), (psA, 512)][m % 4]
                for kk in range(DC):
                    P.do("tensor", lambda e, kk=kk, m=m, w=w, pb=pb, cof=cof, xsl=xsl, wLb=wLb: e.matmul(
                        pb[:, cof:cof + w], wLb[kk][:, m * 128:(m + 1) * 128], xsl[kk][:, 0:w],
                        start=(kk == 0), stop=(kk == DC - 1)),
                        wait=(_merge_waits([(vs, vs_sl), (ss, bank_busy[m % 4])],
                                           [(ss, ss_w)] if (ni == 0 and m == 0) else [])
                              if kk == 0 else []),
                        inc=[(ts, 1)] if kk == DC - 1 else [])
                if mode == "sig_spill":
                    tile = ptile[m % 2]
                    P.do("scalar", lambda e, w=w, pb=pb, cof=cof, tile=tile: e.activation(
                        out=tile[:, 0:w], in_=pb[:, cof:cof + w], func=AF.Sigmoid),
                        wait=_merge_waits([(ts, ts.n)], ptile_busy[m % 2]), inc=[(ss, 1)])
                    P.do("gpsimd", lambda e, m=m, c0=c0, w=w, tile=tile: e.dma_start(
                        out=dst_sp[m * 128:(m + 1) * 128, c0:c0 + w], in_=tile[:, 0:w]),
                        wait=[(ss, ss.n)], inc=[(gd, 16)])
                    ptile_busy[m % 2] = [(gd, gd.n)]
                    bank_busy[m % 4] = ss.n
                elif mode == "copy_spill":
                    tile = ptile[m % 2]
                    P.do("scalar", lambda e, w=w, pb=pb, cof=cof, tile=tile: e.activation(
                        out=tile[:, 0:w], in_=pb[:, cof:cof + w], func=AF.Copy),
                        wait=_merge_waits([(ts, ts.n)], ptile_busy[m % 2]), inc=[(ss, 1)])
                    P.do("gpsimd", lambda e, m=m, c0=c0, w=w, tile=tile: e.dma_start(
                        out=dst_sp[m * 128:(m + 1) * 128, c0:c0 + w], in_=tile[:, 0:w]),
                        wait=[(ss, ss.n)], inc=[(gd, 16)])
                    ptile_busy[m % 2] = [(gd, gd.n)]
                    bank_busy[m % 4] = ss.n
                elif mode == "sig_res":
                    P.do("scalar", lambda e, m=m, c0=c0, w=w, pb=pb, cof=cof: e.activation(
                        out=dst_res[m][:, c0:c0 + w], in_=pb[:, cof:cof + w], func=AF.Sigmoid),
                        wait=[(ts, ts.n)], inc=[(ss, 1)])
                    bank_busy[m % 4] = ss.n
                elif mode == "relusq_res":
                    tile = ptile[m % 2]
                    P.do("scalar", lambda e, w=w, pb=pb, cof=cof, tile=tile: e.activation(
                        out=tile[:, 0:w], in_=pb[:, cof:cof + w], func=AF.Relu),
                        wait=_merge_waits([(ts, ts.n), (vs, vs.n)], ptile_busy[m % 2]), inc=[(ss, 1)])
                    P.do("vector", lambda e, m=m, c0=c0, w=w, tile=tile: e.tensor_tensor(
                        out=dst_res[m][:, c0:c0 + w], in0=tile[:, 0:w], in1=tile[:, 0:w], op=ALU.mult),
                        wait=[(ss, ss.n)], inc=[(vs, 1)])
                    ptile_busy[m % 2] = [(vs, vs.n)]
                    bank_busy[m % 4] = ss.n
            xslq_busy[xp] = [(ts, ts.n)]
        wLbq_busy[wp] = [(ts, ts.n)]

    wh0 = l1_load(0)
    wh1 = l1_load(1)
    l1_proj(wh0, "tmr", "sig_spill", dst_sp=sr_sp)
    gd_sr = gd.n
    wh2 = l1_load(2)
    l1_proj(wh1, "tmk", "copy_spill", dst_sp=k_sp)
    gd_k = gd.n
    wh3 = l1_load(3)
    l1_proj(wh2, "tmv", "copy_spill", dst_sp=v_sp)
    gd_v = gd.n
    wh4 = l1_load(4)
    l1_proj(wh3, "mur", "sig_spill", dst_sp=r2_sp)
    gd_r2 = gd.n
    wh5 = l1_load(5)
    l1_proj(wh4, "muk", "relusq_res", dst_res=kct, extra_first=kct_gu)
    vs_l1p = vs.n
    ss_l1p = ss.n

    wLb, wp5, ss_wv2 = wh5
    wh6 = l1_load(6)
    ffn1_sp = nc.dram_tensor("ffn1_sp", [D, CH], F32).ap()
    rt_busy = [0, 0]
    rt_vs = [0, 0]
    f1_bank = [0, 0]
    for ni, (c0, w) in enumerate(NTS_CH):
        for m in range(DC):
            pb = psD if m % 2 == 0 else psE
            tile = ptile[m % 2]
            P.do("sync", lambda e, m=m, c0=c0, w=w, tile=tile: e.dma_start(
                out=tile[:, 0:w], in_=r2_sp[m * 128:(m + 1) * 128, c0:c0 + w]),
                wait=_merge_waits([(gd, gd_r2), (gd, rt_busy[m % 2]), (vs, rt_vs[m % 2])],
                                  ptile_busy[m % 2] if (ni == 0 and m < 2) else []), inc=[(ld, 16)])
            ld_r2 = ld.n
            for kk in range(DC):
                P.do("tensor", lambda e, kk=kk, m=m, c0=c0, w=w, pb=pb: e.matmul(
                    pb[:, 0:w], wLb[kk][:, m * 128:(m + 1) * 128], kct[kk][:, c0:c0 + w],
                    start=(kk == 0), stop=(kk == DC - 1)),
                    wait=([(ss, ss_wv2), (ss, ss_l1p), (vs, vs_l1p)] if (ni == 0 and m == 0 and kk == 0)
                          else ([(vs, f1_bank[m % 2])] if kk == 0 else [])),
                    inc=[(ts, 1)] if kk == DC - 1 else [])
            P.do("vector", lambda e, m=m, w=w, pb=pb, tile=tile: e.tensor_tensor(
                out=tile[:, 0:w], in0=pb[:, 0:w], in1=tile[:, 0:w], op=ALU.mult),
                wait=[(ts, ts.n), (ld, ld_r2)], inc=[(vs, 1)])
            f1_bank[m % 2] = vs.n
            rt_vs[m % 2] = vs.n
            P.do("gpsimd", lambda e, m=m, c0=c0, w=w, tile=tile: e.dma_start(
                out=ffn1_sp[m * 128:(m + 1) * 128, c0:c0 + w], in_=tile[:, 0:w]),
                wait=[(vs, vs.n)], inc=[(gd, 16)])
            rt_busy[m % 2] = gd.n
    gd_ffn1 = gd.n
    ts_ffn1 = ts.n
    wLbq_busy[wp5] = [(ts, ts_ffn1)]
    for k in range(DC):
        A.free(wLbq[wp5][k], [(ts, ts_ffn1)])
        A.free(xslq[0][k], [(ts, ts_ffn1)])
        A.free(xslq[1][k], [(ts, ts_ffn1)])
    for g in range(DC):
        A.free(kct[g], [(ts, ts_ffn1)])
        A.free(xn1[g], [(vs, vs_l1p)])

    for cp_ in castp:
        A.free(cp_[0], [(ts, ts.n)])
        A.free(cp_[1], [(ts, ts.n)])
    A.free(ln_sc, [(vs, vs.n), (ss, ss.n)])
    ones2048, o2_gu = A.alloc("ones2048b", [128, 2048], F32)
    jrow, jr_gu = A.alloc("jrowb", [128, 2048], F32)
    P.do("gpsimd", lambda e: e.memset(ones2048[:, :], 1.0), wait=_merge_waits(o2_gu, jr_gu), inc=[(gs, 1)])
    P.do("gpsimd", lambda e: e.iota(jrow[:, :], pattern=[[1, 2048]], base=0, channel_multiplier=0,
                                    allow_small_or_imprecise_dtypes=True), inc=[(gs, 1)])
    gs_wkv = gs.n
    sets = []
    for si in range(2):
        kf, gu1 = alloc(f"wkv_kf{si}", [128, CH], F32)
        vf, gu2 = alloc(f"wkv_vf{si}", [128, CH], F32)
        p0a, gu3 = alloc(f"wkv_p0a{si}", [128, CH + 16], F32)
        p0b, gu4 = alloc(f"wkv_p0b{si}", [128, CH + 16], F32)
        et, gu5 = alloc(f"wkv_et{si}", [128, CH], F32)
        srf, gu6 = alloc(f"wkv_sr{si}", [128, CH], F32)
        aj, gu7 = alloc(f"wkv_aj{si}", [128, 24], F32)
        sets.append(dict(kf=kf, vf=vf, p0a=p0a, p0b=p0b, et=et, srf=srf, aj=aj,
                         gu=gu1 + gu2 + gu3 + gu4 + gu5 + gu6 + gu7, vs_done=0, ld_done=0))
    wkv_rs = []
    wkvrs_gu = []
    for g in range(DC):
        h, gu = alloc(f"wkv_rs{g}", [128, CH], BF16)
        wkv_rs.append(h)
        wkvrs_gu += gu

    cs_chunk = [0] * DC

    def wkv_pre(g):
        S = sets[g % 2]
        kf, vf, p0a, p0b, et, srf = S["kf"], S["vf"], S["p0a"], S["p0b"], S["et"], S["srf"]
        wload = _merge_waits(S["gu"] if g < 2 else [], [(vs, S["vs_done"]), (gs, S["gs_done"]),
                                                        (gd, gd_v), (gd, gd_sr)])
        P.do("sync", lambda e, g=g, kf=kf: e.dma_start(out=kf[:, :], in_=k_sp[g * 128:(g + 1) * 128, :]),
             wait=wload, inc=[(ld, 16)])
        P.do("sync", lambda e, g=g, vf=vf: e.dma_start(out=vf[:, :], in_=v_sp[g * 128:(g + 1) * 128, :]),
             inc=[(ld, 16)])
        P.do("sync", lambda e, g=g, srf=srf: e.dma_start(out=srf[:, :], in_=sr_sp[g * 128:(g + 1) * 128, :]),
             inc=[(ld, 16)])
        ld_kv = ld.n
        P.do("scalar", lambda e, kf=kf: e.activation(out=kf[:, :], in_=kf[:, :], func=AF.Exp),
             wait=[(ld, ld_kv)], inc=[(ss, 1)])
        P.do("gpsimd", lambda e, kf=kf, vf=vf: e.tensor_tensor(out=vf[:, :], in0=kf[:, :], in1=vf[:, :],
                                                               op=ALU.mult),
             wait=[(ss, ss.n)], inc=[(gs, 1)])
        gs_ekv = gs.n
        if wkv_dbg is not None:
            P.do("sync", lambda e, g=g, kf=kf: e.dma_start(
                out=wkv_dbg[0, g * 128:(g + 1) * 128, :], in_=kf[:, :]), wait=[(vs, vs.n)], inc=[(ld, 16)])
            P.do("sync", lambda e, g=g, vf=vf: e.dma_start(
                out=wkv_dbg[1, g * 128:(g + 1) * 128, :], in_=vf[:, :]), inc=[(ld, 16)])
        P.do("scalar", lambda e, g=g, et=et: e.activation(out=et[:, :], in_=ones2048[:, :], func=AF.Copy,
                                                          scale=expv[:, 16 + g:17 + g]),
             wait=[(gs, gs_wkv)] if g == 0 else [], inc=[(ss, 1)])
        ss_bc = ss.n
        P.do("vector", lambda e, p0a=p0a: e.memset(p0a[:, 15:16], 0.0), inc=[(vs, 1)])
        P.do("vector", lambda e, p0b=p0b: e.memset(p0b[:, 15:16], 0.0), inc=[(vs, 1)])
        P.do("vector", lambda e, p0a=p0a, et=et, vf=vf: e.tensor_tensor_scan(
            out=p0a[:, 16:16 + CH], data0=et[:, :], data1=vf[:, :], initial=0.0,
            op0=ALU.mult, op1=ALU.add), wait=[(ss, ss_bc), (gs, gs_ekv)], inc=[(vs, 1)])
        P.do("vector", lambda e, p0b=p0b, et=et, kf=kf: e.tensor_tensor_scan(
            out=p0b[:, 16:16 + CH], data0=et[:, :], data1=kf[:, :], initial=0.0,
            op0=ALU.mult, op1=ALU.add), inc=[(vs, 1)])
        vs_scan = vs.n
        if wkv_dbg is not None:
            P.do("sync", lambda e, g=g, p0a=p0a: e.dma_start(
                out=wkv_dbg[2, g * 128:(g + 1) * 128, :], in_=p0a[:, 16:16 + CH]),
                wait=[(vs, vs_scan)], inc=[(ld, 16)])
        P.do("gpsimd", lambda e, g=g, p0a=p0a: e.dma_start(out=cc_wk_i[g][0:1, 0:128],
                                                           in_=p0a[:, 15 + CH:16 + CH]),
             wait=[(vs, vs_scan)], inc=[(gd, 16)])
        P.do("gpsimd", lambda e, g=g, p0b=p0b: e.dma_start(out=cc_wk_i[g][0:1, 128:256],
                                                           in_=p0b[:, 15 + CH:16 + CH]),
             inc=[(gd, 16)])
        P.do("gpsimd", lambda e, g=g: e.collective_compute(
            "AllGather", ALU.bypass, replica_groups=[list(range(8))],
            ins=[cc_wk_i[g].opt()], outs=[cc_wk_o[g].opt()]), wait=[(gd, gd.n)], inc=[(cs, 1)])
        cs_chunk[g] = cs.n
        P.do("vector", lambda e, g=g, et=et: e.tensor_scalar(out=et[:, :], in0=jrow[:, :],
                                                             scalar1=expv[:, 8 + g:9 + g], scalar2=-87.0,
                                                             op0=ALU.mult, op1=ALU.max), inc=[(vs, 1)])
        P.do("scalar", lambda e, et=et: e.activation(out=et[:, :], in_=et[:, :], func=AF.Exp),
             wait=[(vs, vs.n)], inc=[(ss, 1)])
        S["ss_et"] = ss.n

    def wkv_post(g):
        S = sets[g % 2]
        kf, vf, p0a, p0b, et, srf, aj = S["kf"], S["vf"], S["p0a"], S["p0b"], S["et"], S["srf"], S["aj"]
        rb = 0 if g % 2 == 0 else 256
        P.do("sync", lambda e, g=g, rb=rb: e.dma_start(out=rowt[0:8, rb:rb + 256], in_=cc_wk_o[g][:, :]),
             wait=[(cs, cs_chunk[g]), (ts, S["ts_rb"])], inc=[(ld, 16)])
        ld_ag = ld.n
        P.do("tensor", lambda e, rb=rb: e.transpose(psC[:, 0:8], rowt[0:8, rb:rb + 128], idn[0:8, 0:8]),
             wait=[(ld, ld_ag), (vs, vs.n)], inc=[(ts, 1)])
        P.do("tensor", lambda e, rb=rb: e.transpose(psC[:, 128:136], rowt[0:8, rb + 128:rb + 256],
                                                    idn[0:8, 0:8]),
             inc=[(ts, 1)])
        ts_ag = ts.n
        S["ts_rb"] = ts_ag
        P.do("vector", lambda e, aj=aj: e.tensor_copy(aj[:, 0:8], psC[:, 0:8]), wait=[(ts, ts_ag)], inc=[(vs, 1)])
        P.do("vector", lambda e, aj=aj: e.tensor_copy(aj[:, 8:16], psC[:, 128:136]), inc=[(vs, 1)])
        P.do("vector", lambda e, g=g, aj=aj: e.tensor_scalar(out=aj[:, 16:24], in0=e2048bc[:, :],
                                                             scalar1=expv[:, 8 + g:9 + g], scalar2=-87.0,
                                                             op0=ALU.mult, op1=ALU.max), inc=[(vs, 1)])
        P.do("scalar", lambda e, aj=aj: e.activation(out=aj[:, 16:24], in_=aj[:, 16:24], func=AF.Exp),
             wait=[(vs, vs.n)], inc=[(ss, 1)])
        P.do("vector", lambda e, aj=aj: e.tensor_tensor(out=aj[:, 16:24], in0=aj[:, 16:24], in1=mjbc[:, :],
                                                        op=ALU.mult), wait=[(ss, ss.n)], inc=[(vs, 1)])
        P.do("vector", lambda e, aj=aj: e.tensor_tensor(out=aj[:, 0:8], in0=aj[:, 0:8], in1=aj[:, 16:24],
                                                        op=ALU.mult), inc=[(vs, 1)])
        P.do("vector", lambda e, aj=aj: e.tensor_tensor(out=aj[:, 8:16], in0=aj[:, 8:16], in1=aj[:, 16:24],
                                                        op=ALU.mult), inc=[(vs, 1)])
        P.do("vector", lambda e, aj=aj: e.reduce_sum(out=aj[:, 16:17], in_=aj[:, 0:8], axis=AX.X),
             inc=[(vs, 1)])
        P.do("vector", lambda e, aj=aj: e.reduce_sum(out=aj[:, 17:18], in_=aj[:, 8:16], axis=AX.X),
             inc=[(vs, 1)])
        P.do("vector", lambda e, p0a=p0a, et=et, aj=aj: e.scalar_tensor_tensor(
            out=p0a[:, 15:15 + CH], in0=et[:, :], scalar=aj[:, 16:17], in1=p0a[:, 15:15 + CH],
            op0=ALU.mult, op1=ALU.add), wait=[(ss, S["ss_et"])], inc=[(vs, 1)])
        P.do("vector", lambda e, p0b=p0b, et=et, aj=aj: e.scalar_tensor_tensor(
            out=p0b[:, 15:15 + CH], in0=et[:, :], scalar=aj[:, 17:18], in1=p0b[:, 15:15 + CH],
            op0=ALU.mult, op1=ALU.add), inc=[(vs, 1)])
        if wkv_dbg is not None:
            P.do("sync", lambda e, g=g, p0a=p0a: e.dma_start(
                out=wkv_dbg[3, g * 128:(g + 1) * 128, :], in_=p0a[:, 15:15 + CH]),
                wait=[(vs, vs.n)], inc=[(ld, 16)])
        P.do("vector", lambda e, g=g, vf=vf, p0a=p0a: e.scalar_tensor_tensor(
            out=vf[:, :], in0=vf[:, :], scalar=expv[:, 24 + g:25 + g], in1=p0a[:, 15:15 + CH],
            op0=ALU.mult, op1=ALU.add), inc=[(vs, 1)])
        P.do("vector", lambda e, g=g, kf=kf, p0b=p0b: e.scalar_tensor_tensor(
            out=kf[:, :], in0=kf[:, :], scalar=expv[:, 24 + g:25 + g], in1=p0b[:, 15:15 + CH],
            op0=ALU.mult, op1=ALU.add), inc=[(vs, 1)])
        if wkv_dbg is not None:
            P.do("sync", lambda e, g=g, vf=vf: e.dma_start(
                out=wkv_dbg[4, g * 128:(g + 1) * 128, :], in_=vf[:, :]), wait=[(vs, vs.n)], inc=[(ld, 16)])
            P.do("sync", lambda e, g=g, kf=kf: e.dma_start(
                out=wkv_dbg[5, g * 128:(g + 1) * 128, :], in_=kf[:, :]), inc=[(ld, 16)])
        P.do("vector", lambda e, kf=kf: e.reciprocal(out=kf[:, :], in_=kf[:, :]),
             wait=[(ld, ld.n)] if wkv_dbg is not None else [], inc=[(vs, 1)])
        P.do("gpsimd", lambda e, vf=vf, kf=kf: e.tensor_tensor(out=vf[:, :], in0=vf[:, :], in1=kf[:, :],
                                                               op=ALU.mult), wait=[(vs, vs.n)], inc=[(gs, 1)])
        if wkv_dbg is not None:
            P.do("sync", lambda e, g=g, vf=vf: e.dma_start(
                out=wkv_dbg[6, g * 128:(g + 1) * 128, :], in_=vf[:, :]), wait=[(gs, gs.n)], inc=[(ld, 16)])
        P.do("gpsimd", lambda e, g=g, vf=vf, srf=srf: e.tensor_tensor(
            out=wkv_rs[g][:, :], in0=vf[:, :], in1=srf[:, :], op=ALU.mult),
            wait=(_merge_waits(wkvrs_gu, [(ld, ld.n)] if wkv_dbg is not None else []) if g == 0 or wkv_dbg is not None else []),
            inc=[(gs, 1)])
        S["vs_done"] = vs.n
        S["gs_done"] = gs.n

    for S in sets:
        S["ts_rb"] = ts.n
        S["ss_et"] = 0
        S["gs_done"] = 0
    wkv_pre(0)
    for g in range(DC):
        if g + 1 < DC:
            wkv_pre(g + 1)
        wkv_post(g)
    vs_wkv = vs.n
    gs_wkv2 = gs.n
    if dbg == "all":
        for g in range(DC):
            P.do("vector", lambda e, g=g: e.tensor_copy(sets[0]["kf"][:, :], wkv_rs[g][:, :]),
                 wait=[(vs, vs.n), (gs, gs_wkv2)], inc=[(vs, 1)])
            P.do("sync", lambda e, g=g: e.dma_start(out=dbg2[4, g * 128:(g + 1) * 128, 0:CH],
                                                    in_=sets[0]["kf"][:, :]),
                 wait=[(vs, vs.n)], inc=[(ld, 16)])
            P.do("vector", lambda e: e.nop(), wait=[(ld, ld.n)])
    for S in sets:
        for kkey in ("kf", "vf", "p0a", "p0b", "et", "srf", "aj"):
            A.free(S[kkey], [(vs, vs_wkv), (gs, gs_wkv2)])

    wLb6, wp6, ss_wo = wh6
    f1t = []
    x1t = []
    f1_gu = []
    for g in range(DC):
        h, gu = alloc(f"f1t{g}", [128, 512], F32)
        f1t.append(h)
        f1_gu += gu
        h, gu = alloc(f"x1t{g}", [128, 512], F32)
        x1t.append(h)
        f1_gu += gu
    f1_store = [0]
    wo_bank = [0, 0]
    for ni, (c0, w) in enumerate(NTS_CH):
        for g in range(DC):
            P.do("sync", lambda e, g=g, c0=c0, w=w: e.dma_start(
                out=f1t[g][:, 0:w], in_=ffn1_sp[g * 128:(g + 1) * 128, c0:c0 + w]),
                wait=(_merge_waits(f1_gu, [(gd, gd_ffn1), (gd, gd_x1sp), (gd, f1_store[0])])
                      if g == 0 else []), inc=[(ld, 16)])
            P.do("sync", lambda e, g=g, c0=c0, w=w: e.dma_start(
                out=x1t[g][:, 0:w], in_=x1_sp[g * 128:(g + 1) * 128, PAD + c0:PAD + c0 + w]),
                inc=[(ld, 16)])
        ld_fin = ld.n
        for m in range(DC):
            pb = psD if m % 2 == 0 else psE
            P.do("tensor", lambda e, kk=0, m=m, c0=c0, w=w, pb=pb: e.matmul(
                pb[:, 0:w], wLb6[0][:, m * 128:(m + 1) * 128], wkv_rs[0][:, c0:c0 + w],
                start=True, stop=False),
                wait=_merge_waits([(ss, ss_wo), (gs, gs_wkv2)],
                                  [(vs, wo_bank[m % 2])] if not (ni == 0 and m < 2) else [(vs, vs.n)]))
            for kk in range(1, DC):
                P.do("tensor", lambda e, kk=kk, m=m, c0=c0, w=w, pb=pb: e.matmul(
                    pb[:, 0:w], wLb6[kk][:, m * 128:(m + 1) * 128], wkv_rs[kk][:, c0:c0 + w],
                    start=False, stop=(kk == DC - 1)),
                    inc=[(ts, 1)] if kk == DC - 1 else [])
            P.do("vector", lambda e, m=m, w=w, pb=pb: e.tensor_tensor(
                out=f1t[m][:, 0:w], in0=pb[:, 0:w], in1=f1t[m][:, 0:w], op=ALU.add),
                wait=[(ts, ts.n), (ld, ld_fin)], inc=[(vs, 1)])
            wo_bank[m % 2] = vs.n
            P.do("vector", lambda e, m=m, w=w: e.tensor_tensor(
                out=f1t[m][:, 0:w], in0=f1t[m][:, 0:w], in1=x1t[m][:, 0:w], op=ALU.add),
                inc=[(vs, 1)])
            P.do("gpsimd", lambda e, m=m, c0=c0, w=w: e.dma_start(
                out=out_ext[m * 128:(m + 1) * 128, c0:c0 + w], in_=f1t[m][:, 0:w]),
                wait=[(vs, vs.n)], inc=[(gd, 16)])
        f1_store[0] = gd.n

    P.do("gpsimd", lambda e: e.nop(), wait=[(gd, gd.n)])
    P.do("sync", lambda e: e.nop(), wait=[(ld, ld.n)])


_BUILD_CACHE = {}


def _get_program(dbg=None):
    key = dbg
    if key not in _BUILD_CACHE:
        _BUILD_CACHE[key] = build_program(dbg)
    return _BUILD_CACHE[key]


def _stage_inputs(inputs, dbg=None):
    f32 = np.float32
    x = np.asarray(inputs["L0_attn_x"], f32)
    cat = np.ascontiguousarray

    wqkvT = cat(np.asarray(inputs["L0_attn_w_qkv"], f32).T)
    wfcT = cat(np.asarray(inputs["L0_attn_w_fc"], f32).T)
    w1T = cat(np.asarray(inputs["L0_ffn_w1"], f32).T)
    w2T = cat(np.asarray(inputs["L0_ffn_w2"], f32).T)
    b1v = cat(np.asarray(inputs["L0_ffn_b1"], f32))
    wl1T = cat(np.stack([
        np.asarray(inputs["L1_attn_Wr"], f32).T,
        np.asarray(inputs["L1_attn_Wk"], f32).T,
        np.asarray(inputs["L1_attn_Wv"], f32).T,
        np.asarray(inputs["L1_ffn_W_r"], f32).T,
        np.asarray(inputs["L1_ffn_W_k"], f32).T,
        np.asarray(inputs["L1_ffn_W_v"], f32).T,
        np.asarray(inputs["L1_attn_Wo"], f32).T,
    ]))

    in_maps = []
    for cix in range(NCORE):
        b = cix // 4
        t0 = (cix % 4) * CH
        xs = np.zeros((WQ, D), f32)
        lo = t0 - PAD
        if lo < 0:
            xs[PAD:, :] = x[b, t0:t0 + CH]
        else:
            xs[:, :] = x[b, lo:t0 + CH]
        vecs = np.zeros((16, D), f32)
        vecs[0] = np.asarray(inputs["L0_attn_b_fc"], f32)
        vecs[1] = np.asarray(inputs["L0_attn_gamma"], f32)
        vecs[2] = np.asarray(inputs["L0_attn_beta"], f32)
        vecs[3] = np.asarray(inputs["L0_ffn_b2"], f32)
        vecs[4] = np.asarray(inputs["L1_attn_time_decay"], f32)
        vecs[5] = np.asarray(inputs["L1_attn_time_first"], f32)
        vecs[6] = np.asarray(inputs["L1_attn_time_mix_k"], f32)
        vecs[7] = np.asarray(inputs["L1_attn_time_mix_v"], f32)
        vecs[8] = np.asarray(inputs["L1_attn_time_mix_r"], f32)
        vecs[9] = np.asarray(inputs["L1_ffn_mu_r"], f32)
        vecs[10] = np.asarray(inputs["L1_ffn_mu_k"], f32)
        ctl = np.zeros(D, f32)
        ctl[0] = 0.0 if (cix % 4 == 0) else 1.0
        for j in range(8):
            same = (j // 4) == b
            if same and (j % 4) < (cix % 4):
                ctl[1 + j] = 1.0
                ctl[9 + j] = float(CH * ((cix % 4) - 1 - (j % 4)))
            if same:
                ctl[17 + j] = 1.0
        vecs[11] = ctl
        wmx = np.concatenate([w1T[cix * 128:(cix + 1) * 128, :],
                              w2T[cix * 512:(cix + 1) * 512, :].reshape(128, 4096)], axis=1)
        in_maps.append({
            "x_ext": cat(xs.T), "wqkvT": wqkvT, "wfcT": wfcT, "w1T": w1T, "w2T": w2T,
            "vecs": cat(vecs), "b1v": b1v, "wl1T": wl1T, "wmx": cat(wmx),
        })
    return in_maps


def kernel(**inputs):
    dbg = inputs.pop("_dbg", None)
    nc = _get_program(dbg)
    in_maps = _stage_inputs(inputs, dbg)
    res = run_bass_kernel_spmd(nc, in_maps, core_ids=list(range(NCORE)))
    out = np.zeros((B, T, D), np.float32)
    for cix in range(NCORE):
        b = cix // 4
        t0 = (cix % 4) * CH
        out[b, t0:t0 + CH] = res.results[cix]["out_ext"].T
    if dbg:
        kernel._dbg_results = [dict(res.results[cix]) for cix in range(NCORE)]
    return out
